# version 09
# baseline (speedup 1.0000x reference)
import contextlib
import math
import numpy as np

import concourse.bass as bass
import concourse.mybir as mybir
from concourse.bass_utils import run_bass_kernel_spmd

F32 = mybir.dt.float32
BF16 = mybir.dt.bfloat16
I32 = mybir.dt.int32
AF = mybir.ActivationFunctionType
ALU = mybir.AluOpType
AX = mybir.AxisListType

ENGINES = ("sync", "scalar", "vector", "tensor", "gpsimd")

T = 4096
D = 1024
NT = T // 128
NCH = D // 128


def set_dims(T_):
    global T, NT
    T = T_
    NT = T // 128
H = 16
DKH = 64
HID = 4096
E = 8
FHID = 3072
EPS = 1e-5
NEG = -1e30


class Sem:
    def __init__(self, prog, name):
        self.h = prog.stack.enter_context(prog.nc.semaphore(name))
        self.n = 0
        self.dma_issuers = set()


class Prog:

    def __init__(self, nc, stack):
        self.nc = nc
        self.stack = stack
        self.entries = {e: [] for e in ENGINES}
        self._semid = 0
        self._all_sems = []
        self._pool = {False: [], True: []}
        self._phase_sems = []
        self._bar = None
        self._barcnt = 0
        self._qsems = {}

    def sem(self, tag="", persist=False, dma=False):
        if not persist and self._pool[dma]:
            s = self._pool[dma].pop()
        else:
            self._semid += 1
            s = Sem(self, f"s{self._semid}{tag}")
            self._all_sems.append(s)
            s.is_dma = dma
        if not persist:
            self._phase_sems.append(s)
        return s

    def dsem(self, tag=""):
        return self.sem(tag, dma=True)

    def op(self, eng, fn, inc=None, amt=1):
        ent = {"fn": fn, "sem": inc, "amt": amt if inc is not None else 0,
               "wait": None}
        if inc is not None:
            inc.n += amt
            ent["post"] = inc.n
        self.entries[eng].append(ent)
        return ent

    def wait(self, eng, sem, n=None):
        tgt = sem.n if n is None else n
        self.entries[eng].append({"fn": None, "sem": None, "amt": 0,
                                  "wait": (sem, tgt)})
        return tgt

    def dma(self, eng, out, in_, inc, **kw):
        inc.dma_issuers.add(eng)
        return self.op(eng,
                       lambda En, o=out, i=in_, k=dict(kw): En.dma_start(out=o, in_=i, **k),
                       inc=inc, amt=16)

    def barrier(self, tag="", reclaim=False):
        if self._bar is None:
            self._bar = self.sem("bar", persist=True)
            for e in ENGINES:
                self._qsems[e] = self.sem("q" + e, persist=True)
        bar = self._bar
        for s in self._all_sems:
            for e in s.dma_issuers:
                self.wait(e, s, s.n)
        for e in ENGINES:
            last = None
            for ent in reversed(self.entries[e]):
                if ent["fn"] is not None:
                    last = ent
                    break
            if last is not None and last["sem"] is not bar:
                if last["sem"] is None:
                    q = self._qsems[e]
                    q.n += 1
                    last["sem"] = q
                    last["amt"] = 1
                    last["post"] = q.n
                self.wait(e, last["sem"], last["post"])
            self.op(e, lambda En: En.nop(), inc=bar, amt=1)
        self._barcnt += 5
        for e in ENGINES:
            self.wait(e, bar, self._barcnt)
        if reclaim:
            for s in self._phase_sems:
                self._pool[getattr(s, "is_dma", False)].append(s)
            self._phase_sems = []

    def emit(self):
        nc = self.nc
        handles = {"sync": nc.sync, "scalar": nc.scalar, "vector": nc.vector,
                   "tensor": nc.tensor, "gpsimd": nc.gpsimd}

        def run(e):
            En = handles[e]
            for ent in self.entries[e]:
                if ent["wait"] is not None:
                    s, t = ent["wait"]
                    En.wait_ge(s.h, t)
                    continue
                r = ent["fn"](En)
                if ent["sem"] is not None:
                    r.then_inc(ent["sem"].h, ent["amt"])

        with nc.Block() as block:
            block.sync(lambda _: run("sync"))
            block.scalar(lambda _: run("scalar"))
            block.vector(lambda _: run("vector"))
            block.tensor(lambda _: run("tensor"))
            block.gpsimd(lambda _: run("gpsimd"))


def p_ln_tokmajor(p, x_ap, out_d, in_dtype, tag):
    nc = p.nc
    with contextlib.ExitStack() as stk:
        Eo = stk.enter_context
        NB = 3
        xt = [Eo(nc.sbuf_tensor(f"{tag}x{i}", [128, D], in_dtype)) for i in range(NB)]
        xnt = [Eo(nc.sbuf_tensor(f"{tag}n{i}", [128, D], BF16)) for i in range(NB)]
        xsq = [Eo(nc.sbuf_tensor(f"{tag}q{i}", [128, D], F32)) for i in range(2)]
        scol = [Eo(nc.sbuf_tensor(f"{tag}s{i}", [128, 1], F32)) for i in range(NB)]
        qcol = [Eo(nc.sbuf_tensor(f"{tag}c{i}", [128, 1], F32)) for i in range(NB)]
        mcol = [Eo(nc.sbuf_tensor(f"{tag}m{i}", [128, 1], F32)) for i in range(NB)]
        vcol = [Eo(nc.sbuf_tensor(f"{tag}v{i}", [128, 1], F32)) for i in range(NB)]
        rcol = [Eo(nc.sbuf_tensor(f"{tag}r{i}", [128, 1], F32)) for i in range(NB)]
        epsc = Eo(nc.sbuf_tensor(f"{tag}eps", [128, 1], F32))
        ld = [p.dsem(f"ld{i}") for i in range(NB)]
        sto = [p.dsem(f"st{i}") for i in range(NB)]
        act = p.sem("act")
        dve = p.sem("dve")
        rstd = p.sem("rstd")
        app = p.sem("app")
        cst = p.sem("cst")
        p.op("vector", lambda En: En.memset(epsc[:], float(EPS)), inc=cst)
        p.wait("scalar", cst, cst.n)
        sq_post = {}
        app_post = {}
        for i in range(NT):
            b = i % NB
            if i >= NB:
                p.wait("sync", app, app_post[i - NB])
                p.wait("sync", act, sq_post[i - NB])
            p.dma("sync", xt[b][:], x_ap[i * 128:(i + 1) * 128, :], ld[b])
            p.wait("vector", ld[b], ld[b].n)
            p.op("vector", lambda En, b=b: En.reduce_sum(scol[b][:], xt[b][:], AX.X),
                 inc=dve)
            p.wait("vector", dve, dve.n)
            p.op("vector", lambda En, b=b: En.tensor_scalar(
                mcol[b][:], scol[b][:], 1.0 / D, None, ALU.mult), inc=dve)
            p.wait("scalar", ld[b], ld[b].n)
            ent = p.op("scalar", lambda En, b=b, i=i: En.activation(
                xsq[i % 2][:], xt[b][:], AF.Square, accum_out=qcol[b][:, 0:1]),
                inc=act)
            sq_post[i] = ent["post"]
            p.wait("vector", dve, dve.n)
            p.wait("vector", act, act.n)
            p.op("vector", lambda En, b=b: En.tensor_tensor(
                vcol[b][:], mcol[b][:], mcol[b][:], ALU.mult), inc=dve)
            p.wait("vector", dve, dve.n)
            p.op("vector", lambda En, b=b: En.scalar_tensor_tensor(
                vcol[b][:], qcol[b][:], 1.0 / D, vcol[b][:], ALU.mult,
                ALU.subtract), inc=dve)
            p.wait("scalar", dve, dve.n)
            p.op("scalar", lambda En, b=b: En.activation(
                rcol[b][:], vcol[b][:], AF.Ln, bias=epsc[:, 0:1]), inc=act)
            p.wait("scalar", act, act.n)
            p.op("scalar", lambda En, b=b: En.activation(
                rcol[b][:], rcol[b][:], AF.Exp, scale=-0.5), inc=rstd)
            p.wait("vector", rstd, rstd.n)
            p.wait("vector", sto[b], sto[b].n)
            p.op("vector", lambda En, b=b: En.tensor_scalar(
                xnt[b][:], xt[b][:], mcol[b][:, 0:1], rcol[b][:, 0:1],
                ALU.subtract, ALU.mult), inc=app)
            app_post[i] = app.n
            p.wait("scalar", app, app.n)
            p.dma("scalar", out_d[i * 128:(i + 1) * 128, :], xnt[b][:], sto[b])
        for b in range(NB):
            p.wait("scalar", sto[b], sto[b].n)
    p.barrier(tag, reclaim=True)


class AttnCfg:
    def __init__(self, **kw):
        self.tag = kw["tag"]
        self.src_d = kw["src_d"]
        self.out_d = kw["out_d"]
        self.out_f32 = kw.get("out_f32", False)
        self.resid_src = kw.get("resid_src")
        self.causal = kw["causal"]
        self.alibi = kw["alibi"]
        self.qw = kw["qw"]
        self.kw_ = kw["kw_"]
        self.vw = kw["vw"]
        self.latent = kw.get("latent")
        self.ow = kw["ow"]
        self.affine = kw.get("affine")
        self.v_d = kw.get("v_d")
        self.wait_cast = kw.get("wait_cast", [])


def _slope(h):
    return 2.0 ** (-0.5 * (h + 1))


def p_attn(p, cfg):
    nc = p.nc
    tag = cfg.tag
    KV_CH = 2 if cfg.latent else NCH
    NKT = NT
    NQC = T // 512
    with contextlib.ExitStack() as stk:
        Eo = stk.enter_context
        if not cfg.alibi:
            PT = [Eo(nc.sbuf_tensor(f"{tag}pt{i}", [128, 1024], BF16))
                  for i in range(2)]
            NPT = 2
        else:
            PT = [Eo(nc.sbuf_tensor(f"{tag}pt{i}", [128, 512], BF16))
                  for i in range(3)]
            NPT = 3
        wq = Eo(nc.sbuf_tensor(f"{tag}wq", [128, D], BF16))
        wk = Eo(nc.sbuf_tensor(f"{tag}wk", [128, KV_CH * 128], BF16))
        wv = Eo(nc.sbuf_tensor(f"{tag}wv", [128, KV_CH * 128], BF16))
        qsq = [Eo(nc.sbuf_tensor(f"{tag}sq0", [128, 512], BF16))]
        zzb = Eo(nc.sbuf_tensor(f"{tag}zzb", [65, 512], F32))
        zb_sb = zzb
        zr = zzb
        ones_f = Eo(nc.sbuf_tensor(f"{tag}onef", [128, 128], F32))
        ones_b = Eo(nc.sbuf_tensor(f"{tag}oneb", [128, 128], BF16))
        ident_b = Eo(nc.sbuf_tensor(f"{tag}idb", [128, 128], BF16))
        mask0 = Eo(nc.sbuf_tensor(f"{tag}mk", [128, 128], F32))
        nC = [Eo(nc.sbuf_tensor(f"{tag}nC{i}", [128, 1], F32)) for i in range(2)]
        mq2 = Eo(nc.sbuf_tensor(f"{tag}mq", [1, 2], F32))
        cch = Eo(nc.sbuf_tensor(f"{tag}cc", [1, 2], F32))
        if cfg.alibi:
            negB = [Eo(nc.sbuf_tensor(f"{tag}nb{i}", [65, 512], BF16))
                    for i in range(2)]
            negBi0 = Eo(nc.sbuf_tensor(f"{tag}nbi", [65, 512], mybir.dt.int16))
            negBi = [negBi0, negBi0]
            kgl_i = Eo(nc.sbuf_tensor(f"{tag}kgi", [128, NT], I32))
            kgl = Eo(nc.sbuf_tensor(f"{tag}kgf", [128, NT], F32))
            bias_h = [[Eo(nc.sbuf_tensor(f"{tag}bh{i}{j}", [128, NT], F32))
                       for j in range(2)] for i in range(2)]
        gcol = [Eo(nc.sbuf_tensor(f"{tag}gc{i}", [128, 1], F32)) for i in range(NCH)]
        bcol = [Eo(nc.sbuf_tensor(f"{tag}bc{i}", [128, 1], F32)) for i in range(NCH)]
        ps_p0 = Eo(nc.psum_tensor(f"{tag}pp", [128, 512], F32))
        ps_p = [ps_p0, ps_p0]
        WIDE = not cfg.alibi
        if WIDE:
            NS = 2
            ps_s = [Eo(nc.psum_tensor(f"{tag}ps{i}", [128, 1024], F32))
                    for i in range(NS)]
        else:
            NS = 4
            ps_s = [Eo(nc.psum_tensor(f"{tag}ps{i}", [128, 512], F32))
                    for i in range(NS)]
        ps_y = [Eo(nc.psum_tensor(f"{tag}py{i}", [65, 512], F32)) for i in range(2)]
        aux = Eo(nc.psum_tensor(f"{tag}aux", [128, 512], F32))
        vstg = [Eo(nc.sbuf_tensor(f"{tag}vg{i}", [128, 128], BF16))
                for i in range(2)]
        vstg_free = [0, 0]
        st_v = p.dsem("stv")
        ld_vt = [p.dsem("lvt0"), p.dsem("lvt1")]
        ld_pan = p.dsem("ldp"); ld_af = p.dsem("ldaf"); ld_l = p.dsem("ldl")
        ld_w = [p.dsem("ldw0"), p.dsem("ldw1")]; po = p.sem("po")
        pe_p = p.sem("pep"); evq = p.sem("evq"); aff = p.sem("aff")
        pe_n = p.sem("pen"); dve_n = p.sem("dvn")
        act_c = p.sem("acc"); dve_c = p.sem("dvc")
        pe_t = p.sem("pet"); dve_t = p.sem("dvt")
        pe_s = p.sem("pes"); msk = p.sem("msk"); ex = p.sem("ex")
        pe_av = p.sem("pav"); dve_z = p.sem("dvz")
        pe_zb = p.sem("pzb"); act_zb = p.sem("azb"); dve_y = p.sem("dvy")

        for cs, cn in cfg.wait_cast:
            p.wait("sync", cs, cn)

        pp_free = [0, 0]
        ppn_free = [0, 0]
        negb_free = [0, 0]
        yq_slot = [(0, 0), (0, 0)]
        pt_free = [0, 0, 0]
        exp_post = {}
        sc_cnt = [0]

        p.op("vector", lambda En: En.memset(ones_f[:], 1.0), inc=dve_c)
        p.op("vector", lambda En: En.memset(ones_b[:], 1.0), inc=dve_c)
        p.op("vector", lambda En: En.memset(mask0[:], 0.0), inc=dve_c)
        p.op("vector", lambda En: En.memset(ident_b[:], 1.0), inc=dve_c)
        cst0 = dve_c.n
        p.wait("gpsimd", dve_c, cst0)
        p.op("gpsimd", lambda En: En.affine_select(
            ident_b[:], ident_b[:], [[-1, 128]], ALU.is_equal, 0.0,
            base=0, channel_multiplier=1), inc=po)
        ident_post = po.n
        mask_post = 0
        if cfg.causal:
            p.op("gpsimd", lambda En: En.affine_select(
                mask0[:], mask0[:], [[1, 128]], ALU.is_ge, float(NEG),
                base=0, channel_multiplier=-1), inc=po)
            mask_post = po.n
        kgl_post = 0
        if cfg.alibi:
            p.op("gpsimd", lambda En: En.iota(
                kgl_i[:], [[128, NT]], base=0, channel_multiplier=1), inc=po)
            p.wait("vector", po, po.n)
            p.op("vector", lambda En: En.tensor_copy(kgl[:], kgl_i[:]), inc=dve_c)
            kgl_post = dve_c.n
        aff_cols_post = 0
        if cfg.affine is not None:
            with contextlib.ExitStack() as stka:
                wrow = stka.enter_context(nc.sbuf_tensor(f"{tag}wr", [1, D], F32))
                brow = stka.enter_context(nc.sbuf_tensor(f"{tag}br", [1, D], F32))
                p.dma("sync", wrow[:], cfg.affine[0].rearrange("(o d) -> o d", o=1), ld_af)
                p.dma("sync", brow[:], cfg.affine[1].rearrange("(o d) -> o d", o=1), ld_af)
                p.wait("tensor", ld_af, ld_af.n)
                p.wait("tensor", dve_c, cst0)
                for ic in range(NCH):
                    for row, dstc in ((wrow, gcol[ic]), (brow, bcol[ic])):
                        p.wait("tensor", act_c, act_c.n)
                        p.op("tensor", lambda En, ic=ic, row=row: En.transpose(
                            aux[:, 0:1], row[0:1, ic * 128:(ic + 1) * 128],
                            ones_f[0:1, 0:1]), inc=pe_t)
                        p.wait("scalar", pe_t, pe_t.n)
                        p.op("scalar", lambda En, dstc=dstc: En.activation(
                            dstc[:], aux[:, 0:1], AF.Copy), inc=act_c)
                aff_cols_post = act_c.n
                p.wait("sync", act_c, aff_cols_post)

        pan = [Eo(nc.sbuf_tensor(f"{tag}pan{i}", [128, T], BF16)) for i in range(NCH)]
        Y = [Eo(nc.sbuf_tensor(f"{tag}Y{i}", [128, T], BF16)) for i in range(NCH)]
        qsb = [Eo(nc.sbuf_tensor(f"{tag}q{i}", [128, T], BF16)) for i in range(2)]
        ksb = [Eo(nc.sbuf_tensor(f"{tag}k{i}", [128, T], BF16)) for i in range(2)]
        vsb = Eo(nc.sbuf_tensor(f"{tag}v", [128, T], BF16))
        v_tm = [Eo(nc.sbuf_tensor(f"{tag}vt{i}", [128, 2 * NT * 65], BF16))
                for i in range(2)]
        for ic in range(NCH):
            p.dma("sync", pan[ic][:], cfg.src_d[:, ic * 128:(ic + 1) * 128],
                  ld_pan, transpose=True)
        pan_total = ld_pan.n
        if cfg.affine is not None:
            for ic in range(NCH):
                p.wait("vector", ld_pan, pan_total)
                p.wait("vector", act_c, aff_cols_post)
                p.op("vector", lambda En, ic=ic: En.tensor_scalar(
                    pan[ic][:], pan[ic][:], gcol[ic][:, 0:1], bcol[ic][:, 0:1],
                    ALU.mult, ALU.add), inc=aff)
            pan_wait = (aff, aff.n)
        else:
            pan_wait = (ld_pan, pan_total)

        lat_tiles = None
        lat_ld = None
        lat_free = [0, 0]
        if cfg.latent:
            lat_d = cfg.latent["lat_d"]
            lat_tiles = [[Eo(nc.sbuf_tensor(f"{tag}lt{s}{i}", [128, 512], BF16))
                          for i in range(2)] for s in range(2)]
            lat_ld = [p.dsem("lls0"), p.dsem("lls1")]
            wl = wq
            lst = qsq[0]
            stl = p.dsem("stl")
            for ocl in range(2):
                if ocl > 0:
                    p.wait("sync", pe_p, pe_p.n)
                p.dma("sync", wl[:].rearrange("p (ic o) -> p ic o", o=128),
                      cfg.latent["dw"](ocl), ld_l)
                p.wait("tensor", pan_wait[0], pan_wait[1])
                p.wait("tensor", ld_l, ld_l.n)
                for tc in range(NQC):
                    pb = 0
                    p.wait("tensor", evq, pp_free[pb])
                    for ic in range(NCH):
                        p.op("tensor", lambda En, pb=pb, ic=ic, tc=tc: En.matmul(
                            ps_p[pb][:], wl[:, ic * 128:(ic + 1) * 128],
                            pan[ic][:, tc * 512:(tc + 1) * 512],
                            start=(ic == 0), stop=(ic == NCH - 1)), inc=pe_p)
                    p.wait("vector", pe_p, pe_p.n)
                    p.wait("vector", stl, stl.n)
                    p.op("vector", lambda En, pb=pb: En.tensor_copy(
                        lst[:], ps_p[pb][:]), inc=evq)
                    pp_free[pb] = evq.n
                    p.wait("scalar", evq, evq.n)
                    p.dma("scalar", lat_d[ocl * 128:(ocl + 1) * 128,
                                          tc * 512:(tc + 1) * 512], lst[:], stl)
            p.wait("sync", stl, stl.n)
            lat_store_post = stl.n
        kv_pan = pan

        state = {}

        def make_prep(oc, buf, prev_jobs_ex):
            fl = []
            p.wait("sync", pe_p, pe_p.n)
            p.wait("sync", ex, prev_jobs_ex)
            p.dma("sync", wq[:].rearrange("p (ic o) -> p ic o", o=128),
                  cfg.qw(oc), ld_w[0])
            if cfg.latent:
                p.dma("sync", wk[:].rearrange("p (ic o) -> p ic o", o=128),
                      cfg.latent["ku"](oc), ld_w[0])
                p.dma("sync", wv[:].rearrange("p (ic o) -> p ic o", o=128),
                      cfg.latent["vu"](oc), ld_w[0])
            else:
                p.dma("sync", wk[:].rearrange("p (ic o) -> p ic o", o=128),
                      cfg.kw_(oc), ld_w[0])
                p.dma("sync", wv[:].rearrange("p (ic o) -> p ic o", o=128),
                      cfg.vw(oc), ld_w[0])
            ldw_post = ld_w[0].n
            for which, dst, wt, src_pan, nkc, scale in (
                    ("q", qsb[buf], wq, pan, NCH, 0.125),
                    ("k", ksb[buf], wk, kv_pan, KV_CH, 1.0),
                    ("v", vsb, wv, kv_pan, KV_CH, 1.0)):
                for tc in range(NQC):
                    def proj(which=which, dst=dst, wt=wt, src_pan=src_pan,
                             nkc=nkc, scale=scale, tc=tc, ldw_post=ldw_post,
                             prev_jobs_ex=prev_jobs_ex):
                        pb = 0
                        lat_mode = cfg.latent and which != "q"
                        if lat_mode:
                            ls = tc % 2
                            p.wait("sync", pe_p, lat_free[ls])
                            for icl in range(2):
                                p.dma("sync", lat_tiles[ls][icl][:],
                                      cfg.latent["lat_d"][
                                          icl * 128:(icl + 1) * 128,
                                          tc * 512:(tc + 1) * 512], lat_ld[ls])
                            llpost = lat_ld[ls].n
                        p.wait("tensor", pan_wait[0], pan_wait[1])
                        p.wait("tensor", ld_w[0], ldw_post)
                        p.wait("tensor", ex, prev_jobs_ex)
                        if which == "v":
                            p.wait("tensor", st_v, state.get("v_st", 0))
                        p.wait("tensor", evq, pp_free[pb])
                        p.wait("tensor", dve_n, ppn_free[pb])
                        if lat_mode:
                            p.wait("tensor", lat_ld[ls], llpost)
                        for ic in range(nkc):
                            srcap = (lat_tiles[ls][ic][:, 0:512] if lat_mode
                                     else src_pan[ic][:, tc * 512:(tc + 1) * 512])
                            p.op("tensor", lambda En, pb=pb, ic=ic, wt=wt,
                                 nkc=nkc, srcap=srcap: En.matmul(
                                     ps_p[pb][:], wt[:, ic * 128:(ic + 1) * 128],
                                     srcap,
                                     start=(ic == 0), stop=(ic == nkc - 1)),
                                 inc=pe_p)
                        if lat_mode:
                            lat_free[ls] = pe_p.n
                        p.wait("vector", pe_p, pe_p.n)
                        if scale == 1.0:
                            p.op("vector", lambda En, pb=pb, dst=dst, tc=tc:
                                 En.tensor_copy(dst[:, tc * 512:(tc + 1) * 512],
                                                ps_p[pb][:]), inc=evq)
                        else:
                            p.op("vector", lambda En, pb=pb, dst=dst, tc=tc,
                                 scale=scale: En.tensor_scalar(
                                     dst[:, tc * 512:(tc + 1) * 512],
                                     ps_p[pb][:], scale, None, ALU.mult),
                                 inc=evq)
                        pp_free[pb] = evq.n
                    fl.append(proj)

            def wfree_mark():
                state[("evq_all", buf)] = evq.n
            fl.append(wfree_mark)

            def norm_init():
                p.wait("vector", evq, state[("evq_all", buf)])
                p.wait("vector", dve_n, dve_n.n)
                p.op("vector", lambda En: En.memset(mq2[:], 1e-20), inc=dve_n)
            fl.append(norm_init)
            ns_free = {}
            for which, srcf, col in (("q", qsb, 0), ("k", ksb, 1)):
                for tc in range(NQC):
                    def norm(which=which, srcf=srcf, col=col, tc=tc, buf=buf):
                        src = srcf[buf]
                        sb2 = 0
                        p.wait("scalar", evq, state[("evq_all", buf)])
                        p.wait("scalar", pe_n, ns_free.get(("sq", sb2), 0))
                        p.op("scalar", lambda En, sb2=sb2, src=src, tc=tc:
                             En.activation(qsq[sb2][:],
                                           src[:, tc * 512:(tc + 1) * 512],
                                           AF.Square), inc=act_c)
                        p.wait("tensor", act_c, act_c.n)
                        p.wait("tensor", dve_c, cst0)
                        p.wait("tensor", evq, pp_free[sb2])
                        p.wait("tensor", dve_n, ns_free.get(("ps", sb2), 0))
                        
                        p.op("tensor", lambda En, sb2=sb2: En.matmul(
                            ps_p[sb2][0:1, :], ones_b[:, 0:1], qsq[sb2][:],
                            start=True, stop=True), inc=pe_n)
                        ns_free[("sq", sb2)] = pe_n.n
                        p.wait("vector", pe_n, pe_n.n)
                        p.op("vector", lambda En, sb2=sb2: En.tensor_reduce(
                            cch[0:1, 0:1], ps_p[sb2][0:1, :], AX.X, ALU.max),
                            inc=dve_n)
                        ns_free[("ps", sb2)] = dve_n.n
                        ppn_free[sb2] = dve_n.n
                        p.wait("vector", dve_n, dve_n.n)
                        p.op("vector", lambda En, col=col: En.tensor_max(
                            mq2[0:1, col:col + 1], mq2[0:1, col:col + 1],
                            cch[0:1, 0:1]), inc=dve_n)
                        p.wait("vector", dve_n, dve_n.n)
                    fl.append(norm)

            def cchain(buf=buf, oc=oc):
                p.op("vector", lambda En: En.tensor_tensor(
                    cch[0:1, 0:1], mq2[0:1, 0:1], mq2[0:1, 1:2], ALU.mult),
                    inc=dve_n)
                p.wait("scalar", dve_n, dve_n.n)
                p.op("scalar", lambda En: En.activation(
                    cch[0:1, 1:2], cch[0:1, 0:1], AF.Sqrt,
                    scale=(1.01 ** 2) / 64.0), inc=act_c)
                p.wait("vector", act_c, act_c.n)
                p.op("vector", lambda En: En.tensor_scalar(
                    cch[0:1, 0:1], cch[0:1, 1:2], -1.0, -2.0, ALU.mult,
                    ALU.add), inc=dve_n)
                p.wait("tensor", dve_n, dve_n.n)
                p.wait("tensor", act_zb, act_zb.n)
                p.wait("tensor", ex, state.get(("nC_used", buf), 0))
                p.op("tensor", lambda En: En.matmul(
                    aux[:, 0:1], ones_f[0:1, :], cch[0:1, 0:1],
                    start=True, stop=True), inc=pe_zb)
                p.wait("scalar", pe_zb, pe_zb.n)
                p.op("scalar", lambda En, buf=buf: En.activation(
                    nC[buf][:], aux[:, 0:1], AF.Copy), inc=act_zb)
                state[("nC", buf)] = act_zb.n
                if cfg.alibi:
                    for h2 in range(2):
                        h = oc * 2 + h2
                        p.wait("vector", act_zb, state[("nC", buf)])
                        p.wait("vector", dve_c, kgl_post)
                        p.wait("vector", ex, state.get(("nC_used", buf), 0))
                        p.op("vector", lambda En, h2=h2, h=h, buf=buf:
                             En.tensor_scalar(
                                 bias_h[h2][buf][:], kgl[:], _slope(h),
                                 nC[buf][:, 0:1], ALU.mult, ALU.add), inc=dve_c)
                    state[("bias", buf)] = dve_c.n
            fl.append(cchain)

            def vt_init(buf=buf):
                p.wait("vector", evq, state[("evq_all", buf)])
                p.wait("vector", pe_av, state.get(("vtm_used", buf), 0))
                p.op("vector", lambda En, buf=buf: En.memset(v_tm[buf][:], 1.0),
                     inc=dve_t)
                state[("vtm_ms", buf)] = dve_t.n
                p.wait("scalar", evq, state[("evq_all", buf)])
                p.dma("scalar", cfg.v_d, vsb[:], st_v)
                state["v_st"] = st_v.n
            fl.append(vt_init)
            for kt in range(NKT):
                def vt(kt=kt, buf=buf):
                    vs = kt % 2
                    p.wait("sync", st_v, state["v_st"])
                    p.wait("sync", dve_t, vstg_free[vs])
                    p.dma("sync", vstg[vs][:],
                          cfg.v_d[:, kt * 128:(kt + 1) * 128], ld_vt[vs],
                          transpose=True)
                    p.wait("vector", ld_vt[vs], ld_vt[vs].n)
                    p.wait("vector", dve_t, state[("vtm_ms", buf)])
                    for h2 in range(2):
                        p.op("vector", lambda En, h2=h2, kt=kt, buf=buf, vs=vs:
                             En.tensor_copy(
                                 v_tm[buf][:, (h2 * NKT + kt) * 65:
                                           (h2 * NKT + kt) * 65 + 64],
                                 vstg[vs][:, h2 * 64:(h2 + 1) * 64]),
                             inc=dve_t)
                    vstg_free[vs] = dve_t.n
                fl.append(vt)

            def vt_done(buf=buf):
                state[("vtm", buf)] = dve_t.n
            fl.append(vt_done)
            return fl

        def run_jobs(oc, buf, fillers):
            units = []
            for h2 in range(2):
                for qc in range(NQC):
                    qb = qc * 512
                    if cfg.causal:
                        kts = []
                        for kt in range(NKT):
                            if kt * 128 > qb + 384:
                                break
                            kts.append((kt, max(0, (kt * 128 - qb) // 128)))
                    else:
                        kts = [(kt, 0) for kt in range(NKT)]
                    i0 = 0
                    while i0 < len(kts):
                        if (WIDE and i0 + 1 < len(kts)
                                and kts[i0][1] == 0 and kts[i0 + 1][1] == 0):
                            units.append((h2, qc, kts[i0:i0 + 2], i0, len(kts)))
                            i0 += 2
                        else:
                            units.append((h2, qc, kts[i0:i0 + 1], i0, len(kts)))
                            i0 += 1
            LAG = 2
            deferred = []
            negb_built = {}
            nu = len(units)
            nf = len(fillers)
            fill_at = {}
            if nf:
                stepf = max(1, nu // nf)
                for i in range(nf):
                    fill_at.setdefault(min(4 + i * stepf, nu - 1), []).append(
                        fillers[i])

            def emit_S(u):
                h2, qc, kts, i0, nkts = units[u]
                h = oc * 2 + h2
                qb = qc * 512
                prow = h2 * 64
                hs = slice(h2 * 64, (h2 + 1) * 64)
                c = sc_cnt[0]
                sc_cnt[0] += 1
                sb = c % NS
                nbb = None
                if cfg.alibi:
                    nbb = (h2 * NQC + qc) % 2
                    if i0 == 0:
                        p.wait("gpsimd", pe_s, negb_free[nbb])
                        p.wait("gpsimd", act_c, act_c.n)
                        p.op("gpsimd", lambda En, nbb=nbb, qb=qb, prow=prow:
                             En.iota(negBi[nbb][prow:prow + 1, :], [[1, 512]],
                                     base=qb, channel_multiplier=0), inc=po)
                        p.wait("scalar", po, po.n)
                        p.op("scalar", lambda En, nbb=nbb, h=h, prow=prow:
                             En.activation(negB[nbb][prow:prow + 1, :],
                                           negBi[nbb][prow:prow + 1, :],
                                           AF.Copy, scale=-_slope(h)), inc=act_c)
                        negb_built[(h2, qc)] = act_c.n
                if c >= NS:
                    p.wait("tensor", ex, exp_post[c - NS])
                if cfg.alibi:
                    p.wait("tensor", act_c, negb_built[(h2, qc)])
                msk_post = None
                for i, (kt, jj) in enumerate(kts):
                    qsl = slice(qb + jj * 128, qb + 512)
                    psl = slice(512 * i + jj * 128, 512 * (i + 1))
                    p.op("tensor", lambda En, sb=sb, hs=hs, kt=kt, qsl=qsl,
                         psl=psl, buf=buf: En.matmul(
                             ps_s[sb][:, psl],
                             ksb[buf][hs, kt * 128:(kt + 1) * 128],
                             qsb[buf][hs, qsl], start=True,
                             stop=not cfg.alibi), inc=pe_s)
                    if cfg.alibi:
                        p.op("tensor", lambda En, sb=sb, psl=psl, prow=prow,
                             nbb=nbb: En.matmul(
                                 ps_s[sb][:, psl], ones_b[prow:prow + 1, :],
                                 negB[nbb][prow:prow + 1,
                                           slice(jj * 128, 512)],
                                 start=False, stop=True), inc=pe_s)
                        negb_free[nbb] = pe_s.n
                mm_post = pe_s.n
                partial_any = False
                for i, (kt, jj) in enumerate(kts):
                    if cfg.causal and kt * 128 >= qb:
                        if not partial_any:
                            p.wait("vector", pe_s, mm_post)
                            p.wait("vector", po, mask_post)
                            partial_any = True
                        p.op("vector", lambda En, sb=sb, i=i, jj=jj:
                             En.tensor_tensor(
                                 ps_s[sb][:, 512 * i + jj * 128:
                                          512 * i + jj * 128 + 128],
                                 ps_s[sb][:, 512 * i + jj * 128:
                                          512 * i + jj * 128 + 128],
                                 mask0[:], ALU.add), inc=msk)
                if partial_any:
                    msk_post = msk.n
                ptb = c % NPT
                p.wait("scalar", pe_av, pt_free[ptb])
                if partial_any:
                    p.wait("scalar", msk, msk_post)
                else:
                    p.wait("scalar", pe_s, mm_post)
                p.wait("scalar", act_zb, state[("nC", buf)])
                jj0 = kts[0][1]
                espan = slice(jj0 * 128, 512 * len(kts))
                if cfg.alibi:
                    p.wait("scalar", dve_c, state[("bias", buf)])
                    ba = bias_h[h2][buf][:, kts[0][0]:kts[0][0] + 1]
                else:
                    ba = nC[buf][:, 0:1]
                p.op("scalar", lambda En, ptb=ptb, sb=sb, espan=espan, ba=ba:
                     En.activation(PT[ptb][:, espan], ps_s[sb][:, espan],
                                   AF.Exp, bias=ba), inc=ex)
                exp_post[c] = ex.n
                return c

            def emit_AV(u, c):
                h2, qc, kts, i0, nkts = units[u]
                yb = (h2 * NQC + qc) % 2
                qb = qc * 512
                ptb = c % NPT
                p.wait("tensor", ex, exp_post[c])
                if i0 == 0:
                    p.wait("tensor", dve_t, state[("vtm", buf)])
                    p.wait("tensor", dve_z, yq_slot[yb][0])
                    p.wait("tensor", dve_y, yq_slot[yb][1])
                for i, (kt, jj) in enumerate(kts):
                    psl = slice(512 * i + jj * 128, 512 * (i + 1))
                    ysl = slice(jj * 128, 512)
                    p.op("tensor", lambda En, yb=yb, ptb=ptb, psl=psl, ysl=ysl,
                         h2=h2, kt=kt, first=(i0 == 0 and i == 0),
                         last=(i0 + i == nkts - 1), buf=buf: En.matmul(
                             ps_y[yb][:, ysl],
                             v_tm[buf][:, (h2 * NKT + kt) * 65:
                                       (h2 * NKT + kt) * 65 + 65],
                             PT[ptb][:, psl], start=first,
                             stop=last), inc=pe_av)
                pt_free[ptb] = pe_av.n
                if i0 + len(kts) == nkts:
                    for d in list(deferred):
                        d[1]()
                        deferred.remove(d)
                    av_post = pe_av.n
                    p.wait("vector", pe_av, av_post)
                    p.wait("vector", pe_zb, pe_zb.n)
                    p.op("vector", lambda En, yb=yb: En.tensor_scalar_max(
                        zr[64:65, :], ps_y[yb][64:65, :], 1e-35), inc=dve_z)
                    p.wait("scalar", dve_z, dve_z.n)
                    p.op("scalar", lambda En: En.activation(
                        zr[64:65, :], zr[64:65, :], AF.Ln), inc=act_zb)
                    p.wait("scalar", act_zb, act_zb.n)
                    p.op("scalar", lambda En: En.activation(
                        zr[64:65, :], zr[64:65, :], AF.Exp, scale=-1.0),
                        inc=dve_z)
                    z_post = dve_z.n

                    def stage2(z_post=z_post, yb=yb, oc=oc, h2=h2, qb=qb):
                        hs = slice(h2 * 64, (h2 + 1) * 64)
                        p.wait("tensor", dve_z, z_post)
                        p.wait("tensor", act_zb, act_zb.n)
                        p.op("tensor", lambda En: En.matmul(
                            aux[0:64, :], ones_f[64:65, 0:64], zr[64:65, :],
                            start=True, stop=True), inc=pe_zb)
                        p.wait("vector", pe_zb, pe_zb.n)
                        p.wait("vector", dve_y, dve_y.n)
                        p.op("vector", lambda En: En.tensor_copy(
                            zb_sb[0:64, :], aux[0:64, :]), inc=act_zb)
                        p.wait("vector", act_zb, act_zb.n)
                        p.op("vector", lambda En, yb=yb, oc=oc, hs=hs, qb=qb:
                             En.tensor_tensor(
                                 Y[oc][hs, qb:qb + 512], ps_y[yb][0:64, :],
                                 zb_sb[0:64, :], ALU.mult), inc=dve_y)
                        yq_slot[yb] = (dve_z.n, dve_y.n)
                    deferred.append([u + 6, stage2])

            cq = []
            for u in range(nu):
                if u >= LAG:
                    emit_AV(u - LAG, cq[u - LAG])
                cq.append(emit_S(u))
                for f in fill_at.get(u, ()):
                    f()
                for d in list(deferred):
                    if d[0] <= u:
                        d[1]()
                        deferred.remove(d)
            for u in range(nu - LAG, nu):
                if u >= 0:
                    emit_AV(u, cq[u])
            for d in deferred:
                d[1]()
            state[("nC_used", buf)] = ex.n
            state[("vtm_used", buf)] = pe_av.n
            return ex.n

        prev_ex = {0: 0, 1: 0}
        fillers = make_prep(0, 0, 0)
        for f in fillers:
            f()
        for oc in range(NCH):
            buf = oc % 2
            nxt = make_prep(oc + 1, 1 - buf, prev_ex[1 - buf]) \
                if oc + 1 < NCH else []
            prev_ex[buf] = run_jobs(oc, buf, nxt)

        y_done = dve_y.n
        with contextlib.ExitStack() as stk2:
            Eo2 = stk2.enter_context
            res_pan = None
            if cfg.resid_src is not None:
                rp0 = Eo2(nc.sbuf_tensor(f"{tag}rp", [128, 512], BF16))
                res_pan = [rp0, rp0]
            wo = Eo2(nc.sbuf_tensor(f"{tag}wo", [128, D], BF16))
            o_sb0 = Eo2(nc.sbuf_tensor(f"{tag}os", [128, 512],
                                       F32 if cfg.out_f32 else BF16))
            o_sb = [o_sb0, o_sb0]
            ld_r = [p.dsem("ldr0"), p.dsem("ldr1")]
            sto_s = [p.dsem("sts0"), p.dsem("sts1")]
            ld_wo = [p.dsem("lwo0"), p.dsem("lwo1")]
            evo = p.sem("evo")
            wo_free = [0, 0]
            rp_free = [0, 0]
            op_free = [0, 0]
            for oc in range(NCH):
                wb = 0
                p.wait("sync", pe_p, wo_free[0])
                p.dma("sync", wo[:].rearrange("p (ic o) -> p ic o", o=128),
                      cfg.ow(oc), ld_wo[0])
                ldo_post = ld_wo[0].n
                p.wait("tensor", ld_wo[0], ldo_post)
                p.wait("tensor", dve_y, y_done)
                for tc in range(NQC):
                    pb = 0
                    rb = tc % 2
                    if cfg.resid_src is not None:
                        p.wait("sync", evo, rp_free[0])
                        p.wait("sync", evo, rp_free[1])
                        p.dma("sync", res_pan[rb][:],
                              cfg.resid_src[tc * 512:(tc + 1) * 512,
                                            oc * 128:(oc + 1) * 128],
                              ld_r[rb], transpose=True)
                    p.wait("tensor", evq, pp_free[pb])
                    p.wait("tensor", dve_n, ppn_free[pb])
                    p.wait("tensor", evo, op_free[pb])
                    for ic in range(NCH):
                        p.op("tensor", lambda En, pb=pb, ic=ic, tc=tc, wb=wb:
                             En.matmul(
                                 ps_p[pb][:], wo[:, ic * 128:(ic + 1) * 128],
                                 Y[ic][:, tc * 512:(tc + 1) * 512],
                                 start=(ic == 0), stop=(ic == NCH - 1)), inc=pe_p)
                    if cfg.resid_src is not None:
                        p.wait("vector", pe_p, pe_p.n)
                        p.wait("vector", ld_r[rb], ld_r[rb].n)
                        p.wait("vector", sto_s[0], sto_s[0].n)
                        p.wait("vector", sto_s[1], sto_s[1].n)
                        p.op("vector", lambda En, pb=pb, rb=rb: En.tensor_tensor(
                            o_sb[pb][:], ps_p[pb][:], res_pan[rb][:], ALU.add),
                            inc=evo)
                        rp_free[rb] = evo.n
                        op_free[pb] = evo.n
                        p.wait("scalar", evo, evo.n)
                    else:
                        p.wait("scalar", pe_p, pe_p.n)
                        p.wait("scalar", sto_s[0], sto_s[0].n)
                        p.wait("scalar", sto_s[1], sto_s[1].n)
                        p.op("scalar", lambda En, pb=pb: En.activation(
                            o_sb[pb][:], ps_p[pb][:], AF.Copy), inc=evo)
                        op_free[pb] = evo.n
                        p.wait("scalar", evo, evo.n)
                    p.dma("scalar", cfg.out_d[oc * 128:(oc + 1) * 128,
                                              tc * 512:(tc + 1) * 512],
                          o_sb[pb][:], sto_s[pb])
                wo_free[0] = pe_p.n
            p.wait("scalar", sto_s[0], sto_s[0].n)
            p.wait("scalar", sto_s[1], sto_s[1].n)
    p.barrier(tag, reclaim=True)


def p_ln2_gate(p, y1_d, xn3_d, r_d, lnw_ap, lnb_ap, gw_ap, gb_ap,
               cast_gw, tag="p5"):
    nc = p.nc
    with contextlib.ExitStack() as stk:
        Eo = stk.enter_context
        yt = [[Eo(nc.sbuf_tensor(f"{tag}y{r}_{i}", [128, 512], F32))
               for i in range(NCH)] for r in range(2)]
        ysq = [Eo(nc.sbuf_tensor(f"{tag}sq{i}", [128, 512], F32)) for i in range(2)]
        t1 = Eo(nc.sbuf_tensor(f"{tag}t1", [128, 512], F32))
        rstd_b = Eo(nc.sbuf_tensor(f"{tag}rb", [128, 512], F32))
        x3 = [Eo(nc.sbuf_tensor(f"{tag}x3{i}", [128, 512], BF16)) for i in range(2)]
        rows = Eo(nc.sbuf_tensor(f"{tag}rw", [1, 512 * 4], F32))
        epsc = Eo(nc.sbuf_tensor(f"{tag}eps", [1, 1], F32))
        onef = Eo(nc.sbuf_tensor(f"{tag}onef", [128, 128], F32))
        oneb = Eo(nc.sbuf_tensor(f"{tag}oneb", [128, 128], BF16))
        identb = Eo(nc.sbuf_tensor(f"{tag}idb", [128, 128], BF16))
        gcol = [Eo(nc.sbuf_tensor(f"{tag}gc{i}", [128, 1], F32)) for i in range(NCH)]
        bcol = [Eo(nc.sbuf_tensor(f"{tag}bc{i}", [128, 1], F32)) for i in range(NCH)]
        wrow = Eo(nc.sbuf_tensor(f"{tag}wr", [1, D], F32))
        brow = Eo(nc.sbuf_tensor(f"{tag}br", [1, D], F32))
        ps_a = Eo(nc.psum_tensor(f"{tag}pa", [1, 512], F32))
        ps_b = Eo(nc.psum_tensor(f"{tag}pb", [1, 512], F32))
        ps_c = Eo(nc.psum_tensor(f"{tag}pc", [128, 512], F32))
        ps_d = Eo(nc.psum_tensor(f"{tag}pd", [128, 512], F32))
        ld_af = p.dsem("ldaf"); ld_y = [p.dsem("ldy0"), p.dsem("ldy1")]
        gld = p.dsem("gld")
        act = p.sem("act"); dve = p.sem("dve")
        pe = p.sem("pe"); po = p.sem("po"); sto = p.dsem("sto")
        sto_x = [p.dsem("stx0"), p.dsem("stx1")]
        evr = p.sem("evr")
        p.op("vector", lambda En: En.memset(epsc[:], float(EPS)), inc=dve)
        p.op("vector", lambda En: En.memset(onef[:], 1.0), inc=dve)
        p.op("vector", lambda En: En.memset(oneb[:], 1.0), inc=dve)
        p.op("vector", lambda En: En.memset(identb[:], 1.0), inc=dve)
        cst0 = dve.n
        p.wait("gpsimd", dve, cst0)
        p.op("gpsimd", lambda En: En.affine_select(
            identb[:], identb[:], [[-1, 128]], ALU.is_equal, 0.0,
            base=0, channel_multiplier=1), inc=po)
        ident_post = po.n
        p.dma("sync", wrow[:], lnw_ap.rearrange("(o d) -> o d", o=1), ld_af)
        p.dma("sync", brow[:], lnb_ap.rearrange("(o d) -> o d", o=1), ld_af)
        p.wait("tensor", ld_af, ld_af.n)
        p.wait("tensor", dve, cst0)
        for ic in range(NCH):
            for row, dstc in ((wrow, gcol[ic]), (brow, bcol[ic])):
                p.wait("tensor", act, act.n)
                p.op("tensor", lambda En, ic=ic, row=row: En.transpose(
                    ps_c[:, 0:1], row[0:1, ic * 128:(ic + 1) * 128],
                    onef[0:1, 0:1]), inc=pe)
                p.wait("scalar", pe, pe.n)
                p.op("scalar", lambda En, dstc=dstc: En.activation(
                    dstc[:], ps_c[:, 0:1], AF.Copy), inc=act)
        affc_post = act.n
        yt_free = [(0, 0, 0), (0, 0, 0)]
        for tc in range(T // 512):
            r = tc % 2
            p.wait("sync", pe, yt_free[r][0])
            p.wait("sync", act, yt_free[r][1])
            p.wait("sync", dve, yt_free[r][2])
            for ic in range(NCH):
                p.dma("sync", yt[r][ic][:],
                      y1_d[ic * 128:(ic + 1) * 128, tc * 512:(tc + 1) * 512],
                      ld_y[r])
            ld_post = ld_y[r].n
            for ic in range(NCH):
                p.wait("tensor", ld_y[r], ld_post)
                if ic == 0:
                    p.wait("tensor", dve, dve.n)
                p.op("tensor", lambda En, r=r, ic=ic: En.matmul(
                    ps_a[:], onef[:, 0:1], yt[r][ic][:],
                    start=(ic == 0), stop=(ic == NCH - 1)), inc=pe)
            for ic in range(NCH):
                sb2 = ic % 2
                p.wait("scalar", ld_y[r], ld_post)
                p.wait("scalar", pe, pe.n)
                p.op("scalar", lambda En, r=r, ic=ic, sb2=sb2: En.activation(
                    ysq[sb2][:], yt[r][ic][:], AF.Square), inc=act)
                p.wait("tensor", act, act.n)
                p.op("tensor", lambda En, sb2=sb2, ic=ic: En.matmul(
                    ps_b[:], onef[:, 0:1], ysq[sb2][:],
                    start=(ic == 0), stop=(ic == NCH - 1)), inc=pe)
            p.wait("vector", pe, pe.n)
            p.op("vector", lambda En: En.tensor_scalar(
                rows[0:1, 0:512], ps_a[:], 1.0 / D, None, ALU.mult), inc=dve)
            p.wait("vector", dve, dve.n)
            p.op("vector", lambda En: En.tensor_tensor(
                rows[0:1, 1536:2048], rows[0:1, 0:512], rows[0:1, 0:512],
                ALU.mult), inc=dve)
            p.wait("vector", dve, dve.n)
            p.op("vector", lambda En: En.scalar_tensor_tensor(
                rows[0:1, 512:1024], ps_b[:], 1.0 / D, rows[0:1, 1536:2048],
                ALU.mult, ALU.subtract), inc=dve)
            p.wait("scalar", dve, dve.n)
            p.op("scalar", lambda En: En.activation(
                rows[0:1, 1024:1536], rows[0:1, 512:1024], AF.Ln,
                bias=epsc[0:1, 0:1]), inc=act)
            p.wait("scalar", act, act.n)
            p.op("scalar", lambda En: En.activation(
                rows[0:1, 1024:1536], rows[0:1, 1024:1536], AF.Exp,
                scale=-0.5), inc=act)
            rstd_post = act.n
            p.wait("tensor", act, rstd_post)
            p.wait("tensor", dve, dve.n)
            p.op("tensor", lambda En: En.matmul(
                ps_c[:], onef[0:1, :], rows[0:1, 0:512], start=True, stop=True),
                inc=pe)
            p.op("tensor", lambda En: En.matmul(
                ps_d[:], onef[0:1, :], rows[0:1, 1024:1536], start=True,
                stop=True), inc=pe)
            bc_post = pe.n
            p.wait("scalar", bc_post if False else pe, bc_post)
            p.op("scalar", lambda En: En.activation(
                rstd_b[:], ps_d[:], AF.Copy), inc=act)
            rb_post = act.n
            for ic in range(NCH):
                xb = ic % 2
                p.wait("vector", dve, dve.n)
                p.wait("vector", act, rb_post)
                p.wait("vector", pe, bc_post)
                p.wait("vector", act, affc_post)
                p.wait("vector", sto_x[xb], sto_x[xb].n)
                p.op("vector", lambda En, r=r, ic=ic: En.tensor_tensor(
                    t1[:], yt[r][ic][:], ps_c[:], ALU.subtract), inc=dve)
                p.wait("vector", dve, dve.n)
                p.op("vector", lambda En: En.tensor_tensor(
                    t1[:], t1[:], rstd_b[:], ALU.mult), inc=dve)
                p.wait("vector", dve, dve.n)
                p.op("vector", lambda En, ic=ic, xb=xb: En.tensor_scalar(
                    x3[xb][:], t1[:], gcol[ic][:, 0:1], bcol[ic][:, 0:1],
                    ALU.mult, ALU.add), inc=dve)
                p.wait("scalar", dve, dve.n)
                p.dma("scalar", xn3_d[ic * 128:(ic + 1) * 128,
                                      tc * 512:(tc + 1) * 512], x3[xb][:],
                      sto_x[xb])
            yt_free[r] = (pe.n, act.n, dve.n)
        p.wait("scalar", sto_x[0], sto_x[0].n)
        p.wait("scalar", sto_x[1], sto_x[1].n)
        with contextlib.ExitStack() as stk2:
            Eo2 = stk2.enter_context
            x3p = [Eo2(nc.sbuf_tensor(f"{tag}xp{i}", [128, T], BF16))
                   for i in range(NCH)]
            xtf = Eo2(nc.sbuf_tensor(f"{tag}xtf", [128, 128], F32))
            gwt = Eo2(nc.sbuf_tensor(f"{tag}gw", [128, NCH * E], F32))
            gbr = Eo2(nc.sbuf_tensor(f"{tag}gb", [1, E], F32))
            io8i = Eo2(nc.sbuf_tensor(f"{tag}i8i", [128, E], I32))
            io8 = Eo2(nc.sbuf_tensor(f"{tag}i8", [128, E], F32))
            lg = Eo2(nc.sbuf_tensor(f"{tag}lg", [128, E], F32))
            mx = Eo2(nc.sbuf_tensor(f"{tag}mx", [128, 1], F32))
            eq = Eo2(nc.sbuf_tensor(f"{tag}eq", [128, E], F32))
            sel = Eo2(nc.sbuf_tensor(f"{tag}sel", [128, E], F32))
            rcl = Eo2(nc.sbuf_tensor(f"{tag}rcl", [128, 1], F32))
            rclb = Eo2(nc.sbuf_tensor(f"{tag}rcb", [128, 1], BF16))
            rrow = Eo2(nc.sbuf_tensor(f"{tag}rr", [1, T], BF16))
            ps_l = [Eo2(nc.psum_tensor(f"{tag}pl{i}", [128, 512], F32))
                    for i in range(2)]
            ps_r = Eo2(nc.psum_tensor(f"{tag}pr", [1, 128], BF16))
            for cs, cn in cast_gw:
                p.wait("sync", cs, cn)
            p.wait("sync", sto_x[0], sto_x[0].n)
            p.wait("sync", sto_x[1], sto_x[1].n)
            p.dma("sync", gwt[:].rearrange("p (ic e) -> p ic e", e=E),
                  gw_ap.rearrange("(ic p) e -> p ic e", p=128), gld)
            p.dma("sync", gbr[:], gb_ap.rearrange("(o e) -> o e", o=1), gld)
            for ic in range(NCH):
                p.dma("sync", x3p[ic][:], xn3_d[ic * 128:(ic + 1) * 128, :], gld)
            gld_post = gld.n
            p.op("gpsimd", lambda En: En.iota(
                io8i[:], [[1, E]], base=0, channel_multiplier=0), inc=po)
            p.wait("vector", po, po.n)
            p.op("vector", lambda En: En.tensor_copy(io8[:], io8i[:]), inc=dve)
            io_post = dve.n
            lg_free = [0, 0]
            xtf_post = 0
            for tt in range(NT):
                lb = tt % 2
                p.wait("tensor", gld, gld_post)
                p.wait("tensor", dve, lg_free[lb])
                for ic in range(NCH):
                    p.wait("scalar", gld, gld_post)
                    p.wait("scalar", pe, xtf_post)
                    p.op("scalar", lambda En, ic=ic, tt=tt: En.activation(
                        xtf[:], x3p[ic][:, tt * 128:(tt + 1) * 128], AF.Copy),
                        inc=act)
                    p.wait("tensor", act, act.n)
                    ent = p.op("tensor", lambda En, ic=ic, tt=tt, lb=lb: En.matmul(
                        ps_l[lb][:, 0:E], xtf[:],
                        gwt[:, ic * E:(ic + 1) * E],
                        start=(ic == 0), stop=False), inc=pe)
                    xtf_post = ent["post"]
                p.op("tensor", lambda En, lb=lb: En.matmul(
                    ps_l[lb][:, 0:E], onef[0:1, :], gbr[0:1, :],
                    start=False, stop=True), inc=pe)
                p.wait("vector", pe, pe.n)
                p.wait("vector", dve, io_post)
                p.op("vector", lambda En, lb=lb: En.tensor_copy(
                    lg[:], ps_l[lb][:, 0:E]), inc=dve)
                lg_free[lb] = dve.n
                p.wait("vector", dve, dve.n)
                p.op("vector", lambda En: En.reduce_max(mx[:], lg[:], AX.X),
                     inc=dve)
                p.wait("vector", dve, dve.n)
                p.op("vector", lambda En: En.tensor_scalar(
                    eq[:], lg[:], mx[:, 0:1], None, ALU.is_ge), inc=dve)
                p.wait("vector", dve, dve.n)
                p.op("vector", lambda En: En.tensor_scalar(
                    sel[:], eq[:], -999.0, 999.0, ALU.mult, ALU.add), inc=dve)
                p.wait("vector", dve, dve.n)
                p.op("vector", lambda En: En.tensor_add(sel[:], sel[:], io8[:]),
                     inc=dve)
                p.wait("vector", dve, dve.n)
                p.op("vector", lambda En: En.tensor_reduce(
                    rcl[:], sel[:], AX.X, ALU.min), inc=dve)
                p.wait("vector", dve, dve.n)
                p.wait("vector", evr, evr.n)
                p.op("vector", lambda En: En.tensor_copy(rclb[:], rcl[:]),
                     inc=dve)
                p.wait("tensor", dve, dve.n)
                p.wait("tensor", po, ident_post)
                p.wait("tensor", evr, evr.n)
                p.op("tensor", lambda En: En.transpose(
                    ps_r[:], rclb[:, 0:1], identb[:]), inc=pe)
                p.wait("vector", pe, pe.n)
                p.op("vector", lambda En, tt=tt: En.tensor_copy(
                    rrow[0:1, tt * 128:(tt + 1) * 128], ps_r[:]), inc=evr)
            p.wait("scalar", evr, evr.n)
            p.dma("scalar", r_d.rearrange("(o t) -> o t", o=1), rrow[:], sto)
            p.wait("scalar", sto, sto.n)
    p.barrier(tag, reclaim=True)


def p_moe(p, xn3_d, r_d, moe_d, w1_b, w2_b, h_d, h_d2, cast_w2, tag="p6"):
    nc = p.nc
    NHC = HID // 128
    NTC = T // 512
    hds = [h_d, h_d2]
    with contextlib.ExitStack() as stk:
        Eo = stk.enter_context
        acc = [Eo(nc.sbuf_tensor(f"{tag}ac{i}", [128, T], BF16))
               for i in range(NCH)]
        x3p = [Eo(nc.sbuf_tensor(f"{tag}xp{i}", [128, T], BF16))
               for i in range(NCH)]
        w2t = [Eo(nc.sbuf_tensor(f"{tag}w2{i}", [128, HID], BF16))
               for i in range(2)]
        hbl = [Eo(nc.sbuf_tensor(f"{tag}hb{i}", [128, 512], BF16))
               for i in range(NHC)]
        w1t = [Eo(nc.sbuf_tensor(f"{tag}w1{i}", [128, D], BF16))
               for i in range(2)]
        hst = [Eo(nc.sbuf_tensor(f"{tag}hs{i}", [128, 512], BF16))
               for i in range(3)]
        yst = [Eo(nc.sbuf_tensor(f"{tag}ys{i}", [128, 512], BF16))
               for i in range(3)]
        mei = [Eo(nc.sbuf_tensor(f"{tag}me{i}", [128, 512], I32))
               for i in range(2)]
        rrow = Eo(nc.sbuf_tensor(f"{tag}rr", [1, T], BF16))
        oneb = Eo(nc.sbuf_tensor(f"{tag}oneb", [1, 128], BF16))
        rbs = [Eo(nc.sbuf_tensor(f"{tag}rb{i}", [128, 512], BF16))
               for i in range(NTC)]
        ps_h = [Eo(nc.psum_tensor(f"{tag}ph{i}", [128, 512], F32))
                for i in range(3)]
        ps_y = [Eo(nc.psum_tensor(f"{tag}py{i}", [128, 512], F32))
                for i in range(3)]
        ps_rb = [Eo(nc.psum_tensor(f"{tag}prb{i}", [128, 512], F32))
                 for i in range(2)]
        pe = p.sem("pe"); act = p.sem("act"); dve = p.sem("dve")
        ldr = p.sem("ldr"); ldx = p.dsem("ldx")
        ldw1 = [p.dsem("lw0"), p.dsem("lw1")]
        ldw2 = [p.dsem("l20"), p.dsem("l21")]
        ldh = [p.dsem("ldh0"), p.dsem("ldh1")]
        sto1 = [p.dsem("s10"), p.dsem("s11"), p.dsem("s12")]
        sto = p.dsem("sto")
        for cs, cn in cast_w2:
            p.wait("sync", cs, cn)
        p.dma("sync", rrow[:], r_d.rearrange("(o t) -> o t", o=1), ldr)
        for ic in range(NCH):
            p.dma("sync", x3p[ic][:], xn3_d[ic * 128:(ic + 1) * 128, :], ldx)
        x3_post = ldx.n
        p.op("vector", lambda En: En.memset(oneb[:], 1.0), inc=dve)
        for i in range(NCH):
            p.op("vector", lambda En, i=i: En.memset(acc[i][:], 0.0), inc=dve)
        acc0 = dve.n
        rb_free = [0, 0]
        for tc in range(NTC):
            b = tc % 2
            p.wait("tensor", ldr, ldr.n)
            p.wait("tensor", dve, acc0)
            p.wait("tensor", act, rb_free[b])
            p.op("tensor", lambda En, b=b, tc=tc: En.matmul(
                ps_rb[b][:], oneb[:], rrow[0:1, tc * 512:(tc + 1) * 512],
                start=True, stop=True), inc=pe)
            p.wait("scalar", pe, pe.n)
            p.op("scalar", lambda En, b=b, tc=tc: En.activation(
                rbs[tc][:], ps_rb[b][:], AF.Copy), inc=act)
            rb_free[b] = act.n
        rb_post = act.n
        w1_free = [0, 0]
        psh_free = [0, 0, 0]
        w2_free = [0, 0]
        y_free = [0, 0, 0]
        yst_free = [0, 0, 0]
        me_post = [0, 0]
        w2_post = [0, 0]
        ldh_after = {}
        p1_done = {}
        for e in range(E):
            hd = hds[e % 2]
            if e >= 2:
                p.wait("scalar", ldh[0], ldh_after[e - 2][0])
                p.wait("scalar", ldh[1], ldh_after[e - 2][1])
            for hc in range(NHC):
                wb = hc % 2
                p.wait("sync", pe, w1_free[wb])
                p.dma("sync", w1t[wb][:].rearrange("p (ic o) -> p ic o", o=128),
                      w1_b[e][:, hc * 128:(hc + 1) * 128].rearrange(
                          "(ic p) o -> p ic o", p=128), ldw1[wb])
                w1_post = ldw1[wb].n
                p.wait("tensor", ldw1[wb], w1_post)
                if hc == 0 and e == 0:
                    p.wait("tensor", ldx, x3_post)
                for tc in range(NTC):
                    hb = tc % 3
                    p.wait("tensor", act, psh_free[hb])
                    for ic in range(NCH):
                        p.op("tensor", lambda En, hb=hb, wb=wb, ic=ic, tc=tc:
                             En.matmul(
                                 ps_h[hb][:],
                                 w1t[wb][:, ic * 128:(ic + 1) * 128],
                                 x3p[ic][:, tc * 512:(tc + 1) * 512],
                                 start=(ic == 0), stop=(ic == NCH - 1)),
                             inc=pe)
                    p.wait("scalar", pe, pe.n)
                    p.wait("scalar", sto1[hb], sto1[hb].n)
                    p.op("scalar", lambda En, hb=hb: En.activation(
                        hst[hb][:], ps_h[hb][:], AF.Gelu_apprx_tanh), inc=act)
                    psh_free[hb] = act.n
                    p.wait("scalar", act, act.n)
                    p.dma("scalar", hd[hc * 128:(hc + 1) * 128,
                                       tc * 512:(tc + 1) * 512],
                          hst[hb][:], sto1[hb])
                w1_free[wb] = pe.n
            p1_done[e] = (sto1[0].n, sto1[1].n, sto1[2].n)
            for si in range(3):
                p.wait("sync", sto1[si], p1_done[e][si])
            NHH = NHC // 2
            hlast = {}

            def load_half(tc2, half, gate_pe):
                p.wait("sync", pe, gate_pe)
                for hc in range(half * NHH, (half + 1) * NHH):
                    p.dma("sync", hbl[hc][:],
                          hd[hc * 128:(hc + 1) * 128,
                             tc2 * 512:(tc2 + 1) * 512], ldh[half])
                return ldh[half].n

            hp = {(0, 0): load_half(0, 0, 0)}
            hp[(0, 1)] = load_half(0, 1, 0)
            for tc in range(NTC):
                mb = tc % 2
                p.wait("vector", dve, me_post[mb])
                p.wait("vector", act, rb_post)
                p.op("vector", lambda En, mb=mb, tc=tc, e=e: En.tensor_scalar(
                    mei[mb][:], rbs[tc][:], float(e), None, ALU.is_equal),
                    inc=dve)
                mei_post = dve.n
                for oc in range(NCH):
                    yb = oc % 3
                    w2b_ = oc % 2
                    p.wait("sync", pe, w2_free[w2b_])
                    p.dma("sync",
                          w2t[w2b_][:].rearrange("p (hc o) -> p hc o", o=128),
                          w2_b[e][:, oc * 128:(oc + 1) * 128].rearrange(
                              "(hc p) o -> p hc o", p=128), ldw2[w2b_])
                    w2_post[w2b_] = ldw2[w2b_].n
                    p.wait("tensor", ldw2[w2b_], w2_post[w2b_])
                    p.wait("tensor", act, y_free[yb])
                    for half in range(2):
                        p.wait("tensor", ldh[half], hp[(tc, half)])
                        for hc in range(half * NHH, (half + 1) * NHH):
                            p.op("tensor", lambda En, yb=yb, w2b_=w2b_, hc=hc:
                                 En.matmul(
                                     ps_y[yb][:],
                                     w2t[w2b_][:, hc * 128:(hc + 1) * 128],
                                     hbl[hc][:],
                                     start=(hc == 0), stop=(hc == NHC - 1)),
                                 inc=pe)
                        if oc == NCH - 1 and tc + 1 < NTC:
                            hp[(tc + 1, half)] = load_half(tc + 1, half, pe.n)
                    w2_free[w2b_] = pe.n
                    p.wait("scalar", pe, pe.n)
                    p.wait("scalar", dve, yst_free[yb])
                    p.op("scalar", lambda En, yb=yb: En.activation(
                        yst[yb][:], ps_y[yb][:], AF.Copy), inc=act)
                    y_free[yb] = act.n
                    p.wait("vector", act, act.n)
                    p.wait("vector", dve, mei_post)
                    p.op("vector", lambda En, yb=yb, mb=mb, oc=oc, tc=tc:
                         En.copy_predicated(
                             acc[oc][:, tc * 512:(tc + 1) * 512],
                             mei[mb][:], yst[yb][:]), inc=dve)
                    yst_free[yb] = dve.n
                me_post[mb] = dve.n
            ldh_after[e] = (ldh[0].n, ldh[1].n)
        p.wait("scalar", dve, dve.n)
        for oc in range(NCH):
            p.dma("scalar", moe_d[oc * 128:(oc + 1) * 128, :], acc[oc][:], sto)
        p.wait("scalar", sto, sto.n)
    p.barrier(tag, reclaim=True)


def p_gsilu(p, y1_d, moe_d, g_d, tag="p7"):
    nc = p.nc
    with contextlib.ExitStack() as stk:
        Eo = stk.enter_context
        NB3 = 3
        yt = [Eo(nc.sbuf_tensor(f"{tag}y{i}", [128, 512], F32)) for i in range(NB3)]
        mt = [Eo(nc.sbuf_tensor(f"{tag}m{i}", [128, 512], BF16)) for i in range(NB3)]
        st_ = [Eo(nc.sbuf_tensor(f"{tag}s{i}", [128, 512], F32)) for i in range(NB3)]
        gt = [Eo(nc.sbuf_tensor(f"{tag}g{i}", [128, 512], BF16)) for i in range(NB3)]
        ld = [p.dsem(f"ld{i}") for i in range(NB3)]
        sto = [p.dsem(f"st{i}") for i in range(NB3)]
        dve = p.sem("dve"); act = p.sem("act")
        buf_free = [0] * NB3
        st_free = [0] * NB3
        for i in range(NCH * (T // 512)):
            oc, tc = divmod(i, T // 512)
            b = i % NB3
            p.wait("sync", dve, buf_free[b])
            p.wait("sync", act, st_free[b])
            sl = (slice(oc * 128, (oc + 1) * 128), slice(tc * 512, (tc + 1) * 512))
            p.dma("sync", yt[b][:], y1_d[sl[0], sl[1]], ld[b])
            p.dma("sync", mt[b][:], moe_d[sl[0], sl[1]], ld[b])
            p.wait("vector", ld[b], ld[b].n)
            p.op("vector", lambda En, b=b: En.tensor_add(
                st_[b][:], yt[b][:], mt[b][:]), inc=dve)
            buf_free[b] = dve.n
            p.wait("scalar", dve, dve.n)
            p.wait("scalar", sto[b], sto[b].n)
            p.op("scalar", lambda En, b=b: En.activation(
                gt[b][:], st_[b][:], AF.Silu), inc=act)
            st_free[b] = act.n
            p.wait("scalar", act, act.n)
            p.dma("scalar", g_d[sl[0], sl[1]], gt[b][:], sto[b])
        p.wait("scalar", sto[0], sto[0].n)
        p.wait("scalar", sto[1], sto[1].n)
    p.barrier(tag, reclaim=True)


def p_resln(p, x_ap, s_d, g_d, x1_d, xn1_d, tag="p8"):
    nc = p.nc
    with contextlib.ExitStack() as stk:
        Eo = stk.enter_context
        NB = 3
        xt = [Eo(nc.sbuf_tensor(f"{tag}x{i}", [128, D], F32)) for i in range(NB)]
        st_ = [Eo(nc.sbuf_tensor(f"{tag}s{i}", [128, D], BF16)) for i in range(NB)]
        gt = [Eo(nc.sbuf_tensor(f"{tag}g{i}", [128, D], BF16)) for i in range(NB)]
        x1t = [Eo(nc.sbuf_tensor(f"{tag}o{i}", [128, D], F32)) for i in range(NB)]
        xnt = [Eo(nc.sbuf_tensor(f"{tag}n{i}", [128, D], BF16)) for i in range(NB)]
        xsq = [Eo(nc.sbuf_tensor(f"{tag}q{i}", [128, D], F32)) for i in range(2)]
        scol = [Eo(nc.sbuf_tensor(f"{tag}sc{i}", [128, 1], F32)) for i in range(NB)]
        qcol = [Eo(nc.sbuf_tensor(f"{tag}qc{i}", [128, 1], F32)) for i in range(NB)]
        mcol = [Eo(nc.sbuf_tensor(f"{tag}mc{i}", [128, 1], F32)) for i in range(NB)]
        vcol = [Eo(nc.sbuf_tensor(f"{tag}vc{i}", [128, 1], F32)) for i in range(NB)]
        rcol = [Eo(nc.sbuf_tensor(f"{tag}rc{i}", [128, 1], F32)) for i in range(NB)]
        epsc = Eo(nc.sbuf_tensor(f"{tag}eps", [128, 1], F32))
        ld = [p.dsem(f"ld{i}") for i in range(NB)]
        sto1 = [p.dsem(f"sa{i}") for i in range(NB)]
        sto2 = [p.dsem(f"sb{i}") for i in range(NB)]
        act = p.sem("act"); dve = p.sem("dve")
        rstd = p.sem("rstd"); app = p.sem("app")
        cst = p.sem("cst")
        p.op("vector", lambda En: En.memset(epsc[:], float(EPS)), inc=cst)
        p.wait("scalar", cst, cst.n)
        sq_post = {}
        app_post = {}
        for i in range(NT):
            b = i % NB
            if i >= NB:
                p.wait("sync", app, app_post[i - NB])
                p.wait("sync", act, sq_post[i - NB])
                p.wait("sync", sto1[b], sto1[b].n)
            p.dma("sync", xt[b][:], x_ap[i * 128:(i + 1) * 128, :], ld[b])
            p.dma("sync", st_[b][:], s_d[:, i * 128:(i + 1) * 128], ld[b],
                  transpose=True)
            p.dma("sync", gt[b][:], g_d[:, i * 128:(i + 1) * 128], ld[b],
                  transpose=True)
            p.wait("vector", ld[b], ld[b].n)
            p.op("vector", lambda En, b=b: En.tensor_mul(
                x1t[b][:], st_[b][:], gt[b][:]), inc=dve)
            p.wait("vector", dve, dve.n)
            p.op("vector", lambda En, b=b: En.tensor_add(
                x1t[b][:], x1t[b][:], xt[b][:]), inc=dve)
            x1p = dve.n
            p.wait("scalar", dve, x1p)
            p.dma("scalar", x1_d[i * 128:(i + 1) * 128, :], x1t[b][:], sto1[b])
            p.wait("vector", dve, x1p)
            p.op("vector", lambda En, b=b: En.reduce_sum(
                scol[b][:], x1t[b][:], AX.X), inc=dve)
            p.wait("vector", dve, dve.n)
            p.op("vector", lambda En, b=b: En.tensor_scalar(
                mcol[b][:], scol[b][:], 1.0 / D, None, ALU.mult), inc=dve)
            p.wait("scalar", dve, dve.n)
            ent = p.op("scalar", lambda En, b=b, i=i: En.activation(
                xsq[i % 2][:], x1t[b][:], AF.Square, accum_out=qcol[b][:, 0:1]),
                inc=act)
            sq_post[i] = ent["post"]
            p.wait("vector", dve, dve.n)
            p.wait("vector", act, act.n)
            p.op("vector", lambda En, b=b: En.tensor_tensor(
                vcol[b][:], mcol[b][:], mcol[b][:], ALU.mult), inc=dve)
            p.wait("vector", dve, dve.n)
            p.op("vector", lambda En, b=b: En.scalar_tensor_tensor(
                vcol[b][:], qcol[b][:], 1.0 / D, vcol[b][:], ALU.mult,
                ALU.subtract), inc=dve)
            p.wait("scalar", dve, dve.n)
            p.op("scalar", lambda En, b=b: En.activation(
                rcol[b][:], vcol[b][:], AF.Ln, bias=epsc[:, 0:1]), inc=act)
            p.wait("scalar", act, act.n)
            p.op("scalar", lambda En, b=b: En.activation(
                rcol[b][:], rcol[b][:], AF.Exp, scale=-0.5), inc=rstd)
            p.wait("vector", rstd, rstd.n)
            p.wait("vector", sto2[b], sto2[b].n)
            p.op("vector", lambda En, b=b: En.tensor_scalar(
                xnt[b][:], x1t[b][:], mcol[b][:, 0:1], rcol[b][:, 0:1],
                ALU.subtract, ALU.mult), inc=app)
            app_post[i] = app.n
            p.wait("scalar", app, app.n)
            p.dma("scalar", xn1_d[i * 128:(i + 1) * 128, :], xnt[b][:], sto2[b])
        for s in sto1 + sto2:
            p.wait("scalar", s, s.n)
    p.barrier(tag, reclaim=True)


def p_l1ffn(p, xn1_d, wg_b, wu_b, wd_b, g1_d, cast_wd, tag="p11"):
    nc = p.nc
    NH = FHID // 128
    with contextlib.ExitStack() as stk:
        Eo = stk.enter_context
        wgp = [Eo(nc.sbuf_tensor(f"{tag}wg{i}", [128, FHID], BF16))
               for i in range(NCH)]
        wup = [Eo(nc.sbuf_tensor(f"{tag}wu{i}", [128, FHID], BF16))
               for i in range(NCH)]
        wdp = [Eo(nc.sbuf_tensor(f"{tag}wd{i}", [128, D], BF16))
               for i in range(NH)]
        xin = [Eo(nc.sbuf_tensor(f"{tag}xi{i}", [128, 512], BF16))
               for i in range(NCH)]
        hsb = [Eo(nc.sbuf_tensor(f"{tag}h{i}", [128, 512], BF16))
               for i in range(NH)]
        gst = [Eo(nc.sbuf_tensor(f"{tag}gs{i}", [128, 512], BF16))
               for i in range(2)]
        ost = [Eo(nc.sbuf_tensor(f"{tag}os{i}", [128, 512], BF16))
               for i in range(2)]
        ps_g = [Eo(nc.psum_tensor(f"{tag}pg{i}", [128, 512], F32))
                for i in range(2)]
        ps_u = [Eo(nc.psum_tensor(f"{tag}pu{i}", [128, 512], F32))
                for i in range(2)]
        ps_o = [Eo(nc.psum_tensor(f"{tag}po{i}", [128, 512], F32))
                for i in range(2)]
        ld = p.dsem("ld"); ldx = p.dsem("ldx"); pe = p.sem("pe")
        act = p.sem("act"); dve = p.sem("dve")
        sto = [p.dsem("st0"), p.dsem("st1")]
        for cs, cn in cast_wd:
            p.wait("sync", cs, cn)
        for ic in range(NCH):
            p.dma("sync", wgp[ic][:], wg_b[:, ic * 128:(ic + 1) * 128], ld,
                  transpose=True)
            p.dma("sync", wup[ic][:], wu_b[:, ic * 128:(ic + 1) * 128], ld,
                  transpose=True)
        for hc in range(NH):
            p.dma("sync", wdp[hc][:], wd_b[:, hc * 128:(hc + 1) * 128], ld,
                  transpose=True)
        w_post = ld.n
        g_free = [0, 0]
        o_free = [0, 0]
        ost_free = [0, 0]
        for tc in range(T // 512):
            if tc > 0:
                p.wait("sync", pe, pe.n)
            for ic in range(NCH):
                p.dma("sync", xin[ic][:],
                      xn1_d[tc * 512:(tc + 1) * 512, ic * 128:(ic + 1) * 128],
                      ldx, transpose=True)
            x_post = ldx.n
            hsb_post = 0
            for hc in range(NH):
                gb = hc % 2
                p.wait("tensor", ld, w_post)
                p.wait("tensor", ldx, x_post)
                p.wait("tensor", dve, g_free[gb])
                for ic in range(NCH):
                    p.op("tensor", lambda En, gb=gb, hc=hc, ic=ic: En.matmul(
                        ps_g[gb][:], wgp[ic][:, hc * 128:(hc + 1) * 128],
                        xin[ic][:], start=(ic == 0), stop=(ic == NCH - 1)),
                        inc=pe)
                for ic in range(NCH):
                    p.op("tensor", lambda En, gb=gb, hc=hc, ic=ic: En.matmul(
                        ps_u[gb][:], wup[ic][:, hc * 128:(hc + 1) * 128],
                        xin[ic][:], start=(ic == 0), stop=(ic == NCH - 1)),
                        inc=pe)
                p.wait("scalar", pe, pe.n)
                p.wait("scalar", dve, g_free[gb])
                p.op("scalar", lambda En, gb=gb: En.activation(
                    gst[gb][:], ps_g[gb][:], AF.Silu, scale=1.0 / 127.0),
                    inc=act)
                p.wait("vector", act, act.n)
                p.op("vector", lambda En, gb=gb, hc=hc: En.scalar_tensor_tensor(
                    hsb[hc][:], ps_u[gb][:], 1.0 / 127.0, gst[gb][:],
                    ALU.mult, ALU.mult), inc=dve)
                g_free[gb] = dve.n
            h_post = dve.n
            for oc in range(NCH):
                ob = oc % 2
                p.wait("tensor", dve, h_post)
                p.wait("tensor", act, o_free[ob])
                for hc in range(NH):
                    p.op("tensor", lambda En, ob=ob, oc=oc, hc=hc: En.matmul(
                        ps_o[ob][:], wdp[hc][:, oc * 128:(oc + 1) * 128],
                        hsb[hc][:], start=(hc == 0), stop=(hc == NH - 1)),
                        inc=pe)
                p.wait("scalar", pe, pe.n)
                p.wait("scalar", sto[ob], sto[ob].n)
                p.op("scalar", lambda En, ob=ob: En.activation(
                    ost[ob][:], ps_o[ob][:], AF.Silu, scale=1.0 / 127.0),
                    inc=act)
                o_free[ob] = act.n
                p.wait("scalar", act, act.n)
                p.dma("scalar", g1_d[oc * 128:(oc + 1) * 128,
                                     tc * 512:(tc + 1) * 512], ost[ob][:],
                      sto[ob])
        p.wait("scalar", sto[0], sto[0].n)
        p.wait("scalar", sto[1], sto[1].n)
    p.barrier(tag, reclaim=True)


def p_out(p, x1_d, s1_d, g1_d, out_ap, tag="p12"):
    nc = p.nc
    with contextlib.ExitStack() as stk:
        Eo = stk.enter_context
        NB = 3
        xt = [Eo(nc.sbuf_tensor(f"{tag}x{i}", [128, D], F32)) for i in range(NB)]
        st_ = [Eo(nc.sbuf_tensor(f"{tag}s{i}", [128, D], BF16)) for i in range(NB)]
        gt = [Eo(nc.sbuf_tensor(f"{tag}g{i}", [128, D], BF16)) for i in range(NB)]
        ot = [Eo(nc.sbuf_tensor(f"{tag}o{i}", [128, D], F32)) for i in range(NB)]
        ld = [p.dsem(f"ld{i}") for i in range(NB)]
        sto = [p.dsem(f"st{i}") for i in range(NB)]
        dve = p.sem("dve")
        dv_post = {}
        for i in range(NT):
            b = i % NB
            if i >= NB:
                p.wait("sync", dve, dv_post[i - NB])
                p.wait("sync", sto[b], sto[b].n)
            p.dma("sync", xt[b][:], x1_d[i * 128:(i + 1) * 128, :], ld[b])
            p.dma("sync", st_[b][:], s1_d[:, i * 128:(i + 1) * 128], ld[b],
                  transpose=True)
            p.dma("sync", gt[b][:], g1_d[:, i * 128:(i + 1) * 128], ld[b],
                  transpose=True)
            p.wait("vector", ld[b], ld[b].n)
            p.op("vector", lambda En, b=b: En.tensor_mul(
                ot[b][:], st_[b][:], gt[b][:]), inc=dve)
            p.wait("vector", dve, dve.n)
            p.op("vector", lambda En, b=b: En.tensor_add(
                ot[b][:], ot[b][:], xt[b][:]), inc=dve)
            dv_post[i] = dve.n
            p.wait("scalar", dve, dve.n)
            p.dma("scalar", out_ap[i * 128:(i + 1) * 128, :], ot[b][:], sto[b])
        for s in sto:
            p.wait("scalar", s, s.n)
    p.barrier(tag, reclaim=True)


def declare_scratch(nc):
    d = {}
    def mk(name, shape, dt):
        d[name] = nc.dram_tensor(name, shape, dt, kind="Internal").ap()
    mk("xn_d", [T, D], BF16)
    mk("xn2_d", [T, D], BF16)
    mk("s_d", [D, T], BF16)
    mk("y1_d", [D, T], F32)
    mk("xn3_d", [D, T], BF16)
    mk("r_d", [T], BF16)
    mk("moe_d", [D, T], BF16)
    mk("g_d", [D, T], BF16)
    mk("x1_d", [T, D], F32)
    mk("xn1_d", [T, D], BF16)
    mk("h_d", [HID, T], BF16)
    mk("h_d2", [HID, T], BF16)
    mk("lat_d", [256, T], BF16)
    mk("vd_d", [128, T], BF16)
    mk("s1_d", [D, T], BF16)
    mk("g1_d", [D, T], BF16)
    mk("qw_b", [D, D], BF16)
    mk("kw_b", [D, D], BF16)
    mk("vw_b", [D, D], BF16)
    mk("ow_b", [D, D], BF16)
    mk("wqkv_b", [D, 3 * D], BF16)
    mk("wproj_b", [D, D], BF16)
    mk("gw_b", [D, E], BF16)
    mk("gb_b", [E], BF16)
    mk("w1_b", [E, D, HID], BF16)
    mk("w2_b", [E, HID, D], BF16)
    mk("l1q_b", [D, D], BF16)
    mk("l1kv_b", [D, 256], BF16)
    mk("l1ku_b", [256, D], BF16)
    mk("l1vu_b", [256, D], BF16)
    mk("l1o_b", [D, D], BF16)
    mk("wg_b", [FHID, D], BF16)
    mk("wu_b", [FHID, D], BF16)
    mk("wd_b", [D, FHID], BF16)
    return d


def declare_inputs(nc):
    a = {}
    def mk(name, shape, dt):
        a[name] = nc.dram_tensor(name, shape, dt, kind="ExternalInput").ap()
    mk("x", [T, D], F32)
    mk("L0_attn_q_w", [D, D], F32)
    mk("L0_attn_k_w", [D, D], F32)
    mk("L0_attn_v_w", [D, D], F32)
    mk("L0_attn_o_w", [D, D], F32)
    mk("L0_ffn_ln1_w", [D], F32)
    mk("L0_ffn_ln1_b", [D], F32)
    mk("L0_ffn_w_qkv", [D, 3 * D], F32)
    mk("L0_ffn_w_proj", [D, D], F32)
    mk("L0_ffn_ln2_w", [D], F32)
    mk("L0_ffn_ln2_b", [D], F32)
    mk("L0_ffn_gate_w", [D, E], F32)
    mk("L0_ffn_gate_b", [E], F32)
    mk("L0_ffn_w1", [E, D, HID], F32)
    mk("L0_ffn_w2", [E, HID, D], F32)
    mk("L1_attn_Wq", [D, H, DKH], F32)
    mk("L1_attn_Wkv_down", [D, 256], F32)
    mk("L1_attn_Wk_up", [256, H, DKH], F32)
    mk("L1_attn_Wv_up", [256, H, DKH], F32)
    mk("L1_attn_Wout", [H, DKH, D], F32)
    mk("L1_ffn_w_gate", [FHID, D], I32)
    mk("L1_ffn_w_up", [FHID, D], I32)
    mk("L1_ffn_w_down", [D, FHID], I32)
    return a


CASTS = [
    ("L0_attn_q_w", "qw_b"), ("L0_attn_k_w", "kw_b"), ("L0_attn_v_w", "vw_b"),
    ("L0_attn_o_w", "ow_b"),
    ("L0_ffn_w_qkv", "wqkv_b"), ("L0_ffn_w_proj", "wproj_b"),
    ("L1_attn_Wq", "l1q_b"), ("L1_attn_Wkv_down", "l1kv_b"),
    ("L1_attn_Wk_up", "l1ku_b"), ("L1_attn_Wv_up", "l1vu_b"),
    ("L1_attn_Wout", "l1o_b"),
    ("L1_ffn_w_gate", "wg_b"), ("L1_ffn_w_up", "wu_b"),
    ("L1_ffn_w_down", "wd_b"),
    ("L0_ffn_w1", "w1_b"), ("L0_ffn_w2", "w2_b"),
]


def emit_casts(p, ins, dram):
    after = {}
    for src, dst in CASTS:
        sap = ins[src]
        dap = dram[dst]
        cs = p.sem("c_" + dst, persist=True)
        if src in ("L0_ffn_w1", "L0_ffn_w2"):
            for e in range(E):
                p.dma("gpsimd", dap[e], sap[e], cs)
        elif src in ("L1_attn_Wq", "L1_attn_Wk_up", "L1_attn_Wv_up"):
            p.dma("gpsimd", dap, sap.rearrange("a h k -> a (h k)"), cs)
        elif src == "L1_attn_Wout":
            p.dma("gpsimd", dap, sap.rearrange("h k o -> (h k) o"), cs)
        else:
            p.dma("gpsimd", dap, sap, cs)
        after[dst] = (cs, cs.n)
    return after


def build_launch(phase_fns):
    nc = bass.Bass("TRN2", target_bir_lowering=False)
    with contextlib.ExitStack() as stack:
        p = Prog(nc, stack)
        ins = declare_inputs(nc)
        dram = declare_scratch(nc)
        out = nc.dram_tensor("out", [T, D], F32, kind="ExternalOutput").ap()
        for f in phase_fns:
            f(p, ins, dram, out)
        p.emit()
    return nc


def wcol(ap, oc, off=0):
    return ap[:, off + oc * 128: off + (oc + 1) * 128].rearrange(
        "(ic p) o -> p ic o", p=128)


def all_phases(p, ins, dram, out):
    ca = emit_casts(p, ins, dram)
    p_ln_tokmajor(p, ins["x"], dram["xn_d"], F32, "p01")
    p_attn(p, AttnCfg(
        tag="a0", v_d=dram["vd_d"], src_d=dram["xn_d"], out_d=dram["s_d"], causal=True, alibi=True,
        qw=lambda oc: wcol(dram["qw_b"], oc),
        kw_=lambda oc: wcol(dram["kw_b"], oc),
        vw=lambda oc: wcol(dram["vw_b"], oc),
        ow=lambda oc: wcol(dram["ow_b"], oc),
        wait_cast=[ca["qw_b"], ca["kw_b"], ca["vw_b"], ca["ow_b"]]))
    p_ln_tokmajor(p, dram["xn_d"], dram["xn2_d"], BF16, "p3")
    p_attn(p, AttnCfg(
        tag="a1", v_d=dram["vd_d"], src_d=dram["xn2_d"], out_d=dram["y1_d"], causal=False,
        alibi=False, out_f32=True, resid_src=dram["xn_d"],
        affine=(ins["L0_ffn_ln1_w"], ins["L0_ffn_ln1_b"]),
        qw=lambda oc: wcol(dram["wqkv_b"], oc, 0),
        kw_=lambda oc: wcol(dram["wqkv_b"], oc, D),
        vw=lambda oc: wcol(dram["wqkv_b"], oc, 2 * D),
        ow=lambda oc: wcol(dram["wproj_b"], oc),
        wait_cast=[ca["wqkv_b"], ca["wproj_b"]]))
    p_ln2_gate(p, dram["y1_d"], dram["xn3_d"], dram["r_d"],
               ins["L0_ffn_ln2_w"], ins["L0_ffn_ln2_b"],
               ins["L0_ffn_gate_w"], ins["L0_ffn_gate_b"], [])
    p_moe(p, dram["xn3_d"], dram["r_d"], dram["moe_d"],
          dram["w1_b"], dram["w2_b"], dram["h_d"], dram["h_d2"],
          [ca["w1_b"], ca["w2_b"]])
    p_gsilu(p, dram["y1_d"], dram["moe_d"], dram["g_d"])
    p_resln(p, ins["x"], dram["s_d"], dram["g_d"], dram["x1_d"], dram["xn1_d"])
    p_attn(p, AttnCfg(
        tag="a2", v_d=dram["vd_d"], src_d=dram["xn1_d"], out_d=dram["s1_d"], causal=True,
        alibi=False,
        qw=lambda oc: wcol(dram["l1q_b"], oc),
        kw_=None, vw=None,
        latent=dict(dw=lambda ocl: wcol(dram["l1kv_b"], ocl),
                    ku=lambda oc: wcol(dram["l1ku_b"], oc),
                    vu=lambda oc: wcol(dram["l1vu_b"], oc),
                    lat_d=dram["lat_d"]),
        ow=lambda oc: wcol(dram["l1o_b"], oc),
        wait_cast=[ca["l1q_b"], ca["l1kv_b"], ca["l1ku_b"], ca["l1vu_b"],
                   ca["l1o_b"]]))
    p_l1ffn(p, dram["xn1_d"], dram["wg_b"], dram["wu_b"], dram["wd_b"],
            dram["g1_d"], [ca["wg_b"], ca["wu_b"], ca["wd_b"]])
    p_out(p, dram["x1_d"], dram["s1_d"], dram["g1_d"], out)


def kernel(**inputs):
    xfull = np.ascontiguousarray(inputs["L0_attn_x"])
    weights = {k: np.ascontiguousarray(v) for k, v in inputs.items()
               if k != "L0_attn_x"}
    in_maps = []
    for c in range(8):
        m = {"x": xfull[c]}
        m.update(weights)
        in_maps.append(m)
    nc = build_launch([all_phases])
    res = run_bass_kernel_spmd(nc, in_maps, core_ids=list(range(8)))
    return np.stack([r["out"] for r in res.results], axis=0)


# version 10
# speedup vs baseline: 1.1873x; 1.1873x over previous
import contextlib
import math
import numpy as np

import concourse.bass as bass
import concourse.mybir as mybir
from concourse.bass_utils import run_bass_kernel_spmd

F32 = mybir.dt.float32
BF16 = mybir.dt.bfloat16
I32 = mybir.dt.int32
AF = mybir.ActivationFunctionType
ALU = mybir.AluOpType
AX = mybir.AxisListType

ENGINES = ("sync", "scalar", "vector", "tensor", "gpsimd")

T = 4096
D = 1024
NT = T // 128
NCH = D // 128


def set_dims(T_):
    global T, NT
    T = T_
    NT = T // 128
H = 16
DKH = 64
HID = 4096
E = 8
FHID = 3072
EPS = 1e-5
NEG = -1e30


class Sem:
    def __init__(self, prog, name):
        self.h = prog.stack.enter_context(prog.nc.semaphore(name))
        self.n = 0
        self.dma_issuers = set()


class Prog:

    def __init__(self, nc, stack):
        self.nc = nc
        self.stack = stack
        self.entries = {e: [] for e in ENGINES}
        self._semid = 0
        self._all_sems = []
        self._pool = {False: [], True: []}
        self._phase_sems = []
        self._bar = None
        self._barcnt = 0
        self._qsems = {}

    def sem(self, tag="", persist=False, dma=False):
        if not persist and self._pool[dma]:
            s = self._pool[dma].pop()
        else:
            self._semid += 1
            s = Sem(self, f"s{self._semid}{tag}")
            self._all_sems.append(s)
            s.is_dma = dma
        if not persist:
            self._phase_sems.append(s)
        return s

    def dsem(self, tag=""):
        return self.sem(tag, dma=True)

    def op(self, eng, fn, inc=None, amt=1):
        ent = {"fn": fn, "sem": inc, "amt": amt if inc is not None else 0,
               "wait": None}
        if inc is not None:
            inc.n += amt
            ent["post"] = inc.n
        self.entries[eng].append(ent)
        return ent

    def wait(self, eng, sem, n=None):
        tgt = sem.n if n is None else n
        self.entries[eng].append({"fn": None, "sem": None, "amt": 0,
                                  "wait": (sem, tgt)})
        return tgt

    def dma(self, eng, out, in_, inc, **kw):
        inc.dma_issuers.add(eng)
        return self.op(eng,
                       lambda En, o=out, i=in_, k=dict(kw): En.dma_start(out=o, in_=i, **k),
                       inc=inc, amt=16)

    def barrier(self, tag="", reclaim=False):
        if self._bar is None:
            self._bar = self.sem("bar", persist=True)
            for e in ENGINES:
                self._qsems[e] = self.sem("q" + e, persist=True)
        bar = self._bar
        for s in self._all_sems:
            for e in s.dma_issuers:
                self.wait(e, s, s.n)
        for e in ENGINES:
            last = None
            for ent in reversed(self.entries[e]):
                if ent["fn"] is not None:
                    last = ent
                    break
            if last is not None and last["sem"] is not bar:
                if last["sem"] is None:
                    q = self._qsems[e]
                    q.n += 1
                    last["sem"] = q
                    last["amt"] = 1
                    last["post"] = q.n
                self.wait(e, last["sem"], last["post"])
            self.op(e, lambda En: En.nop(), inc=bar, amt=1)
        self._barcnt += 5
        for e in ENGINES:
            self.wait(e, bar, self._barcnt)
        if reclaim:
            for s in self._phase_sems:
                self._pool[getattr(s, "is_dma", False)].append(s)
            self._phase_sems = []

    def emit(self):
        nc = self.nc
        handles = {"sync": nc.sync, "scalar": nc.scalar, "vector": nc.vector,
                   "tensor": nc.tensor, "gpsimd": nc.gpsimd}

        def run(e):
            En = handles[e]
            for ent in self.entries[e]:
                if ent["wait"] is not None:
                    s, t = ent["wait"]
                    En.wait_ge(s.h, t)
                    continue
                r = ent["fn"](En)
                if ent["sem"] is not None:
                    r.then_inc(ent["sem"].h, ent["amt"])

        with nc.Block() as block:
            block.sync(lambda _: run("sync"))
            block.scalar(lambda _: run("scalar"))
            block.vector(lambda _: run("vector"))
            block.tensor(lambda _: run("tensor"))
            block.gpsimd(lambda _: run("gpsimd"))


def p_ln_tokmajor(p, x_ap, out_d, in_dtype, tag):
    nc = p.nc
    with contextlib.ExitStack() as stk:
        Eo = stk.enter_context
        NB = 3
        xt = [Eo(nc.sbuf_tensor(f"{tag}x{i}", [128, D], in_dtype)) for i in range(NB)]
        xnt = [Eo(nc.sbuf_tensor(f"{tag}n{i}", [128, D], BF16)) for i in range(NB)]
        xsq = [Eo(nc.sbuf_tensor(f"{tag}q{i}", [128, D], F32)) for i in range(2)]
        scol = [Eo(nc.sbuf_tensor(f"{tag}s{i}", [128, 1], F32)) for i in range(NB)]
        qcol = [Eo(nc.sbuf_tensor(f"{tag}c{i}", [128, 1], F32)) for i in range(NB)]
        mcol = [Eo(nc.sbuf_tensor(f"{tag}m{i}", [128, 1], F32)) for i in range(NB)]
        vcol = [Eo(nc.sbuf_tensor(f"{tag}v{i}", [128, 1], F32)) for i in range(NB)]
        rcol = [Eo(nc.sbuf_tensor(f"{tag}r{i}", [128, 1], F32)) for i in range(NB)]
        epsc = Eo(nc.sbuf_tensor(f"{tag}eps", [128, 1], F32))
        ld = [p.dsem(f"ld{i}") for i in range(NB)]
        sto = [p.dsem(f"st{i}") for i in range(NB)]
        act = p.sem("act")
        dve = p.sem("dve")
        rstd = p.sem("rstd")
        app = p.sem("app")
        cst = p.sem("cst")
        p.op("vector", lambda En: En.memset(epsc[:], float(EPS)), inc=cst)
        p.wait("scalar", cst, cst.n)
        sq_post = {}
        app_post = {}
        for i in range(NT):
            b = i % NB
            if i >= NB:
                p.wait("sync", app, app_post[i - NB])
                p.wait("sync", act, sq_post[i - NB])
            p.dma("sync", xt[b][:], x_ap[i * 128:(i + 1) * 128, :], ld[b])
            p.wait("vector", ld[b], ld[b].n)
            p.op("vector", lambda En, b=b: En.reduce_sum(scol[b][:], xt[b][:], AX.X),
                 inc=dve)
            p.wait("vector", dve, dve.n)
            p.op("vector", lambda En, b=b: En.tensor_scalar(
                mcol[b][:], scol[b][:], 1.0 / D, None, ALU.mult), inc=dve)
            p.wait("scalar", ld[b], ld[b].n)
            ent = p.op("scalar", lambda En, b=b, i=i: En.activation(
                xsq[i % 2][:], xt[b][:], AF.Square, accum_out=qcol[b][:, 0:1]),
                inc=act)
            sq_post[i] = ent["post"]
            p.wait("vector", dve, dve.n)
            p.wait("vector", act, act.n)
            p.op("vector", lambda En, b=b: En.tensor_tensor(
                vcol[b][:], mcol[b][:], mcol[b][:], ALU.mult), inc=dve)
            p.wait("vector", dve, dve.n)
            p.op("vector", lambda En, b=b: En.scalar_tensor_tensor(
                vcol[b][:], qcol[b][:], 1.0 / D, vcol[b][:], ALU.mult,
                ALU.subtract), inc=dve)
            p.wait("scalar", dve, dve.n)
            p.op("scalar", lambda En, b=b: En.activation(
                rcol[b][:], vcol[b][:], AF.Ln, bias=epsc[:, 0:1]), inc=act)
            p.wait("scalar", act, act.n)
            p.op("scalar", lambda En, b=b: En.activation(
                rcol[b][:], rcol[b][:], AF.Exp, scale=-0.5), inc=rstd)
            p.wait("vector", rstd, rstd.n)
            p.wait("vector", sto[b], sto[b].n)
            p.op("vector", lambda En, b=b: En.tensor_scalar(
                xnt[b][:], xt[b][:], mcol[b][:, 0:1], rcol[b][:, 0:1],
                ALU.subtract, ALU.mult), inc=app)
            app_post[i] = app.n
            p.wait("scalar", app, app.n)
            p.dma("scalar", out_d[i * 128:(i + 1) * 128, :], xnt[b][:], sto[b])
        for b in range(NB):
            p.wait("scalar", sto[b], sto[b].n)
    p.barrier(tag, reclaim=True)


class AttnCfg:
    def __init__(self, **kw):
        self.tag = kw["tag"]
        self.src_d = kw["src_d"]
        self.out_d = kw["out_d"]
        self.out_f32 = kw.get("out_f32", False)
        self.resid_src = kw.get("resid_src")
        self.causal = kw["causal"]
        self.alibi = kw["alibi"]
        self.qw = kw["qw"]
        self.kw_ = kw["kw_"]
        self.vw = kw["vw"]
        self.latent = kw.get("latent")
        self.ow = kw["ow"]
        self.affine = kw.get("affine")
        self.v_d = kw.get("v_d")
        self.wait_cast = kw.get("wait_cast", [])


def _slope(h):
    return 2.0 ** (-0.5 * (h + 1))


def p_attn(p, cfg):
    nc = p.nc
    tag = cfg.tag
    KV_CH = 2 if cfg.latent else NCH
    NKT = NT
    NQC = T // 512
    with contextlib.ExitStack() as stk:
        Eo = stk.enter_context
        if not cfg.alibi:
            PT = [Eo(nc.sbuf_tensor(f"{tag}pt{i}", [128, 1024], BF16))
                  for i in range(2)]
            NPT = 2
        else:
            PT = [Eo(nc.sbuf_tensor(f"{tag}pt{i}", [128, 512], BF16))
                  for i in range(3)]
            NPT = 3
        wq = Eo(nc.sbuf_tensor(f"{tag}wq", [128, D], BF16))
        wk = Eo(nc.sbuf_tensor(f"{tag}wk", [128, KV_CH * 128], BF16))
        wv = Eo(nc.sbuf_tensor(f"{tag}wv", [128, KV_CH * 128], BF16))
        qsq = [Eo(nc.sbuf_tensor(f"{tag}sq0", [128, 512], BF16))]
        zzb = Eo(nc.sbuf_tensor(f"{tag}zzb", [65, 512], F32))
        zb_sb = zzb
        zr = zzb
        ones_f = Eo(nc.sbuf_tensor(f"{tag}onef", [128, 128], F32))
        ones_b = Eo(nc.sbuf_tensor(f"{tag}oneb", [128, 128], BF16))
        ident_b = Eo(nc.sbuf_tensor(f"{tag}idb", [128, 128], BF16))
        mask0 = Eo(nc.sbuf_tensor(f"{tag}mk", [128, 128], F32))
        nC = [Eo(nc.sbuf_tensor(f"{tag}nC{i}", [128, 1], F32)) for i in range(2)]
        mq2 = Eo(nc.sbuf_tensor(f"{tag}mq", [1, 2], F32))
        cch = Eo(nc.sbuf_tensor(f"{tag}cc", [1, 2], F32))
        if cfg.alibi:
            negB = [Eo(nc.sbuf_tensor(f"{tag}nb{i}", [65, 512], BF16))
                    for i in range(2)]
            negBi0 = Eo(nc.sbuf_tensor(f"{tag}nbi", [65, 512], mybir.dt.int16))
            negBi = [negBi0, negBi0]
            kgl_i = Eo(nc.sbuf_tensor(f"{tag}kgi", [128, NT], I32))
            kgl = Eo(nc.sbuf_tensor(f"{tag}kgf", [128, NT], F32))
            bias_h = [[Eo(nc.sbuf_tensor(f"{tag}bh{i}{j}", [128, NT], F32))
                       for j in range(2)] for i in range(2)]
        gcol = [Eo(nc.sbuf_tensor(f"{tag}gc{i}", [128, 1], F32)) for i in range(NCH)]
        bcol = [Eo(nc.sbuf_tensor(f"{tag}bc{i}", [128, 1], F32)) for i in range(NCH)]
        ps_p0 = Eo(nc.psum_tensor(f"{tag}pp", [128, 512], F32))
        ps_p = [ps_p0, ps_p0]
        WIDE = not cfg.alibi
        if WIDE:
            NS = 2
            ps_s = [Eo(nc.psum_tensor(f"{tag}ps{i}", [128, 1024], F32))
                    for i in range(NS)]
        else:
            NS = 4
            ps_s = [Eo(nc.psum_tensor(f"{tag}ps{i}", [128, 512], F32))
                    for i in range(NS)]
        ps_y = [Eo(nc.psum_tensor(f"{tag}py{i}", [65, 512], F32)) for i in range(2)]
        aux = Eo(nc.psum_tensor(f"{tag}aux", [128, 512], F32))
        vstg = [Eo(nc.sbuf_tensor(f"{tag}vg{i}", [128, 128], BF16))
                for i in range(2)]
        vstg_free = [0, 0]
        st_v = p.dsem("stv")
        ld_vt = [p.dsem("lvt0"), p.dsem("lvt1")]
        ld_pan = p.dsem("ldp"); ld_af = p.dsem("ldaf"); ld_l = p.dsem("ldl")
        ld_w = [p.dsem("ldw0"), p.dsem("ldw1")]; po = p.sem("po")
        pe_p = p.sem("pep"); evq = p.sem("evq"); aff = p.sem("aff")
        pe_n = p.sem("pen"); dve_n = p.sem("dvn")
        act_c = p.sem("acc"); dve_c = p.sem("dvc")
        pe_t = p.sem("pet"); dve_t = p.sem("dvt")
        pe_s = p.sem("pes"); msk = p.sem("msk"); ex = p.sem("ex")
        pe_av = p.sem("pav"); dve_z = p.sem("dvz")
        pe_zb = p.sem("pzb"); act_zb = p.sem("azb"); dve_y = p.sem("dvy")

        for cs, cn in cfg.wait_cast:
            p.wait("sync", cs, cn)

        pp_free = [0, 0]
        negb_free = [0, 0]
        yq_slot = [(0, 0), (0, 0)]
        pt_free = [0, 0, 0]
        exp_post = {}
        sc_cnt = [0]

        p.op("vector", lambda En: En.memset(ones_f[:], 1.0), inc=dve_c)
        p.op("vector", lambda En: En.memset(ones_b[:], 1.0), inc=dve_c)
        p.op("vector", lambda En: En.memset(mask0[:], 0.0), inc=dve_c)
        p.op("vector", lambda En: En.memset(ident_b[:], 1.0), inc=dve_c)
        cst0 = dve_c.n
        p.wait("gpsimd", dve_c, cst0)
        p.op("gpsimd", lambda En: En.affine_select(
            ident_b[:], ident_b[:], [[-1, 128]], ALU.is_equal, 0.0,
            base=0, channel_multiplier=1), inc=po)
        ident_post = po.n
        mask_post = 0
        if cfg.causal:
            p.op("gpsimd", lambda En: En.affine_select(
                mask0[:], mask0[:], [[1, 128]], ALU.is_ge, float(NEG),
                base=0, channel_multiplier=-1), inc=po)
            mask_post = po.n
        kgl_post = 0
        if cfg.alibi:
            p.op("gpsimd", lambda En: En.iota(
                kgl_i[:], [[128, NT]], base=0, channel_multiplier=1), inc=po)
            p.wait("vector", po, po.n)
            p.op("vector", lambda En: En.tensor_copy(kgl[:], kgl_i[:]), inc=dve_c)
            kgl_post = dve_c.n
        aff_cols_post = 0
        if cfg.affine is not None:
            with contextlib.ExitStack() as stka:
                wrow = stka.enter_context(nc.sbuf_tensor(f"{tag}wr", [1, D], F32))
                brow = stka.enter_context(nc.sbuf_tensor(f"{tag}br", [1, D], F32))
                p.dma("sync", wrow[:], cfg.affine[0].rearrange("(o d) -> o d", o=1), ld_af)
                p.dma("sync", brow[:], cfg.affine[1].rearrange("(o d) -> o d", o=1), ld_af)
                p.wait("tensor", ld_af, ld_af.n)
                p.wait("tensor", dve_c, cst0)
                for ic in range(NCH):
                    for row, dstc in ((wrow, gcol[ic]), (brow, bcol[ic])):
                        p.wait("tensor", act_c, act_c.n)
                        p.op("tensor", lambda En, ic=ic, row=row: En.transpose(
                            aux[:, 0:1], row[0:1, ic * 128:(ic + 1) * 128],
                            ones_f[0:1, 0:1]), inc=pe_t)
                        p.wait("scalar", pe_t, pe_t.n)
                        p.op("scalar", lambda En, dstc=dstc: En.activation(
                            dstc[:], aux[:, 0:1], AF.Copy), inc=act_c)
                aff_cols_post = act_c.n
                p.wait("sync", act_c, aff_cols_post)

        pan = [Eo(nc.sbuf_tensor(f"{tag}pan{i}", [128, T], BF16)) for i in range(NCH)]
        Y = [Eo(nc.sbuf_tensor(f"{tag}Y{i}", [128, T], BF16)) for i in range(NCH)]
        qsb = [Eo(nc.sbuf_tensor(f"{tag}q{i}", [128, T], BF16)) for i in range(2)]
        ksb = [Eo(nc.sbuf_tensor(f"{tag}k{i}", [128, T], BF16)) for i in range(2)]
        vsb = Eo(nc.sbuf_tensor(f"{tag}v", [128, T], BF16))
        v_tm = [Eo(nc.sbuf_tensor(f"{tag}vt{i}", [128, 2 * NT * 65], BF16))
                for i in range(2)]
        for ic in range(NCH):
            p.dma("sync", pan[ic][:], cfg.src_d[:, ic * 128:(ic + 1) * 128],
                  ld_pan, transpose=True)
        pan_total = ld_pan.n
        if cfg.affine is not None:
            for ic in range(NCH):
                p.wait("vector", ld_pan, pan_total)
                p.wait("vector", act_c, aff_cols_post)
                p.op("vector", lambda En, ic=ic: En.tensor_scalar(
                    pan[ic][:], pan[ic][:], gcol[ic][:, 0:1], bcol[ic][:, 0:1],
                    ALU.mult, ALU.add), inc=aff)
            pan_wait = (aff, aff.n)
        else:
            pan_wait = (ld_pan, pan_total)

        lat_tiles = None
        lat_ld = None
        lat_free = [0, 0]
        if cfg.latent:
            lat_d = cfg.latent["lat_d"]
            lat_tiles = [[Eo(nc.sbuf_tensor(f"{tag}lt{s}{i}", [128, 512], BF16))
                          for i in range(2)] for s in range(2)]
            lat_ld = [p.dsem("lls0"), p.dsem("lls1")]
            wl = wq
            lst = qsq[0]
            stl = p.dsem("stl")
            for ocl in range(2):
                if ocl > 0:
                    p.wait("sync", pe_p, pe_p.n)
                p.dma("sync", wl[:].rearrange("p (ic o) -> p ic o", o=128),
                      cfg.latent["dw"](ocl), ld_l)
                p.wait("tensor", pan_wait[0], pan_wait[1])
                p.wait("tensor", ld_l, ld_l.n)
                for tc in range(NQC):
                    pb = 0
                    p.wait("tensor", evq, pp_free[pb])
                    for ic in range(NCH):
                        p.op("tensor", lambda En, pb=pb, ic=ic, tc=tc: En.matmul(
                            ps_p[pb][:], wl[:, ic * 128:(ic + 1) * 128],
                            pan[ic][:, tc * 512:(tc + 1) * 512],
                            start=(ic == 0), stop=(ic == NCH - 1)), inc=pe_p)
                    p.wait("vector", pe_p, pe_p.n)
                    p.wait("vector", stl, stl.n)
                    p.op("vector", lambda En, pb=pb: En.tensor_copy(
                        lst[:], ps_p[pb][:]), inc=evq)
                    pp_free[pb] = evq.n
                    p.wait("scalar", evq, evq.n)
                    p.dma("scalar", lat_d[ocl * 128:(ocl + 1) * 128,
                                          tc * 512:(tc + 1) * 512], lst[:], stl)
            p.wait("sync", stl, stl.n)
            lat_store_post = stl.n
        kv_pan = pan

        state = {}

        def make_prep(oc, buf, prev_jobs_ex):
            fl = []
            p.wait("sync", pe_p, pe_p.n)
            p.wait("sync", ex, prev_jobs_ex)
            p.dma("sync", wq[:].rearrange("p (ic o) -> p ic o", o=128),
                  cfg.qw(oc), ld_w[0])
            if cfg.latent:
                p.dma("sync", wk[:].rearrange("p (ic o) -> p ic o", o=128),
                      cfg.latent["ku"](oc), ld_w[0])
                p.dma("sync", wv[:].rearrange("p (ic o) -> p ic o", o=128),
                      cfg.latent["vu"](oc), ld_w[0])
            else:
                p.dma("sync", wk[:].rearrange("p (ic o) -> p ic o", o=128),
                      cfg.kw_(oc), ld_w[0])
                p.dma("sync", wv[:].rearrange("p (ic o) -> p ic o", o=128),
                      cfg.vw(oc), ld_w[0])
            ldw_post = ld_w[0].n
            for which, dst, wt, src_pan, nkc, scale in (
                    ("q", qsb[buf], wq, pan, NCH, 0.125),
                    ("k", ksb[buf], wk, kv_pan, KV_CH, 1.0),
                    ("v", vsb, wv, kv_pan, KV_CH, 1.0)):
                for tc in range(NQC):
                    def proj(which=which, dst=dst, wt=wt, src_pan=src_pan,
                             nkc=nkc, scale=scale, tc=tc, ldw_post=ldw_post,
                             prev_jobs_ex=prev_jobs_ex):
                        pb = 0
                        lat_mode = cfg.latent and which != "q"
                        if lat_mode:
                            ls = tc % 2
                            p.wait("sync", pe_p, lat_free[ls])
                            for icl in range(2):
                                p.dma("sync", lat_tiles[ls][icl][:],
                                      cfg.latent["lat_d"][
                                          icl * 128:(icl + 1) * 128,
                                          tc * 512:(tc + 1) * 512], lat_ld[ls])
                            llpost = lat_ld[ls].n
                        p.wait("tensor", pan_wait[0], pan_wait[1])
                        p.wait("tensor", ld_w[0], ldw_post)
                        p.wait("tensor", ex, prev_jobs_ex)
                        if which == "v":
                            p.wait("tensor", st_v, state.get("v_st", 0))
                        p.wait("tensor", evq, pp_free[pb])
                        if lat_mode:
                            p.wait("tensor", lat_ld[ls], llpost)
                        for ic in range(nkc):
                            srcap = (lat_tiles[ls][ic][:, 0:512] if lat_mode
                                     else src_pan[ic][:, tc * 512:(tc + 1) * 512])
                            p.op("tensor", lambda En, pb=pb, ic=ic, wt=wt,
                                 nkc=nkc, srcap=srcap: En.matmul(
                                     ps_p[pb][:], wt[:, ic * 128:(ic + 1) * 128],
                                     srcap,
                                     start=(ic == 0), stop=(ic == nkc - 1)),
                                 inc=pe_p)
                        if lat_mode:
                            lat_free[ls] = pe_p.n
                        p.wait("vector", pe_p, pe_p.n)
                        if scale == 1.0:
                            p.op("vector", lambda En, pb=pb, dst=dst, tc=tc:
                                 En.tensor_copy(dst[:, tc * 512:(tc + 1) * 512],
                                                ps_p[pb][:]), inc=evq)
                        else:
                            p.op("vector", lambda En, pb=pb, dst=dst, tc=tc,
                                 scale=scale: En.tensor_scalar(
                                     dst[:, tc * 512:(tc + 1) * 512],
                                     ps_p[pb][:], scale, None, ALU.mult),
                                 inc=evq)
                        pp_free[pb] = evq.n
                    fl.append(proj)

            def wfree_mark():
                state[("evq_all", buf)] = evq.n
            fl.append(wfree_mark)

            nl = []

            def norm_init():
                p.wait("vector", evq, state[("evq_all", buf)])
                p.wait("vector", dve_n, dve_n.n)
                p.op("vector", lambda En: En.memset(mq2[:], 1e-20), inc=dve_n)
            nl.append(norm_init)
            ns_free = {}
            nslots = NS

            def nslot(i):
                return ps_s[i % nslots][0:1, 0:512]

            for wi, (which, srcf, col) in enumerate(
                    (("q", qsb, 0), ("k", ksb, 1))):
                for tc in range(NQC):
                    def norm(which=which, srcf=srcf, col=col, tc=tc, buf=buf,
                             idx2=None, wi=wi):
                        src = srcf[buf]
                        i2 = wi * NQC + tc
                        sb2 = 0
                        p.wait("scalar", evq, state[("evq_all", buf)])
                        p.wait("scalar", pe_n, ns_free.get(("sq", sb2), 0))
                        p.op("scalar", lambda En, sb2=sb2, src=src, tc=tc:
                             En.activation(qsq[sb2][:],
                                           src[:, tc * 512:(tc + 1) * 512],
                                           AF.Square), inc=act_c)
                        p.wait("tensor", act_c, act_c.n)
                        p.wait("tensor", dve_c, cst0)
                        p.wait("tensor", ex, ex.n)
                        p.wait("tensor", dve_n, ns_free.get(("ps", i2 % nslots), 0))
                        p.op("tensor", lambda En, i2=i2, sb2=sb2: En.matmul(
                            nslot(i2), ones_b[:, 0:1], qsq[sb2][:],
                            start=True, stop=True), inc=pe_n)
                        ns_free[("sq", sb2)] = pe_n.n
                        p.wait("vector", pe_n, pe_n.n)
                        p.op("vector", lambda En, i2=i2: En.tensor_reduce(
                            cch[0:1, 0:1], nslot(i2), AX.X, ALU.max),
                            inc=dve_n)
                        ns_free[("ps", i2 % nslots)] = dve_n.n
                        p.wait("vector", dve_n, dve_n.n)
                        p.op("vector", lambda En, col=col: En.tensor_max(
                            mq2[0:1, col:col + 1], mq2[0:1, col:col + 1],
                            cch[0:1, 0:1]), inc=dve_n)
                        p.wait("vector", dve_n, dve_n.n)
                    nl.append(norm)

            def cchain(buf=buf, oc=oc):
                p.op("vector", lambda En: En.tensor_tensor(
                    cch[0:1, 0:1], mq2[0:1, 0:1], mq2[0:1, 1:2], ALU.mult),
                    inc=dve_n)
                p.wait("scalar", dve_n, dve_n.n)
                p.op("scalar", lambda En: En.activation(
                    cch[0:1, 1:2], cch[0:1, 0:1], AF.Sqrt,
                    scale=(1.01 ** 2) / 64.0), inc=act_c)
                p.wait("vector", act_c, act_c.n)
                p.op("vector", lambda En: En.tensor_scalar(
                    cch[0:1, 0:1], cch[0:1, 1:2], -1.0, -2.0, ALU.mult,
                    ALU.add), inc=dve_n)
                p.wait("tensor", dve_n, dve_n.n)
                p.wait("tensor", act_zb, act_zb.n)
                p.wait("tensor", ex, state.get(("nC_used", buf), 0))
                p.op("tensor", lambda En: En.matmul(
                    aux[:, 0:1], ones_f[0:1, :], cch[0:1, 0:1],
                    start=True, stop=True), inc=pe_zb)
                p.wait("scalar", pe_zb, pe_zb.n)
                p.op("scalar", lambda En, buf=buf: En.activation(
                    nC[buf][:], aux[:, 0:1], AF.Copy), inc=act_zb)
                state[("nC", buf)] = act_zb.n
                if cfg.alibi:
                    for h2 in range(2):
                        h = oc * 2 + h2
                        p.wait("vector", act_zb, state[("nC", buf)])
                        p.wait("vector", dve_c, kgl_post)
                        p.wait("vector", ex, state.get(("nC_used", buf), 0))
                        p.op("vector", lambda En, h2=h2, h=h, buf=buf:
                             En.tensor_scalar(
                                 bias_h[h2][buf][:], kgl[:], _slope(h),
                                 nC[buf][:, 0:1], ALU.mult, ALU.add), inc=dve_c)
                    state[("bias", buf)] = dve_c.n
            nl.append(cchain)

            def vt_init(buf=buf):
                p.wait("vector", evq, state[("evq_all", buf)])
                p.wait("vector", pe_av, state.get(("vtm_used", buf), 0))
                p.op("vector", lambda En, buf=buf: En.memset(v_tm[buf][:], 1.0),
                     inc=dve_t)
                state[("vtm_ms", buf)] = dve_t.n
                p.wait("scalar", evq, state[("evq_all", buf)])
                p.dma("scalar", cfg.v_d, vsb[:], st_v)
                state["v_st"] = st_v.n
            fl.append(vt_init)
            for kt in range(NKT):
                def vt(kt=kt, buf=buf):
                    vs = kt % 2
                    p.wait("sync", st_v, state["v_st"])
                    p.wait("sync", dve_t, vstg_free[vs])
                    p.dma("sync", vstg[vs][:],
                          cfg.v_d[:, kt * 128:(kt + 1) * 128], ld_vt[vs],
                          transpose=True)
                    p.wait("vector", ld_vt[vs], ld_vt[vs].n)
                    p.wait("vector", dve_t, state[("vtm_ms", buf)])
                    for h2 in range(2):
                        p.op("vector", lambda En, h2=h2, kt=kt, buf=buf, vs=vs:
                             En.tensor_copy(
                                 v_tm[buf][:, (h2 * NKT + kt) * 65:
                                           (h2 * NKT + kt) * 65 + 64],
                                 vstg[vs][:, h2 * 64:(h2 + 1) * 64]),
                             inc=dve_t)
                    vstg_free[vs] = dve_t.n
                fl.append(vt)

            def vt_done(buf=buf):
                state[("vtm", buf)] = dve_t.n
            fl.append(vt_done)
            return fl, nl

        def run_jobs(oc, buf, fillers):
            units = []
            for h2 in range(2):
                for qc in range(NQC):
                    qb = qc * 512
                    if cfg.causal:
                        kts = []
                        for kt in range(NKT):
                            if kt * 128 > qb + 384:
                                break
                            kts.append((kt, max(0, (kt * 128 - qb) // 128)))
                    else:
                        kts = [(kt, 0) for kt in range(NKT)]
                    i0 = 0
                    while i0 < len(kts):
                        if (WIDE and i0 + 1 < len(kts)
                                and kts[i0][1] == 0 and kts[i0 + 1][1] == 0):
                            units.append((h2, qc, kts[i0:i0 + 2], i0, len(kts)))
                            i0 += 2
                        else:
                            units.append((h2, qc, kts[i0:i0 + 1], i0, len(kts)))
                            i0 += 1
            LAG = 2
            deferred = []
            negb_built = {}
            nu = len(units)
            nf = len(fillers)
            fill_at = {}
            if nf:
                stepf = max(1, nu // nf)
                for i in range(nf):
                    fill_at.setdefault(min(4 + i * stepf, nu - 1), []).append(
                        fillers[i])

            def emit_S(u):
                h2, qc, kts, i0, nkts = units[u]
                h = oc * 2 + h2
                qb = qc * 512
                prow = h2 * 64
                hs = slice(h2 * 64, (h2 + 1) * 64)
                c = sc_cnt[0]
                sc_cnt[0] += 1
                sb = c % NS
                nbb = None
                if cfg.alibi:
                    nbb = (h2 * NQC + qc) % 2
                    if i0 == 0:
                        p.wait("gpsimd", pe_s, negb_free[nbb])
                        p.wait("gpsimd", act_c, act_c.n)
                        p.op("gpsimd", lambda En, nbb=nbb, qb=qb, prow=prow:
                             En.iota(negBi[nbb][prow:prow + 1, :], [[1, 512]],
                                     base=qb, channel_multiplier=0), inc=po)
                        p.wait("scalar", po, po.n)
                        p.op("scalar", lambda En, nbb=nbb, h=h, prow=prow:
                             En.activation(negB[nbb][prow:prow + 1, :],
                                           negBi[nbb][prow:prow + 1, :],
                                           AF.Copy, scale=-_slope(h)), inc=act_c)
                        negb_built[(h2, qc)] = act_c.n
                if c >= NS:
                    p.wait("tensor", ex, exp_post[c - NS])
                if cfg.alibi:
                    p.wait("tensor", act_c, negb_built[(h2, qc)])
                msk_post = None
                for i, (kt, jj) in enumerate(kts):
                    qsl = slice(qb + jj * 128, qb + 512)
                    psl = slice(512 * i + jj * 128, 512 * (i + 1))
                    p.op("tensor", lambda En, sb=sb, hs=hs, kt=kt, qsl=qsl,
                         psl=psl, buf=buf: En.matmul(
                             ps_s[sb][:, psl],
                             ksb[buf][hs, kt * 128:(kt + 1) * 128],
                             qsb[buf][hs, qsl], start=True,
                             stop=not cfg.alibi), inc=pe_s)
                    if cfg.alibi:
                        p.op("tensor", lambda En, sb=sb, psl=psl, prow=prow,
                             nbb=nbb: En.matmul(
                                 ps_s[sb][:, psl], ones_b[prow:prow + 1, :],
                                 negB[nbb][prow:prow + 1,
                                           slice(jj * 128, 512)],
                                 start=False, stop=True), inc=pe_s)
                        negb_free[nbb] = pe_s.n
                mm_post = pe_s.n
                partial_any = False
                for i, (kt, jj) in enumerate(kts):
                    if cfg.causal and kt * 128 >= qb:
                        if not partial_any:
                            p.wait("vector", pe_s, mm_post)
                            p.wait("vector", po, mask_post)
                            partial_any = True
                        p.op("vector", lambda En, sb=sb, i=i, jj=jj:
                             En.tensor_tensor(
                                 ps_s[sb][:, 512 * i + jj * 128:
                                          512 * i + jj * 128 + 128],
                                 ps_s[sb][:, 512 * i + jj * 128:
                                          512 * i + jj * 128 + 128],
                                 mask0[:], ALU.add), inc=msk)
                if partial_any:
                    msk_post = msk.n
                ptb = c % NPT
                p.wait("scalar", pe_av, pt_free[ptb])
                if partial_any:
                    p.wait("scalar", msk, msk_post)
                else:
                    p.wait("scalar", pe_s, mm_post)
                jj0 = kts[0][1]
                espan = slice(jj0 * 128, 512 * len(kts))
                if cfg.alibi:
                    ba = bias_h[h2][buf][:, kts[0][0]:kts[0][0] + 1]
                else:
                    ba = nC[buf][:, 0:1]
                p.op("scalar", lambda En, ptb=ptb, sb=sb, espan=espan, ba=ba:
                     En.activation(PT[ptb][:, espan], ps_s[sb][:, espan],
                                   AF.Exp, bias=ba), inc=ex)
                exp_post[c] = ex.n
                return c

            def emit_AV(u, c):
                h2, qc, kts, i0, nkts = units[u]
                yb = (h2 * NQC + qc) % 2
                qb = qc * 512
                ptb = c % NPT
                p.wait("tensor", ex, exp_post[c])
                if i0 == 0:
                    p.wait("tensor", dve_t, state[("vtm", buf)])
                    p.wait("tensor", dve_z, yq_slot[yb][0])
                    p.wait("tensor", dve_y, yq_slot[yb][1])
                for i, (kt, jj) in enumerate(kts):
                    psl = slice(512 * i + jj * 128, 512 * (i + 1))
                    ysl = slice(jj * 128, 512)
                    p.op("tensor", lambda En, yb=yb, ptb=ptb, psl=psl, ysl=ysl,
                         h2=h2, kt=kt, first=(i0 == 0 and i == 0),
                         last=(i0 + i == nkts - 1), buf=buf: En.matmul(
                             ps_y[yb][:, ysl],
                             v_tm[buf][:, (h2 * NKT + kt) * 65:
                                       (h2 * NKT + kt) * 65 + 65],
                             PT[ptb][:, psl], start=first,
                             stop=last), inc=pe_av)
                pt_free[ptb] = pe_av.n
                if i0 + len(kts) == nkts:
                    for d in list(deferred):
                        d[1]()
                        deferred.remove(d)
                    av_post = pe_av.n
                    p.wait("vector", pe_av, av_post)
                    p.wait("vector", pe_zb, pe_zb.n)
                    p.op("vector", lambda En, yb=yb: En.tensor_scalar_max(
                        zr[64:65, :], ps_y[yb][64:65, :], 1e-35), inc=dve_z)
                    p.wait("scalar", dve_z, dve_z.n)
                    p.op("scalar", lambda En: En.activation(
                        zr[64:65, :], zr[64:65, :], AF.Ln), inc=act_zb)
                    p.wait("scalar", act_zb, act_zb.n)
                    p.op("scalar", lambda En: En.activation(
                        zr[64:65, :], zr[64:65, :], AF.Exp, scale=-1.0),
                        inc=dve_z)
                    z_post = dve_z.n

                    def stage2(z_post=z_post, yb=yb, oc=oc, h2=h2, qb=qb):
                        hs = slice(h2 * 64, (h2 + 1) * 64)
                        p.wait("tensor", dve_z, z_post)
                        p.wait("tensor", act_zb, act_zb.n)
                        p.op("tensor", lambda En: En.matmul(
                            aux[0:64, :], ones_f[64:65, 0:64], zr[64:65, :],
                            start=True, stop=True), inc=pe_zb)
                        p.wait("vector", pe_zb, pe_zb.n)
                        p.wait("vector", dve_y, dve_y.n)
                        p.op("vector", lambda En: En.tensor_copy(
                            zb_sb[0:64, :], aux[0:64, :]), inc=act_zb)
                        p.wait("vector", act_zb, act_zb.n)
                        p.op("vector", lambda En, yb=yb, oc=oc, hs=hs, qb=qb:
                             En.tensor_tensor(
                                 Y[oc][hs, qb:qb + 512], ps_y[yb][0:64, :],
                                 zb_sb[0:64, :], ALU.mult), inc=dve_y)
                        yq_slot[yb] = (dve_z.n, dve_y.n)
                    deferred.append([u + 6, stage2])

            p.wait("scalar", act_zb, state[("nC", buf)])
            if cfg.alibi:
                p.wait("scalar", dve_c, state[("bias", buf)])
            cq = []
            for u in range(nu):
                if u >= LAG:
                    emit_AV(u - LAG, cq[u - LAG])
                cq.append(emit_S(u))
                for f in fill_at.get(u, ()):
                    f()
                for d in list(deferred):
                    if d[0] <= u:
                        d[1]()
                        deferred.remove(d)
            for u in range(nu - LAG, nu):
                if u >= 0:
                    emit_AV(u, cq[u])
            for d in deferred:
                d[1]()
            state[("nC_used", buf)] = ex.n
            state[("vtm_used", buf)] = pe_av.n
            return ex.n

        prev_ex = {0: 0, 1: 0}
        fillers, norms0 = make_prep(0, 0, 0)
        for f in fillers:
            f()
        for f in norms0:
            f()
        p.wait("tensor", dve_n, dve_n.n)
        for oc in range(NCH):
            buf = oc % 2
            if oc + 1 < NCH:
                nxt, nxt_norms = make_prep(oc + 1, 1 - buf, prev_ex[1 - buf])
            else:
                nxt, nxt_norms = [], []
            prev_ex[buf] = run_jobs(oc, buf, nxt)
            for f in nxt_norms:
                f()
            if nxt_norms:
                p.wait("tensor", dve_n, dve_n.n)

        y_done = dve_y.n
        with contextlib.ExitStack() as stk2:
            Eo2 = stk2.enter_context
            res_pan = None
            if cfg.resid_src is not None:
                rp0 = Eo2(nc.sbuf_tensor(f"{tag}rp", [128, 512], BF16))
                res_pan = [rp0, rp0]
            wo = Eo2(nc.sbuf_tensor(f"{tag}wo", [128, D], BF16))
            o_sb0 = Eo2(nc.sbuf_tensor(f"{tag}os", [128, 512],
                                       F32 if cfg.out_f32 else BF16))
            o_sb = [o_sb0, o_sb0]
            ld_r = [p.dsem("ldr0"), p.dsem("ldr1")]
            sto_s = [p.dsem("sts0"), p.dsem("sts1")]
            ld_wo = [p.dsem("lwo0"), p.dsem("lwo1")]
            evo = p.sem("evo")
            wo_free = [0, 0]
            rp_free = [0, 0]
            op_free = [0, 0]
            for oc in range(NCH):
                wb = 0
                p.wait("sync", pe_p, wo_free[0])
                p.dma("sync", wo[:].rearrange("p (ic o) -> p ic o", o=128),
                      cfg.ow(oc), ld_wo[0])
                ldo_post = ld_wo[0].n
                p.wait("tensor", ld_wo[0], ldo_post)
                p.wait("tensor", dve_y, y_done)
                for tc in range(NQC):
                    pb = 0
                    rb = tc % 2
                    if cfg.resid_src is not None:
                        p.wait("sync", evo, rp_free[0])
                        p.wait("sync", evo, rp_free[1])
                        p.dma("sync", res_pan[rb][:],
                              cfg.resid_src[tc * 512:(tc + 1) * 512,
                                            oc * 128:(oc + 1) * 128],
                              ld_r[rb], transpose=True)
                    p.wait("tensor", evq, pp_free[pb])
                    p.wait("tensor", evo, op_free[pb])
                    for ic in range(NCH):
                        p.op("tensor", lambda En, pb=pb, ic=ic, tc=tc, wb=wb:
                             En.matmul(
                                 ps_p[pb][:], wo[:, ic * 128:(ic + 1) * 128],
                                 Y[ic][:, tc * 512:(tc + 1) * 512],
                                 start=(ic == 0), stop=(ic == NCH - 1)), inc=pe_p)
                    if cfg.resid_src is not None:
                        p.wait("vector", pe_p, pe_p.n)
                        p.wait("vector", ld_r[rb], ld_r[rb].n)
                        p.wait("vector", sto_s[0], sto_s[0].n)
                        p.wait("vector", sto_s[1], sto_s[1].n)
                        p.op("vector", lambda En, pb=pb, rb=rb: En.tensor_tensor(
                            o_sb[pb][:], ps_p[pb][:], res_pan[rb][:], ALU.add),
                            inc=evo)
                        rp_free[rb] = evo.n
                        op_free[pb] = evo.n
                        p.wait("scalar", evo, evo.n)
                    else:
                        p.wait("scalar", pe_p, pe_p.n)
                        p.wait("scalar", sto_s[0], sto_s[0].n)
                        p.wait("scalar", sto_s[1], sto_s[1].n)
                        p.op("scalar", lambda En, pb=pb: En.activation(
                            o_sb[pb][:], ps_p[pb][:], AF.Copy), inc=evo)
                        op_free[pb] = evo.n
                        p.wait("scalar", evo, evo.n)
                    p.dma("scalar", cfg.out_d[oc * 128:(oc + 1) * 128,
                                              tc * 512:(tc + 1) * 512],
                          o_sb[pb][:], sto_s[pb])
                wo_free[0] = pe_p.n
            p.wait("scalar", sto_s[0], sto_s[0].n)
            p.wait("scalar", sto_s[1], sto_s[1].n)
    p.barrier(tag, reclaim=True)


def p_ln2_gate(p, y1_d, xn3_d, r_d, lnw_ap, lnb_ap, gw_ap, gb_ap,
               cast_gw, tag="p5"):
    nc = p.nc
    with contextlib.ExitStack() as stk:
        Eo = stk.enter_context
        yt = [[Eo(nc.sbuf_tensor(f"{tag}y{r}_{i}", [128, 512], F32))
               for i in range(NCH)] for r in range(2)]
        ysq = [Eo(nc.sbuf_tensor(f"{tag}sq{i}", [128, 512], F32)) for i in range(2)]
        t1 = Eo(nc.sbuf_tensor(f"{tag}t1", [128, 512], F32))
        rstd_b = Eo(nc.sbuf_tensor(f"{tag}rb", [128, 512], F32))
        x3 = [Eo(nc.sbuf_tensor(f"{tag}x3{i}", [128, 512], BF16)) for i in range(2)]
        rows = Eo(nc.sbuf_tensor(f"{tag}rw", [1, 512 * 4], F32))
        epsc = Eo(nc.sbuf_tensor(f"{tag}eps", [1, 1], F32))
        onef = Eo(nc.sbuf_tensor(f"{tag}onef", [128, 128], F32))
        oneb = Eo(nc.sbuf_tensor(f"{tag}oneb", [128, 128], BF16))
        identb = Eo(nc.sbuf_tensor(f"{tag}idb", [128, 128], BF16))
        gcol = [Eo(nc.sbuf_tensor(f"{tag}gc{i}", [128, 1], F32)) for i in range(NCH)]
        bcol = [Eo(nc.sbuf_tensor(f"{tag}bc{i}", [128, 1], F32)) for i in range(NCH)]
        wrow = Eo(nc.sbuf_tensor(f"{tag}wr", [1, D], F32))
        brow = Eo(nc.sbuf_tensor(f"{tag}br", [1, D], F32))
        ps_a = Eo(nc.psum_tensor(f"{tag}pa", [1, 512], F32))
        ps_b = Eo(nc.psum_tensor(f"{tag}pb", [1, 512], F32))
        ps_c = Eo(nc.psum_tensor(f"{tag}pc", [128, 512], F32))
        ps_d = Eo(nc.psum_tensor(f"{tag}pd", [128, 512], F32))
        ld_af = p.dsem("ldaf"); ld_y = [p.dsem("ldy0"), p.dsem("ldy1")]
        gld = p.dsem("gld")
        act = p.sem("act"); dve = p.sem("dve")
        pe = p.sem("pe"); po = p.sem("po"); sto = p.dsem("sto")
        sto_x = [p.dsem("stx0"), p.dsem("stx1")]
        evr = p.sem("evr")
        p.op("vector", lambda En: En.memset(epsc[:], float(EPS)), inc=dve)
        p.op("vector", lambda En: En.memset(onef[:], 1.0), inc=dve)
        p.op("vector", lambda En: En.memset(oneb[:], 1.0), inc=dve)
        p.op("vector", lambda En: En.memset(identb[:], 1.0), inc=dve)
        cst0 = dve.n
        p.wait("gpsimd", dve, cst0)
        p.op("gpsimd", lambda En: En.affine_select(
            identb[:], identb[:], [[-1, 128]], ALU.is_equal, 0.0,
            base=0, channel_multiplier=1), inc=po)
        ident_post = po.n
        p.dma("sync", wrow[:], lnw_ap.rearrange("(o d) -> o d", o=1), ld_af)
        p.dma("sync", brow[:], lnb_ap.rearrange("(o d) -> o d", o=1), ld_af)
        p.wait("tensor", ld_af, ld_af.n)
        p.wait("tensor", dve, cst0)
        for ic in range(NCH):
            for row, dstc in ((wrow, gcol[ic]), (brow, bcol[ic])):
                p.wait("tensor", act, act.n)
                p.op("tensor", lambda En, ic=ic, row=row: En.transpose(
                    ps_c[:, 0:1], row[0:1, ic * 128:(ic + 1) * 128],
                    onef[0:1, 0:1]), inc=pe)
                p.wait("scalar", pe, pe.n)
                p.op("scalar", lambda En, dstc=dstc: En.activation(
                    dstc[:], ps_c[:, 0:1], AF.Copy), inc=act)
        affc_post = act.n
        yt_free = [(0, 0, 0), (0, 0, 0)]
        for tc in range(T // 512):
            r = tc % 2
            p.wait("sync", pe, yt_free[r][0])
            p.wait("sync", act, yt_free[r][1])
            p.wait("sync", dve, yt_free[r][2])
            for ic in range(NCH):
                p.dma("sync", yt[r][ic][:],
                      y1_d[ic * 128:(ic + 1) * 128, tc * 512:(tc + 1) * 512],
                      ld_y[r])
            ld_post = ld_y[r].n
            for ic in range(NCH):
                p.wait("tensor", ld_y[r], ld_post)
                if ic == 0:
                    p.wait("tensor", dve, dve.n)
                p.op("tensor", lambda En, r=r, ic=ic: En.matmul(
                    ps_a[:], onef[:, 0:1], yt[r][ic][:],
                    start=(ic == 0), stop=(ic == NCH - 1)), inc=pe)
            for ic in range(NCH):
                sb2 = ic % 2
                p.wait("scalar", ld_y[r], ld_post)
                p.wait("scalar", pe, pe.n)
                p.op("scalar", lambda En, r=r, ic=ic, sb2=sb2: En.activation(
                    ysq[sb2][:], yt[r][ic][:], AF.Square), inc=act)
                p.wait("tensor", act, act.n)
                p.op("tensor", lambda En, sb2=sb2, ic=ic: En.matmul(
                    ps_b[:], onef[:, 0:1], ysq[sb2][:],
                    start=(ic == 0), stop=(ic == NCH - 1)), inc=pe)
            p.wait("vector", pe, pe.n)
            p.op("vector", lambda En: En.tensor_scalar(
                rows[0:1, 0:512], ps_a[:], 1.0 / D, None, ALU.mult), inc=dve)
            p.wait("vector", dve, dve.n)
            p.op("vector", lambda En: En.tensor_tensor(
                rows[0:1, 1536:2048], rows[0:1, 0:512], rows[0:1, 0:512],
                ALU.mult), inc=dve)
            p.wait("vector", dve, dve.n)
            p.op("vector", lambda En: En.scalar_tensor_tensor(
                rows[0:1, 512:1024], ps_b[:], 1.0 / D, rows[0:1, 1536:2048],
                ALU.mult, ALU.subtract), inc=dve)
            p.wait("scalar", dve, dve.n)
            p.op("scalar", lambda En: En.activation(
                rows[0:1, 1024:1536], rows[0:1, 512:1024], AF.Ln,
                bias=epsc[0:1, 0:1]), inc=act)
            p.wait("scalar", act, act.n)
            p.op("scalar", lambda En: En.activation(
                rows[0:1, 1024:1536], rows[0:1, 1024:1536], AF.Exp,
                scale=-0.5), inc=act)
            rstd_post = act.n
            p.wait("tensor", act, rstd_post)
            p.wait("tensor", dve, dve.n)
            p.op("tensor", lambda En: En.matmul(
                ps_c[:], onef[0:1, :], rows[0:1, 0:512], start=True, stop=True),
                inc=pe)
            p.op("tensor", lambda En: En.matmul(
                ps_d[:], onef[0:1, :], rows[0:1, 1024:1536], start=True,
                stop=True), inc=pe)
            bc_post = pe.n
            p.wait("scalar", bc_post if False else pe, bc_post)
            p.op("scalar", lambda En: En.activation(
                rstd_b[:], ps_d[:], AF.Copy), inc=act)
            rb_post = act.n
            for ic in range(NCH):
                xb = ic % 2
                p.wait("vector", dve, dve.n)
                p.wait("vector", act, rb_post)
                p.wait("vector", pe, bc_post)
                p.wait("vector", act, affc_post)
                p.wait("vector", sto_x[xb], sto_x[xb].n)
                p.op("vector", lambda En, r=r, ic=ic: En.tensor_tensor(
                    t1[:], yt[r][ic][:], ps_c[:], ALU.subtract), inc=dve)
                p.wait("vector", dve, dve.n)
                p.op("vector", lambda En: En.tensor_tensor(
                    t1[:], t1[:], rstd_b[:], ALU.mult), inc=dve)
                p.wait("vector", dve, dve.n)
                p.op("vector", lambda En, ic=ic, xb=xb: En.tensor_scalar(
                    x3[xb][:], t1[:], gcol[ic][:, 0:1], bcol[ic][:, 0:1],
                    ALU.mult, ALU.add), inc=dve)
                p.wait("scalar", dve, dve.n)
                p.dma("scalar", xn3_d[ic * 128:(ic + 1) * 128,
                                      tc * 512:(tc + 1) * 512], x3[xb][:],
                      sto_x[xb])
            yt_free[r] = (pe.n, act.n, dve.n)
        p.wait("scalar", sto_x[0], sto_x[0].n)
        p.wait("scalar", sto_x[1], sto_x[1].n)
        with contextlib.ExitStack() as stk2:
            Eo2 = stk2.enter_context
            x3p = [Eo2(nc.sbuf_tensor(f"{tag}xp{i}", [128, T], BF16))
                   for i in range(NCH)]
            xtf = Eo2(nc.sbuf_tensor(f"{tag}xtf", [128, 128], F32))
            gwt = Eo2(nc.sbuf_tensor(f"{tag}gw", [128, NCH * E], F32))
            gbr = Eo2(nc.sbuf_tensor(f"{tag}gb", [1, E], F32))
            io8i = Eo2(nc.sbuf_tensor(f"{tag}i8i", [128, E], I32))
            io8 = Eo2(nc.sbuf_tensor(f"{tag}i8", [128, E], F32))
            lg = Eo2(nc.sbuf_tensor(f"{tag}lg", [128, E], F32))
            mx = Eo2(nc.sbuf_tensor(f"{tag}mx", [128, 1], F32))
            eq = Eo2(nc.sbuf_tensor(f"{tag}eq", [128, E], F32))
            sel = Eo2(nc.sbuf_tensor(f"{tag}sel", [128, E], F32))
            rcl = Eo2(nc.sbuf_tensor(f"{tag}rcl", [128, 1], F32))
            rclb = Eo2(nc.sbuf_tensor(f"{tag}rcb", [128, 1], BF16))
            rrow = Eo2(nc.sbuf_tensor(f"{tag}rr", [1, T], BF16))
            ps_l = [Eo2(nc.psum_tensor(f"{tag}pl{i}", [128, 512], F32))
                    for i in range(2)]
            ps_r = Eo2(nc.psum_tensor(f"{tag}pr", [1, 128], BF16))
            for cs, cn in cast_gw:
                p.wait("sync", cs, cn)
            p.wait("sync", sto_x[0], sto_x[0].n)
            p.wait("sync", sto_x[1], sto_x[1].n)
            p.dma("sync", gwt[:].rearrange("p (ic e) -> p ic e", e=E),
                  gw_ap.rearrange("(ic p) e -> p ic e", p=128), gld)
            p.dma("sync", gbr[:], gb_ap.rearrange("(o e) -> o e", o=1), gld)
            for ic in range(NCH):
                p.dma("sync", x3p[ic][:], xn3_d[ic * 128:(ic + 1) * 128, :], gld)
            gld_post = gld.n
            p.op("gpsimd", lambda En: En.iota(
                io8i[:], [[1, E]], base=0, channel_multiplier=0), inc=po)
            p.wait("vector", po, po.n)
            p.op("vector", lambda En: En.tensor_copy(io8[:], io8i[:]), inc=dve)
            io_post = dve.n
            lg_free = [0, 0]
            xtf_post = 0
            for tt in range(NT):
                lb = tt % 2
                p.wait("tensor", gld, gld_post)
                p.wait("tensor", dve, lg_free[lb])
                for ic in range(NCH):
                    p.wait("scalar", gld, gld_post)
                    p.wait("scalar", pe, xtf_post)
                    p.op("scalar", lambda En, ic=ic, tt=tt: En.activation(
                        xtf[:], x3p[ic][:, tt * 128:(tt + 1) * 128], AF.Copy),
                        inc=act)
                    p.wait("tensor", act, act.n)
                    ent = p.op("tensor", lambda En, ic=ic, tt=tt, lb=lb: En.matmul(
                        ps_l[lb][:, 0:E], xtf[:],
                        gwt[:, ic * E:(ic + 1) * E],
                        start=(ic == 0), stop=False), inc=pe)
                    xtf_post = ent["post"]
                p.op("tensor", lambda En, lb=lb: En.matmul(
                    ps_l[lb][:, 0:E], onef[0:1, :], gbr[0:1, :],
                    start=False, stop=True), inc=pe)
                p.wait("vector", pe, pe.n)
                p.wait("vector", dve, io_post)
                p.op("vector", lambda En, lb=lb: En.tensor_copy(
                    lg[:], ps_l[lb][:, 0:E]), inc=dve)
                lg_free[lb] = dve.n
                p.wait("vector", dve, dve.n)
                p.op("vector", lambda En: En.reduce_max(mx[:], lg[:], AX.X),
                     inc=dve)
                p.wait("vector", dve, dve.n)
                p.op("vector", lambda En: En.tensor_scalar(
                    eq[:], lg[:], mx[:, 0:1], None, ALU.is_ge), inc=dve)
                p.wait("vector", dve, dve.n)
                p.op("vector", lambda En: En.tensor_scalar(
                    sel[:], eq[:], -999.0, 999.0, ALU.mult, ALU.add), inc=dve)
                p.wait("vector", dve, dve.n)
                p.op("vector", lambda En: En.tensor_add(sel[:], sel[:], io8[:]),
                     inc=dve)
                p.wait("vector", dve, dve.n)
                p.op("vector", lambda En: En.tensor_reduce(
                    rcl[:], sel[:], AX.X, ALU.min), inc=dve)
                p.wait("vector", dve, dve.n)
                p.wait("vector", evr, evr.n)
                p.op("vector", lambda En: En.tensor_copy(rclb[:], rcl[:]),
                     inc=dve)
                p.wait("tensor", dve, dve.n)
                p.wait("tensor", po, ident_post)
                p.wait("tensor", evr, evr.n)
                p.op("tensor", lambda En: En.transpose(
                    ps_r[:], rclb[:, 0:1], identb[:]), inc=pe)
                p.wait("vector", pe, pe.n)
                p.op("vector", lambda En, tt=tt: En.tensor_copy(
                    rrow[0:1, tt * 128:(tt + 1) * 128], ps_r[:]), inc=evr)
            p.wait("scalar", evr, evr.n)
            p.dma("scalar", r_d.rearrange("(o t) -> o t", o=1), rrow[:], sto)
            p.wait("scalar", sto, sto.n)
    p.barrier(tag, reclaim=True)


def p_moe(p, xn3_d, r_d, moe_d, w1_b, w2_b, h_d, h_d2, cast_w2, tag="p6"):
    nc = p.nc
    NHC = HID // 128
    NTC = T // 512
    hds = [h_d, h_d2]
    with contextlib.ExitStack() as stk:
        Eo = stk.enter_context
        acc = [Eo(nc.sbuf_tensor(f"{tag}ac{i}", [128, T], BF16))
               for i in range(NCH)]
        x3p = [Eo(nc.sbuf_tensor(f"{tag}xp{i}", [128, T], BF16))
               for i in range(NCH)]
        w2t = [Eo(nc.sbuf_tensor(f"{tag}w2{i}", [128, HID], BF16))
               for i in range(2)]
        hbl = [Eo(nc.sbuf_tensor(f"{tag}hb{i}", [128, 512], BF16))
               for i in range(NHC)]
        w1t = [Eo(nc.sbuf_tensor(f"{tag}w1{i}", [128, D], BF16))
               for i in range(2)]
        hst = [Eo(nc.sbuf_tensor(f"{tag}hs{i}", [128, 512], BF16))
               for i in range(3)]
        yst = [Eo(nc.sbuf_tensor(f"{tag}ys{i}", [128, 512], BF16))
               for i in range(3)]
        mei = [Eo(nc.sbuf_tensor(f"{tag}me{i}", [128, 512], I32))
               for i in range(2)]
        rrow = Eo(nc.sbuf_tensor(f"{tag}rr", [1, T], BF16))
        oneb = Eo(nc.sbuf_tensor(f"{tag}oneb", [1, 128], BF16))
        rbs = [Eo(nc.sbuf_tensor(f"{tag}rb{i}", [128, 512], BF16))
               for i in range(NTC)]
        ps_h = [Eo(nc.psum_tensor(f"{tag}ph{i}", [128, 512], F32))
                for i in range(3)]
        ps_y = [Eo(nc.psum_tensor(f"{tag}py{i}", [128, 512], F32))
                for i in range(3)]
        ps_rb = [Eo(nc.psum_tensor(f"{tag}prb{i}", [128, 512], F32))
                 for i in range(2)]
        pe = p.sem("pe"); act = p.sem("act"); dve = p.sem("dve")
        ldr = p.sem("ldr"); ldx = p.dsem("ldx")
        ldw1 = [p.dsem("lw0"), p.dsem("lw1")]
        ldw2 = [p.dsem("l20"), p.dsem("l21")]
        ldh = [p.dsem("ldh0"), p.dsem("ldh1")]
        sto1 = [p.dsem("s10"), p.dsem("s11"), p.dsem("s12")]
        sto = p.dsem("sto")
        for cs, cn in cast_w2:
            p.wait("sync", cs, cn)
        p.dma("sync", rrow[:], r_d.rearrange("(o t) -> o t", o=1), ldr)
        for ic in range(NCH):
            p.dma("sync", x3p[ic][:], xn3_d[ic * 128:(ic + 1) * 128, :], ldx)
        x3_post = ldx.n
        p.op("vector", lambda En: En.memset(oneb[:], 1.0), inc=dve)
        for i in range(NCH):
            p.op("vector", lambda En, i=i: En.memset(acc[i][:], 0.0), inc=dve)
        acc0 = dve.n
        rb_free = [0, 0]
        for tc in range(NTC):
            b = tc % 2
            p.wait("tensor", ldr, ldr.n)
            p.wait("tensor", dve, acc0)
            p.wait("tensor", act, rb_free[b])
            p.op("tensor", lambda En, b=b, tc=tc: En.matmul(
                ps_rb[b][:], oneb[:], rrow[0:1, tc * 512:(tc + 1) * 512],
                start=True, stop=True), inc=pe)
            p.wait("scalar", pe, pe.n)
            p.op("scalar", lambda En, b=b, tc=tc: En.activation(
                rbs[tc][:], ps_rb[b][:], AF.Copy), inc=act)
            rb_free[b] = act.n
        rb_post = act.n
        w1_free = [0, 0]
        psh_free = [0, 0, 0]
        w2_free = [0, 0]
        y_free = [0, 0, 0]
        yst_free = [0, 0, 0]
        me_post = [0, 0]
        w2_post = [0, 0]
        ldh_after = {}
        p1_done = {}
        for e in range(E):
            hd = hds[e % 2]
            if e >= 2:
                p.wait("scalar", ldh[0], ldh_after[e - 2][0])
                p.wait("scalar", ldh[1], ldh_after[e - 2][1])
            for hc in range(NHC):
                wb = hc % 2
                p.wait("sync", pe, w1_free[wb])
                p.dma("sync", w1t[wb][:].rearrange("p (ic o) -> p ic o", o=128),
                      w1_b[e][:, hc * 128:(hc + 1) * 128].rearrange(
                          "(ic p) o -> p ic o", p=128), ldw1[wb])
                w1_post = ldw1[wb].n
                p.wait("tensor", ldw1[wb], w1_post)
                if hc == 0 and e == 0:
                    p.wait("tensor", ldx, x3_post)
                for tc in range(NTC):
                    hb = tc % 3
                    p.wait("tensor", act, psh_free[hb])
                    for ic in range(NCH):
                        p.op("tensor", lambda En, hb=hb, wb=wb, ic=ic, tc=tc:
                             En.matmul(
                                 ps_h[hb][:],
                                 w1t[wb][:, ic * 128:(ic + 1) * 128],
                                 x3p[ic][:, tc * 512:(tc + 1) * 512],
                                 start=(ic == 0), stop=(ic == NCH - 1)),
                             inc=pe)
                    p.wait("scalar", pe, pe.n)
                    p.wait("scalar", sto1[hb], sto1[hb].n)
                    p.op("scalar", lambda En, hb=hb: En.activation(
                        hst[hb][:], ps_h[hb][:], AF.Gelu_apprx_tanh), inc=act)
                    psh_free[hb] = act.n
                    p.wait("scalar", act, act.n)
                    p.dma("scalar", hd[hc * 128:(hc + 1) * 128,
                                       tc * 512:(tc + 1) * 512],
                          hst[hb][:], sto1[hb])
                w1_free[wb] = pe.n
            p1_done[e] = (sto1[0].n, sto1[1].n, sto1[2].n)
            for si in range(3):
                p.wait("sync", sto1[si], p1_done[e][si])
            NHH = NHC // 2
            hlast = {}

            def load_half(tc2, half, gate_pe):
                p.wait("sync", pe, gate_pe)
                for hc in range(half * NHH, (half + 1) * NHH):
                    p.dma("sync", hbl[hc][:],
                          hd[hc * 128:(hc + 1) * 128,
                             tc2 * 512:(tc2 + 1) * 512], ldh[half])
                return ldh[half].n

            hp = {(0, 0): load_half(0, 0, 0)}
            hp[(0, 1)] = load_half(0, 1, 0)
            for tc in range(NTC):
                mb = tc % 2
                p.wait("vector", dve, me_post[mb])
                p.wait("vector", act, rb_post)
                p.op("vector", lambda En, mb=mb, tc=tc, e=e: En.tensor_scalar(
                    mei[mb][:], rbs[tc][:], float(e), None, ALU.is_equal),
                    inc=dve)
                mei_post = dve.n
                for oc in range(NCH):
                    yb = oc % 3
                    w2b_ = oc % 2
                    p.wait("sync", pe, w2_free[w2b_])
                    p.dma("sync",
                          w2t[w2b_][:].rearrange("p (hc o) -> p hc o", o=128),
                          w2_b[e][:, oc * 128:(oc + 1) * 128].rearrange(
                              "(hc p) o -> p hc o", p=128), ldw2[w2b_])
                    w2_post[w2b_] = ldw2[w2b_].n
                    p.wait("tensor", ldw2[w2b_], w2_post[w2b_])
                    p.wait("tensor", act, y_free[yb])
                    for half in range(2):
                        p.wait("tensor", ldh[half], hp[(tc, half)])
                        for hc in range(half * NHH, (half + 1) * NHH):
                            p.op("tensor", lambda En, yb=yb, w2b_=w2b_, hc=hc:
                                 En.matmul(
                                     ps_y[yb][:],
                                     w2t[w2b_][:, hc * 128:(hc + 1) * 128],
                                     hbl[hc][:],
                                     start=(hc == 0), stop=(hc == NHC - 1)),
                                 inc=pe)
                        if oc == NCH - 1 and tc + 1 < NTC:
                            hp[(tc + 1, half)] = load_half(tc + 1, half, pe.n)
                    w2_free[w2b_] = pe.n
                    p.wait("scalar", pe, pe.n)
                    p.wait("scalar", dve, yst_free[yb])
                    p.op("scalar", lambda En, yb=yb: En.activation(
                        yst[yb][:], ps_y[yb][:], AF.Copy), inc=act)
                    y_free[yb] = act.n
                    p.wait("vector", act, act.n)
                    p.wait("vector", dve, mei_post)
                    p.op("vector", lambda En, yb=yb, mb=mb, oc=oc, tc=tc:
                         En.copy_predicated(
                             acc[oc][:, tc * 512:(tc + 1) * 512],
                             mei[mb][:], yst[yb][:]), inc=dve)
                    yst_free[yb] = dve.n
                me_post[mb] = dve.n
            ldh_after[e] = (ldh[0].n, ldh[1].n)
        p.wait("scalar", dve, dve.n)
        for oc in range(NCH):
            p.dma("scalar", moe_d[oc * 128:(oc + 1) * 128, :], acc[oc][:], sto)
        p.wait("scalar", sto, sto.n)
    p.barrier(tag, reclaim=True)


def p_gsilu(p, y1_d, moe_d, g_d, tag="p7"):
    nc = p.nc
    with contextlib.ExitStack() as stk:
        Eo = stk.enter_context
        NB3 = 3
        yt = [Eo(nc.sbuf_tensor(f"{tag}y{i}", [128, 512], F32)) for i in range(NB3)]
        mt = [Eo(nc.sbuf_tensor(f"{tag}m{i}", [128, 512], BF16)) for i in range(NB3)]
        st_ = [Eo(nc.sbuf_tensor(f"{tag}s{i}", [128, 512], F32)) for i in range(NB3)]
        gt = [Eo(nc.sbuf_tensor(f"{tag}g{i}", [128, 512], BF16)) for i in range(NB3)]
        ld = [p.dsem(f"ld{i}") for i in range(NB3)]
        sto = [p.dsem(f"st{i}") for i in range(NB3)]
        dve = p.sem("dve"); act = p.sem("act")
        buf_free = [0] * NB3
        st_free = [0] * NB3
        for i in range(NCH * (T // 512)):
            oc, tc = divmod(i, T // 512)
            b = i % NB3
            p.wait("sync", dve, buf_free[b])
            p.wait("sync", act, st_free[b])
            sl = (slice(oc * 128, (oc + 1) * 128), slice(tc * 512, (tc + 1) * 512))
            p.dma("sync", yt[b][:], y1_d[sl[0], sl[1]], ld[b])
            p.dma("sync", mt[b][:], moe_d[sl[0], sl[1]], ld[b])
            p.wait("vector", ld[b], ld[b].n)
            p.op("vector", lambda En, b=b: En.tensor_add(
                st_[b][:], yt[b][:], mt[b][:]), inc=dve)
            buf_free[b] = dve.n
            p.wait("scalar", dve, dve.n)
            p.wait("scalar", sto[b], sto[b].n)
            p.op("scalar", lambda En, b=b: En.activation(
                gt[b][:], st_[b][:], AF.Silu), inc=act)
            st_free[b] = act.n
            p.wait("scalar", act, act.n)
            p.dma("scalar", g_d[sl[0], sl[1]], gt[b][:], sto[b])
        p.wait("scalar", sto[0], sto[0].n)
        p.wait("scalar", sto[1], sto[1].n)
    p.barrier(tag, reclaim=True)


def p_resln(p, x_ap, s_d, g_d, x1_d, xn1_d, tag="p8"):
    nc = p.nc
    with contextlib.ExitStack() as stk:
        Eo = stk.enter_context
        NB = 3
        xt = [Eo(nc.sbuf_tensor(f"{tag}x{i}", [128, D], F32)) for i in range(NB)]
        st_ = [Eo(nc.sbuf_tensor(f"{tag}s{i}", [128, D], BF16)) for i in range(NB)]
        gt = [Eo(nc.sbuf_tensor(f"{tag}g{i}", [128, D], BF16)) for i in range(NB)]
        x1t = [Eo(nc.sbuf_tensor(f"{tag}o{i}", [128, D], F32)) for i in range(NB)]
        xnt = [Eo(nc.sbuf_tensor(f"{tag}n{i}", [128, D], BF16)) for i in range(NB)]
        xsq = [Eo(nc.sbuf_tensor(f"{tag}q{i}", [128, D], F32)) for i in range(2)]
        scol = [Eo(nc.sbuf_tensor(f"{tag}sc{i}", [128, 1], F32)) for i in range(NB)]
        qcol = [Eo(nc.sbuf_tensor(f"{tag}qc{i}", [128, 1], F32)) for i in range(NB)]
        mcol = [Eo(nc.sbuf_tensor(f"{tag}mc{i}", [128, 1], F32)) for i in range(NB)]
        vcol = [Eo(nc.sbuf_tensor(f"{tag}vc{i}", [128, 1], F32)) for i in range(NB)]
        rcol = [Eo(nc.sbuf_tensor(f"{tag}rc{i}", [128, 1], F32)) for i in range(NB)]
        epsc = Eo(nc.sbuf_tensor(f"{tag}eps", [128, 1], F32))
        ld = [p.dsem(f"ld{i}") for i in range(NB)]
        sto1 = [p.dsem(f"sa{i}") for i in range(NB)]
        sto2 = [p.dsem(f"sb{i}") for i in range(NB)]
        act = p.sem("act"); dve = p.sem("dve")
        rstd = p.sem("rstd"); app = p.sem("app")
        cst = p.sem("cst")
        p.op("vector", lambda En: En.memset(epsc[:], float(EPS)), inc=cst)
        p.wait("scalar", cst, cst.n)
        sq_post = {}
        app_post = {}
        for i in range(NT):
            b = i % NB
            if i >= NB:
                p.wait("sync", app, app_post[i - NB])
                p.wait("sync", act, sq_post[i - NB])
                p.wait("sync", sto1[b], sto1[b].n)
            p.dma("sync", xt[b][:], x_ap[i * 128:(i + 1) * 128, :], ld[b])
            p.dma("sync", st_[b][:], s_d[:, i * 128:(i + 1) * 128], ld[b],
                  transpose=True)
            p.dma("sync", gt[b][:], g_d[:, i * 128:(i + 1) * 128], ld[b],
                  transpose=True)
            p.wait("vector", ld[b], ld[b].n)
            p.op("vector", lambda En, b=b: En.tensor_mul(
                x1t[b][:], st_[b][:], gt[b][:]), inc=dve)
            p.wait("vector", dve, dve.n)
            p.op("vector", lambda En, b=b: En.tensor_add(
                x1t[b][:], x1t[b][:], xt[b][:]), inc=dve)
            x1p = dve.n
            p.wait("scalar", dve, x1p)
            p.dma("scalar", x1_d[i * 128:(i + 1) * 128, :], x1t[b][:], sto1[b])
            p.wait("vector", dve, x1p)
            p.op("vector", lambda En, b=b: En.reduce_sum(
                scol[b][:], x1t[b][:], AX.X), inc=dve)
            p.wait("vector", dve, dve.n)
            p.op("vector", lambda En, b=b: En.tensor_scalar(
                mcol[b][:], scol[b][:], 1.0 / D, None, ALU.mult), inc=dve)
            p.wait("scalar", dve, dve.n)
            ent = p.op("scalar", lambda En, b=b, i=i: En.activation(
                xsq[i % 2][:], x1t[b][:], AF.Square, accum_out=qcol[b][:, 0:1]),
                inc=act)
            sq_post[i] = ent["post"]
            p.wait("vector", dve, dve.n)
            p.wait("vector", act, act.n)
            p.op("vector", lambda En, b=b: En.tensor_tensor(
                vcol[b][:], mcol[b][:], mcol[b][:], ALU.mult), inc=dve)
            p.wait("vector", dve, dve.n)
            p.op("vector", lambda En, b=b: En.scalar_tensor_tensor(
                vcol[b][:], qcol[b][:], 1.0 / D, vcol[b][:], ALU.mult,
                ALU.subtract), inc=dve)
            p.wait("scalar", dve, dve.n)
            p.op("scalar", lambda En, b=b: En.activation(
                rcol[b][:], vcol[b][:], AF.Ln, bias=epsc[:, 0:1]), inc=act)
            p.wait("scalar", act, act.n)
            p.op("scalar", lambda En, b=b: En.activation(
                rcol[b][:], rcol[b][:], AF.Exp, scale=-0.5), inc=rstd)
            p.wait("vector", rstd, rstd.n)
            p.wait("vector", sto2[b], sto2[b].n)
            p.op("vector", lambda En, b=b: En.tensor_scalar(
                xnt[b][:], x1t[b][:], mcol[b][:, 0:1], rcol[b][:, 0:1],
                ALU.subtract, ALU.mult), inc=app)
            app_post[i] = app.n
            p.wait("scalar", app, app.n)
            p.dma("scalar", xn1_d[i * 128:(i + 1) * 128, :], xnt[b][:], sto2[b])
        for s in sto1 + sto2:
            p.wait("scalar", s, s.n)
    p.barrier(tag, reclaim=True)


def p_l1ffn(p, xn1_d, wg_b, wu_b, wd_b, g1_d, cast_wd, tag="p11"):
    nc = p.nc
    NH = FHID // 128
    with contextlib.ExitStack() as stk:
        Eo = stk.enter_context
        wgp = [Eo(nc.sbuf_tensor(f"{tag}wg{i}", [128, FHID], BF16))
               for i in range(NCH)]
        wup = [Eo(nc.sbuf_tensor(f"{tag}wu{i}", [128, FHID], BF16))
               for i in range(NCH)]
        wdp = [Eo(nc.sbuf_tensor(f"{tag}wd{i}", [128, D], BF16))
               for i in range(NH)]
        xin = [Eo(nc.sbuf_tensor(f"{tag}xi{i}", [128, 512], BF16))
               for i in range(NCH)]
        hsb = [Eo(nc.sbuf_tensor(f"{tag}h{i}", [128, 512], BF16))
               for i in range(NH)]
        gst = [Eo(nc.sbuf_tensor(f"{tag}gs{i}", [128, 512], BF16))
               for i in range(2)]
        ost = [Eo(nc.sbuf_tensor(f"{tag}os{i}", [128, 512], BF16))
               for i in range(2)]
        ps_g = [Eo(nc.psum_tensor(f"{tag}pg{i}", [128, 512], F32))
                for i in range(2)]
        ps_u = [Eo(nc.psum_tensor(f"{tag}pu{i}", [128, 512], F32))
                for i in range(2)]
        ps_o = [Eo(nc.psum_tensor(f"{tag}po{i}", [128, 512], F32))
                for i in range(2)]
        ld = p.dsem("ld"); ldx = p.dsem("ldx"); pe = p.sem("pe")
        act = p.sem("act"); dve = p.sem("dve")
        sto = [p.dsem("st0"), p.dsem("st1")]
        for cs, cn in cast_wd:
            p.wait("sync", cs, cn)
        for ic in range(NCH):
            p.dma("sync", wgp[ic][:], wg_b[:, ic * 128:(ic + 1) * 128], ld,
                  transpose=True)
            p.dma("sync", wup[ic][:], wu_b[:, ic * 128:(ic + 1) * 128], ld,
                  transpose=True)
        for hc in range(NH):
            p.dma("sync", wdp[hc][:], wd_b[:, hc * 128:(hc + 1) * 128], ld,
                  transpose=True)
        w_post = ld.n
        g_free = [0, 0]
        o_free = [0, 0]
        ost_free = [0, 0]
        for tc in range(T // 512):
            if tc > 0:
                p.wait("sync", pe, pe.n)
            for ic in range(NCH):
                p.dma("sync", xin[ic][:],
                      xn1_d[tc * 512:(tc + 1) * 512, ic * 128:(ic + 1) * 128],
                      ldx, transpose=True)
            x_post = ldx.n
            hsb_post = 0
            for hc in range(NH):
                gb = hc % 2
                p.wait("tensor", ld, w_post)
                p.wait("tensor", ldx, x_post)
                p.wait("tensor", dve, g_free[gb])
                for ic in range(NCH):
                    p.op("tensor", lambda En, gb=gb, hc=hc, ic=ic: En.matmul(
                        ps_g[gb][:], wgp[ic][:, hc * 128:(hc + 1) * 128],
                        xin[ic][:], start=(ic == 0), stop=(ic == NCH - 1)),
                        inc=pe)
                for ic in range(NCH):
                    p.op("tensor", lambda En, gb=gb, hc=hc, ic=ic: En.matmul(
                        ps_u[gb][:], wup[ic][:, hc * 128:(hc + 1) * 128],
                        xin[ic][:], start=(ic == 0), stop=(ic == NCH - 1)),
                        inc=pe)
                p.wait("scalar", pe, pe.n)
                p.wait("scalar", dve, g_free[gb])
                p.op("scalar", lambda En, gb=gb: En.activation(
                    gst[gb][:], ps_g[gb][:], AF.Silu, scale=1.0 / 127.0),
                    inc=act)
                p.wait("vector", act, act.n)
                p.op("vector", lambda En, gb=gb, hc=hc: En.scalar_tensor_tensor(
                    hsb[hc][:], ps_u[gb][:], 1.0 / 127.0, gst[gb][:],
                    ALU.mult, ALU.mult), inc=dve)
                g_free[gb] = dve.n
            h_post = dve.n
            for oc in range(NCH):
                ob = oc % 2
                p.wait("tensor", dve, h_post)
                p.wait("tensor", act, o_free[ob])
                for hc in range(NH):
                    p.op("tensor", lambda En, ob=ob, oc=oc, hc=hc: En.matmul(
                        ps_o[ob][:], wdp[hc][:, oc * 128:(oc + 1) * 128],
                        hsb[hc][:], start=(hc == 0), stop=(hc == NH - 1)),
                        inc=pe)
                p.wait("scalar", pe, pe.n)
                p.wait("scalar", sto[ob], sto[ob].n)
                p.op("scalar", lambda En, ob=ob: En.activation(
                    ost[ob][:], ps_o[ob][:], AF.Silu, scale=1.0 / 127.0),
                    inc=act)
                o_free[ob] = act.n
                p.wait("scalar", act, act.n)
                p.dma("scalar", g1_d[oc * 128:(oc + 1) * 128,
                                     tc * 512:(tc + 1) * 512], ost[ob][:],
                      sto[ob])
        p.wait("scalar", sto[0], sto[0].n)
        p.wait("scalar", sto[1], sto[1].n)
    p.barrier(tag, reclaim=True)


def p_out(p, x1_d, s1_d, g1_d, out_ap, tag="p12"):
    nc = p.nc
    with contextlib.ExitStack() as stk:
        Eo = stk.enter_context
        NB = 3
        xt = [Eo(nc.sbuf_tensor(f"{tag}x{i}", [128, D], F32)) for i in range(NB)]
        st_ = [Eo(nc.sbuf_tensor(f"{tag}s{i}", [128, D], BF16)) for i in range(NB)]
        gt = [Eo(nc.sbuf_tensor(f"{tag}g{i}", [128, D], BF16)) for i in range(NB)]
        ot = [Eo(nc.sbuf_tensor(f"{tag}o{i}", [128, D], F32)) for i in range(NB)]
        ld = [p.dsem(f"ld{i}") for i in range(NB)]
        sto = [p.dsem(f"st{i}") for i in range(NB)]
        dve = p.sem("dve")
        dv_post = {}
        for i in range(NT):
            b = i % NB
            if i >= NB:
                p.wait("sync", dve, dv_post[i - NB])
                p.wait("sync", sto[b], sto[b].n)
            p.dma("sync", xt[b][:], x1_d[i * 128:(i + 1) * 128, :], ld[b])
            p.dma("sync", st_[b][:], s1_d[:, i * 128:(i + 1) * 128], ld[b],
                  transpose=True)
            p.dma("sync", gt[b][:], g1_d[:, i * 128:(i + 1) * 128], ld[b],
                  transpose=True)
            p.wait("vector", ld[b], ld[b].n)
            p.op("vector", lambda En, b=b: En.tensor_mul(
                ot[b][:], st_[b][:], gt[b][:]), inc=dve)
            p.wait("vector", dve, dve.n)
            p.op("vector", lambda En, b=b: En.tensor_add(
                ot[b][:], ot[b][:], xt[b][:]), inc=dve)
            dv_post[i] = dve.n
            p.wait("scalar", dve, dve.n)
            p.dma("scalar", out_ap[i * 128:(i + 1) * 128, :], ot[b][:], sto[b])
        for s in sto:
            p.wait("scalar", s, s.n)
    p.barrier(tag, reclaim=True)


def declare_scratch(nc):
    d = {}
    def mk(name, shape, dt):
        d[name] = nc.dram_tensor(name, shape, dt, kind="Internal").ap()
    mk("xn_d", [T, D], BF16)
    mk("xn2_d", [T, D], BF16)
    mk("s_d", [D, T], BF16)
    mk("y1_d", [D, T], F32)
    mk("xn3_d", [D, T], BF16)
    mk("r_d", [T], BF16)
    mk("moe_d", [D, T], BF16)
    mk("g_d", [D, T], BF16)
    mk("x1_d", [T, D], F32)
    mk("xn1_d", [T, D], BF16)
    mk("h_d", [HID, T], BF16)
    mk("h_d2", [HID, T], BF16)
    mk("lat_d", [256, T], BF16)
    mk("vd_d", [128, T], BF16)
    mk("s1_d", [D, T], BF16)
    mk("g1_d", [D, T], BF16)
    mk("qw_b", [D, D], BF16)
    mk("kw_b", [D, D], BF16)
    mk("vw_b", [D, D], BF16)
    mk("ow_b", [D, D], BF16)
    mk("wqkv_b", [D, 3 * D], BF16)
    mk("wproj_b", [D, D], BF16)
    mk("gw_b", [D, E], BF16)
    mk("gb_b", [E], BF16)
    mk("w1_b", [E, D, HID], BF16)
    mk("w2_b", [E, HID, D], BF16)
    mk("l1q_b", [D, D], BF16)
    mk("l1kv_b", [D, 256], BF16)
    mk("l1ku_b", [256, D], BF16)
    mk("l1vu_b", [256, D], BF16)
    mk("l1o_b", [D, D], BF16)
    mk("wg_b", [FHID, D], BF16)
    mk("wu_b", [FHID, D], BF16)
    mk("wd_b", [D, FHID], BF16)
    return d


def declare_inputs(nc):
    a = {}
    def mk(name, shape, dt):
        a[name] = nc.dram_tensor(name, shape, dt, kind="ExternalInput").ap()
    mk("x", [T, D], F32)
    mk("L0_attn_q_w", [D, D], F32)
    mk("L0_attn_k_w", [D, D], F32)
    mk("L0_attn_v_w", [D, D], F32)
    mk("L0_attn_o_w", [D, D], F32)
    mk("L0_ffn_ln1_w", [D], F32)
    mk("L0_ffn_ln1_b", [D], F32)
    mk("L0_ffn_w_qkv", [D, 3 * D], F32)
    mk("L0_ffn_w_proj", [D, D], F32)
    mk("L0_ffn_ln2_w", [D], F32)
    mk("L0_ffn_ln2_b", [D], F32)
    mk("L0_ffn_gate_w", [D, E], F32)
    mk("L0_ffn_gate_b", [E], F32)
    mk("L0_ffn_w1", [E, D, HID], F32)
    mk("L0_ffn_w2", [E, HID, D], F32)
    mk("L1_attn_Wq", [D, H, DKH], F32)
    mk("L1_attn_Wkv_down", [D, 256], F32)
    mk("L1_attn_Wk_up", [256, H, DKH], F32)
    mk("L1_attn_Wv_up", [256, H, DKH], F32)
    mk("L1_attn_Wout", [H, DKH, D], F32)
    mk("L1_ffn_w_gate", [FHID, D], I32)
    mk("L1_ffn_w_up", [FHID, D], I32)
    mk("L1_ffn_w_down", [D, FHID], I32)
    return a


CASTS = [
    ("L0_attn_q_w", "qw_b"), ("L0_attn_k_w", "kw_b"), ("L0_attn_v_w", "vw_b"),
    ("L0_attn_o_w", "ow_b"),
    ("L0_ffn_w_qkv", "wqkv_b"), ("L0_ffn_w_proj", "wproj_b"),
    ("L1_attn_Wq", "l1q_b"), ("L1_attn_Wkv_down", "l1kv_b"),
    ("L1_attn_Wk_up", "l1ku_b"), ("L1_attn_Wv_up", "l1vu_b"),
    ("L1_attn_Wout", "l1o_b"),
    ("L1_ffn_w_gate", "wg_b"), ("L1_ffn_w_up", "wu_b"),
    ("L1_ffn_w_down", "wd_b"),
    ("L0_ffn_w1", "w1_b"), ("L0_ffn_w2", "w2_b"),
]


def emit_casts(p, ins, dram):
    after = {}
    for src, dst in CASTS:
        sap = ins[src]
        dap = dram[dst]
        cs = p.sem("c_" + dst, persist=True)
        if src in ("L0_ffn_w1", "L0_ffn_w2"):
            for e in range(E):
                p.dma("gpsimd", dap[e], sap[e], cs)
        elif src in ("L1_attn_Wq", "L1_attn_Wk_up", "L1_attn_Wv_up"):
            p.dma("gpsimd", dap, sap.rearrange("a h k -> a (h k)"), cs)
        elif src == "L1_attn_Wout":
            p.dma("gpsimd", dap, sap.rearrange("h k o -> (h k) o"), cs)
        else:
            p.dma("gpsimd", dap, sap, cs)
        after[dst] = (cs, cs.n)
    return after


def build_launch(phase_fns):
    nc = bass.Bass("TRN2", target_bir_lowering=False)
    with contextlib.ExitStack() as stack:
        p = Prog(nc, stack)
        ins = declare_inputs(nc)
        dram = declare_scratch(nc)
        out = nc.dram_tensor("out", [T, D], F32, kind="ExternalOutput").ap()
        for f in phase_fns:
            f(p, ins, dram, out)
        p.emit()
    return nc


def wcol(ap, oc, off=0):
    return ap[:, off + oc * 128: off + (oc + 1) * 128].rearrange(
        "(ic p) o -> p ic o", p=128)


def all_phases(p, ins, dram, out):
    ca = emit_casts(p, ins, dram)
    p_ln_tokmajor(p, ins["x"], dram["xn_d"], F32, "p01")
    p_attn(p, AttnCfg(
        tag="a0", v_d=dram["vd_d"], src_d=dram["xn_d"], out_d=dram["s_d"], causal=True, alibi=True,
        qw=lambda oc: wcol(dram["qw_b"], oc),
        kw_=lambda oc: wcol(dram["kw_b"], oc),
        vw=lambda oc: wcol(dram["vw_b"], oc),
        ow=lambda oc: wcol(dram["ow_b"], oc),
        wait_cast=[ca["qw_b"], ca["kw_b"], ca["vw_b"], ca["ow_b"]]))
    p_ln_tokmajor(p, dram["xn_d"], dram["xn2_d"], BF16, "p3")
    p_attn(p, AttnCfg(
        tag="a1", v_d=dram["vd_d"], src_d=dram["xn2_d"], out_d=dram["y1_d"], causal=False,
        alibi=False, out_f32=True, resid_src=dram["xn_d"],
        affine=(ins["L0_ffn_ln1_w"], ins["L0_ffn_ln1_b"]),
        qw=lambda oc: wcol(dram["wqkv_b"], oc, 0),
        kw_=lambda oc: wcol(dram["wqkv_b"], oc, D),
        vw=lambda oc: wcol(dram["wqkv_b"], oc, 2 * D),
        ow=lambda oc: wcol(dram["wproj_b"], oc),
        wait_cast=[ca["wqkv_b"], ca["wproj_b"]]))
    p_ln2_gate(p, dram["y1_d"], dram["xn3_d"], dram["r_d"],
               ins["L0_ffn_ln2_w"], ins["L0_ffn_ln2_b"],
               ins["L0_ffn_gate_w"], ins["L0_ffn_gate_b"], [])
    p_moe(p, dram["xn3_d"], dram["r_d"], dram["moe_d"],
          dram["w1_b"], dram["w2_b"], dram["h_d"], dram["h_d2"],
          [ca["w1_b"], ca["w2_b"]])
    p_gsilu(p, dram["y1_d"], dram["moe_d"], dram["g_d"])
    p_resln(p, ins["x"], dram["s_d"], dram["g_d"], dram["x1_d"], dram["xn1_d"])
    p_attn(p, AttnCfg(
        tag="a2", v_d=dram["vd_d"], src_d=dram["xn1_d"], out_d=dram["s1_d"], causal=True,
        alibi=False,
        qw=lambda oc: wcol(dram["l1q_b"], oc),
        kw_=None, vw=None,
        latent=dict(dw=lambda ocl: wcol(dram["l1kv_b"], ocl),
                    ku=lambda oc: wcol(dram["l1ku_b"], oc),
                    vu=lambda oc: wcol(dram["l1vu_b"], oc),
                    lat_d=dram["lat_d"]),
        ow=lambda oc: wcol(dram["l1o_b"], oc),
        wait_cast=[ca["l1q_b"], ca["l1kv_b"], ca["l1ku_b"], ca["l1vu_b"],
                   ca["l1o_b"]]))
    p_l1ffn(p, dram["xn1_d"], dram["wg_b"], dram["wu_b"], dram["wd_b"],
            dram["g1_d"], [ca["wg_b"], ca["wu_b"], ca["wd_b"]])
    p_out(p, dram["x1_d"], dram["s1_d"], dram["g1_d"], out)


def kernel(**inputs):
    xfull = np.ascontiguousarray(inputs["L0_attn_x"])
    weights = {k: np.ascontiguousarray(v) for k, v in inputs.items()
               if k != "L0_attn_x"}
    in_maps = []
    for c in range(8):
        m = {"x": xfull[c]}
        m.update(weights)
        in_maps.append(m)
    nc = build_launch([all_phases])
    res = run_bass_kernel_spmd(nc, in_maps, core_ids=list(range(8)))
    return np.stack([r["out"] for r in res.results], axis=0)


# version 11
# speedup vs baseline: 1.2179x; 1.0258x over previous
import contextlib
import math
import numpy as np

import concourse.bass as bass
import concourse.mybir as mybir
from concourse.bass_utils import run_bass_kernel_spmd

F32 = mybir.dt.float32
BF16 = mybir.dt.bfloat16
I32 = mybir.dt.int32
AF = mybir.ActivationFunctionType
ALU = mybir.AluOpType
AX = mybir.AxisListType

ENGINES = ("sync", "scalar", "vector", "tensor", "gpsimd")

T = 4096
D = 1024
NT = T // 128
NCH = D // 128


def set_dims(T_):
    global T, NT
    T = T_
    NT = T // 128
H = 16
DKH = 64
HID = 4096
E = 8
FHID = 3072
EPS = 1e-5
NEG = -1e30


class Sem:
    def __init__(self, prog, name):
        self.h = prog.stack.enter_context(prog.nc.semaphore(name))
        self.n = 0
        self.dma_issuers = set()


class Prog:

    def __init__(self, nc, stack):
        self.nc = nc
        self.stack = stack
        self.entries = {e: [] for e in ENGINES}
        self._semid = 0
        self._all_sems = []
        self._pool = {False: [], True: []}
        self._phase_sems = []
        self._bar = None
        self._barcnt = 0
        self._qsems = {}

    def sem(self, tag="", persist=False, dma=False):
        if not persist and self._pool[dma]:
            s = self._pool[dma].pop()
        else:
            self._semid += 1
            s = Sem(self, f"s{self._semid}{tag}")
            self._all_sems.append(s)
            s.is_dma = dma
        if not persist:
            self._phase_sems.append(s)
        return s

    def dsem(self, tag=""):
        return self.sem(tag, dma=True)

    def op(self, eng, fn, inc=None, amt=1):
        ent = {"fn": fn, "sem": inc, "amt": amt if inc is not None else 0,
               "wait": None}
        if inc is not None:
            inc.n += amt
            ent["post"] = inc.n
        self.entries[eng].append(ent)
        return ent

    def wait(self, eng, sem, n=None):
        tgt = sem.n if n is None else n
        self.entries[eng].append({"fn": None, "sem": None, "amt": 0,
                                  "wait": (sem, tgt)})
        return tgt

    def dma(self, eng, out, in_, inc, **kw):
        inc.dma_issuers.add(eng)
        return self.op(eng,
                       lambda En, o=out, i=in_, k=dict(kw): En.dma_start(out=o, in_=i, **k),
                       inc=inc, amt=16)

    def barrier(self, tag="", reclaim=False):
        if self._bar is None:
            self._bar = self.sem("bar", persist=True)
            for e in ENGINES:
                self._qsems[e] = self.sem("q" + e, persist=True)
        bar = self._bar
        for s in self._all_sems:
            for e in s.dma_issuers:
                self.wait(e, s, s.n)
        for e in ENGINES:
            last = None
            for ent in reversed(self.entries[e]):
                if ent["fn"] is not None:
                    last = ent
                    break
            if last is not None and last["sem"] is not bar:
                if last["sem"] is None:
                    q = self._qsems[e]
                    q.n += 1
                    last["sem"] = q
                    last["amt"] = 1
                    last["post"] = q.n
                self.wait(e, last["sem"], last["post"])
            self.op(e, lambda En: En.nop(), inc=bar, amt=1)
        self._barcnt += 5
        for e in ENGINES:
            self.wait(e, bar, self._barcnt)
        if reclaim:
            for s in self._phase_sems:
                self._pool[getattr(s, "is_dma", False)].append(s)
            self._phase_sems = []

    def emit(self):
        nc = self.nc
        handles = {"sync": nc.sync, "scalar": nc.scalar, "vector": nc.vector,
                   "tensor": nc.tensor, "gpsimd": nc.gpsimd}

        def run(e):
            En = handles[e]
            for ent in self.entries[e]:
                if ent["wait"] is not None:
                    s, t = ent["wait"]
                    En.wait_ge(s.h, t)
                    continue
                r = ent["fn"](En)
                if ent["sem"] is not None:
                    r.then_inc(ent["sem"].h, ent["amt"])

        with nc.Block() as block:
            block.sync(lambda _: run("sync"))
            block.scalar(lambda _: run("scalar"))
            block.vector(lambda _: run("vector"))
            block.tensor(lambda _: run("tensor"))
            block.gpsimd(lambda _: run("gpsimd"))


def p_ln_tokmajor(p, x_ap, out_d, in_dtype, tag):
    nc = p.nc
    with contextlib.ExitStack() as stk:
        Eo = stk.enter_context
        NB = 3
        xt = [Eo(nc.sbuf_tensor(f"{tag}x{i}", [128, D], in_dtype)) for i in range(NB)]
        xnt = [Eo(nc.sbuf_tensor(f"{tag}n{i}", [128, D], BF16)) for i in range(NB)]
        xsq = [Eo(nc.sbuf_tensor(f"{tag}q{i}", [128, D], F32)) for i in range(2)]
        scol = [Eo(nc.sbuf_tensor(f"{tag}s{i}", [128, 1], F32)) for i in range(NB)]
        qcol = [Eo(nc.sbuf_tensor(f"{tag}c{i}", [128, 1], F32)) for i in range(NB)]
        mcol = [Eo(nc.sbuf_tensor(f"{tag}m{i}", [128, 1], F32)) for i in range(NB)]
        vcol = [Eo(nc.sbuf_tensor(f"{tag}v{i}", [128, 1], F32)) for i in range(NB)]
        rcol = [Eo(nc.sbuf_tensor(f"{tag}r{i}", [128, 1], F32)) for i in range(NB)]
        epsc = Eo(nc.sbuf_tensor(f"{tag}eps", [128, 1], F32))
        ld = [p.dsem(f"ld{i}") for i in range(NB)]
        sto = [p.dsem(f"st{i}") for i in range(NB)]
        act = p.sem("act")
        dve = p.sem("dve")
        rstd = p.sem("rstd")
        app = p.sem("app")
        cst = p.sem("cst")
        p.op("vector", lambda En: En.memset(epsc[:], float(EPS)), inc=cst)
        p.wait("scalar", cst, cst.n)
        sq_post = {}
        app_post = {}
        for i in range(NT):
            b = i % NB
            if i >= NB:
                p.wait("sync", app, app_post[i - NB])
                p.wait("sync", act, sq_post[i - NB])
            p.dma("sync", xt[b][:], x_ap[i * 128:(i + 1) * 128, :], ld[b])
            p.wait("vector", ld[b], ld[b].n)
            p.op("vector", lambda En, b=b: En.reduce_sum(scol[b][:], xt[b][:], AX.X),
                 inc=dve)
            p.wait("vector", dve, dve.n)
            p.op("vector", lambda En, b=b: En.tensor_scalar(
                mcol[b][:], scol[b][:], 1.0 / D, None, ALU.mult), inc=dve)
            p.wait("scalar", ld[b], ld[b].n)
            ent = p.op("scalar", lambda En, b=b, i=i: En.activation(
                xsq[i % 2][:], xt[b][:], AF.Square, accum_out=qcol[b][:, 0:1]),
                inc=act)
            sq_post[i] = ent["post"]
            p.wait("vector", dve, dve.n)
            p.wait("vector", act, act.n)
            p.op("vector", lambda En, b=b: En.tensor_tensor(
                vcol[b][:], mcol[b][:], mcol[b][:], ALU.mult), inc=dve)
            p.wait("vector", dve, dve.n)
            p.op("vector", lambda En, b=b: En.scalar_tensor_tensor(
                vcol[b][:], qcol[b][:], 1.0 / D, vcol[b][:], ALU.mult,
                ALU.subtract), inc=dve)
            p.wait("scalar", dve, dve.n)
            p.op("scalar", lambda En, b=b: En.activation(
                rcol[b][:], vcol[b][:], AF.Ln, bias=epsc[:, 0:1]), inc=act)
            p.wait("scalar", act, act.n)
            p.op("scalar", lambda En, b=b: En.activation(
                rcol[b][:], rcol[b][:], AF.Exp, scale=-0.5), inc=rstd)
            p.wait("vector", rstd, rstd.n)
            p.wait("vector", sto[b], sto[b].n)
            p.op("vector", lambda En, b=b: En.tensor_scalar(
                xnt[b][:], xt[b][:], mcol[b][:, 0:1], rcol[b][:, 0:1],
                ALU.subtract, ALU.mult), inc=app)
            app_post[i] = app.n
            p.wait("scalar", app, app.n)
            p.dma("scalar", out_d[i * 128:(i + 1) * 128, :], xnt[b][:], sto[b])
        for b in range(NB):
            p.wait("scalar", sto[b], sto[b].n)
    p.barrier(tag, reclaim=True)


class AttnCfg:
    def __init__(self, **kw):
        self.tag = kw["tag"]
        self.src_d = kw["src_d"]
        self.out_d = kw["out_d"]
        self.out_f32 = kw.get("out_f32", False)
        self.resid_src = kw.get("resid_src")
        self.causal = kw["causal"]
        self.alibi = kw["alibi"]
        self.qw = kw["qw"]
        self.kw_ = kw["kw_"]
        self.vw = kw["vw"]
        self.latent = kw.get("latent")
        self.ow = kw["ow"]
        self.affine = kw.get("affine")
        self.v_d = kw.get("v_d")
        self.wait_cast = kw.get("wait_cast", [])


def _slope(h):
    return 2.0 ** (-0.5 * (h + 1))


def p_attn(p, cfg):
    nc = p.nc
    tag = cfg.tag
    KV_CH = 2 if cfg.latent else NCH
    NKT = NT
    NQC = T // 512
    with contextlib.ExitStack() as stk:
        Eo = stk.enter_context
        if not cfg.alibi:
            PT = [Eo(nc.sbuf_tensor(f"{tag}pt{i}", [128, 1024], BF16))
                  for i in range(2)]
            NPT = 2
        else:
            PT = [Eo(nc.sbuf_tensor(f"{tag}pt{i}", [128, 512], BF16))
                  for i in range(3)]
            NPT = 3
        wq = Eo(nc.sbuf_tensor(f"{tag}wq", [128, D], BF16))
        wk = Eo(nc.sbuf_tensor(f"{tag}wk", [128, KV_CH * 128], BF16))
        wv = Eo(nc.sbuf_tensor(f"{tag}wv", [128, KV_CH * 128], BF16))
        qsq = [Eo(nc.sbuf_tensor(f"{tag}sq{i}", [128, 512], BF16))
               for i in range(2)]
        zzb = Eo(nc.sbuf_tensor(f"{tag}zzb", [65, 512], F32))
        zb_sb = zzb
        zr = zzb
        ones_f = Eo(nc.sbuf_tensor(f"{tag}onef", [128, 128], F32))
        ones_b = Eo(nc.sbuf_tensor(f"{tag}oneb", [128, 128], BF16))
        ident_b = Eo(nc.sbuf_tensor(f"{tag}idb", [128, 128], BF16))
        mask0 = Eo(nc.sbuf_tensor(f"{tag}mk", [128, 128], F32))
        nC = [Eo(nc.sbuf_tensor(f"{tag}nC{i}", [128, 1], F32)) for i in range(2)]
        mq2 = Eo(nc.sbuf_tensor(f"{tag}mq", [1, 2], F32))
        cch = Eo(nc.sbuf_tensor(f"{tag}cc", [1, 2], F32))
        if cfg.alibi:
            negB = [Eo(nc.sbuf_tensor(f"{tag}nb{i}", [65, 512], BF16))
                    for i in range(2)]
            negBi0 = Eo(nc.sbuf_tensor(f"{tag}nbi", [65, 512], mybir.dt.int16))
            negBi = [negBi0, negBi0]
            kgl_i = Eo(nc.sbuf_tensor(f"{tag}kgi", [128, NT], I32))
            kgl = Eo(nc.sbuf_tensor(f"{tag}kgf", [128, NT], F32))
            bias_h = [[Eo(nc.sbuf_tensor(f"{tag}bh{i}{j}", [128, NT], F32))
                       for j in range(2)] for i in range(2)]
        gcol = [Eo(nc.sbuf_tensor(f"{tag}gc{i}", [128, 1], F32)) for i in range(NCH)]
        bcol = [Eo(nc.sbuf_tensor(f"{tag}bc{i}", [128, 1], F32)) for i in range(NCH)]
        ps_p0 = Eo(nc.psum_tensor(f"{tag}pp", [128, 512], F32))
        ps_p = [ps_p0, ps_p0]
        WIDE = not cfg.alibi
        if WIDE:
            NS = 2
            ps_s = [Eo(nc.psum_tensor(f"{tag}ps{i}", [128, 1024], F32))
                    for i in range(NS)]
        else:
            NS = 4
            ps_s = [Eo(nc.psum_tensor(f"{tag}ps{i}", [128, 512], F32))
                    for i in range(NS)]
        ps_y = [Eo(nc.psum_tensor(f"{tag}py{i}", [65, 512], F32)) for i in range(2)]
        aux = Eo(nc.psum_tensor(f"{tag}aux", [128, 512], F32))
        vstg = [Eo(nc.sbuf_tensor(f"{tag}vg{i}", [128, 128], BF16))
                for i in range(2)]
        vstg_free = [0, 0]
        st_v = p.dsem("stv")
        ld_vt = [p.dsem("lvt0"), p.dsem("lvt1")]
        ld_pan = p.dsem("ldp"); ld_af = p.dsem("ldaf"); ld_l = p.dsem("ldl")
        ld_w = [p.dsem("ldw0"), p.dsem("ldw1")]; po = p.sem("po")
        pe_p = p.sem("pep"); evq = p.sem("evq"); aff = p.sem("aff")
        pe_n = p.sem("pen"); dve_n = p.sem("dvn")
        act_c = p.sem("acc"); dve_c = p.sem("dvc")
        pe_t = p.sem("pet"); dve_t = p.sem("dvt")
        pe_s = p.sem("pes"); msk = p.sem("msk"); ex = p.sem("ex")
        pe_av = p.sem("pav"); dve_z = p.sem("dvz")
        pe_zb = p.sem("pzb"); act_zb = p.sem("azb"); dve_y = p.sem("dvy")

        for cs, cn in cfg.wait_cast:
            p.wait("sync", cs, cn)

        pp_free = [0, 0]
        negb_free = [0, 0]
        yq_slot = [(0, 0), (0, 0)]
        pt_free = [0, 0, 0]
        exp_post = {}
        sc_cnt = [0]

        p.op("vector", lambda En: En.memset(ones_f[:], 1.0), inc=dve_c)
        p.op("vector", lambda En: En.memset(ones_b[:], 1.0), inc=dve_c)
        p.op("vector", lambda En: En.memset(mask0[:], 0.0), inc=dve_c)
        p.op("vector", lambda En: En.memset(ident_b[:], 1.0), inc=dve_c)
        cst0 = dve_c.n
        p.wait("gpsimd", dve_c, cst0)
        p.op("gpsimd", lambda En: En.affine_select(
            ident_b[:], ident_b[:], [[-1, 128]], ALU.is_equal, 0.0,
            base=0, channel_multiplier=1), inc=po)
        ident_post = po.n
        mask_post = 0
        if cfg.causal:
            p.op("gpsimd", lambda En: En.affine_select(
                mask0[:], mask0[:], [[1, 128]], ALU.is_ge, float(NEG),
                base=0, channel_multiplier=-1), inc=po)
            mask_post = po.n
        kgl_post = 0
        if cfg.alibi:
            p.op("gpsimd", lambda En: En.iota(
                kgl_i[:], [[128, NT]], base=0, channel_multiplier=1), inc=po)
            p.wait("vector", po, po.n)
            p.op("vector", lambda En: En.tensor_copy(kgl[:], kgl_i[:]), inc=dve_c)
            kgl_post = dve_c.n
        aff_cols_post = 0
        if cfg.affine is not None:
            with contextlib.ExitStack() as stka:
                wrow = stka.enter_context(nc.sbuf_tensor(f"{tag}wr", [1, D], F32))
                brow = stka.enter_context(nc.sbuf_tensor(f"{tag}br", [1, D], F32))
                p.dma("sync", wrow[:], cfg.affine[0].rearrange("(o d) -> o d", o=1), ld_af)
                p.dma("sync", brow[:], cfg.affine[1].rearrange("(o d) -> o d", o=1), ld_af)
                p.wait("tensor", ld_af, ld_af.n)
                p.wait("tensor", dve_c, cst0)
                for ic in range(NCH):
                    for row, dstc in ((wrow, gcol[ic]), (brow, bcol[ic])):
                        p.wait("tensor", act_c, act_c.n)
                        p.op("tensor", lambda En, ic=ic, row=row: En.transpose(
                            aux[:, 0:1], row[0:1, ic * 128:(ic + 1) * 128],
                            ones_f[0:1, 0:1]), inc=pe_t)
                        p.wait("scalar", pe_t, pe_t.n)
                        p.op("scalar", lambda En, dstc=dstc: En.activation(
                            dstc[:], aux[:, 0:1], AF.Copy), inc=act_c)
                aff_cols_post = act_c.n
                p.wait("sync", act_c, aff_cols_post)

        pan = [Eo(nc.sbuf_tensor(f"{tag}pan{i}", [128, T], BF16)) for i in range(NCH)]
        Y = [Eo(nc.sbuf_tensor(f"{tag}Y{i}", [128, T], BF16)) for i in range(NCH)]
        qsb = [Eo(nc.sbuf_tensor(f"{tag}q{i}", [128, T], BF16)) for i in range(2)]
        ksb = [Eo(nc.sbuf_tensor(f"{tag}k{i}", [128, T], BF16)) for i in range(2)]
        vsb = Eo(nc.sbuf_tensor(f"{tag}v", [128, T], BF16))
        v_tm = [Eo(nc.sbuf_tensor(f"{tag}vt{i}", [128, 2 * NT * 65], BF16))
                for i in range(2)]
        for ic in range(NCH):
            p.dma("sync", pan[ic][:], cfg.src_d[:, ic * 128:(ic + 1) * 128],
                  ld_pan, transpose=True)
        pan_total = ld_pan.n
        if cfg.affine is not None:
            for ic in range(NCH):
                p.wait("vector", ld_pan, pan_total)
                p.wait("vector", act_c, aff_cols_post)
                p.op("vector", lambda En, ic=ic: En.tensor_scalar(
                    pan[ic][:], pan[ic][:], gcol[ic][:, 0:1], bcol[ic][:, 0:1],
                    ALU.mult, ALU.add), inc=aff)
            pan_wait = (aff, aff.n)
        else:
            pan_wait = (ld_pan, pan_total)

        lat_tiles = None
        lat_ld = None
        lat_free = [0, 0]
        if cfg.latent:
            lat_d = cfg.latent["lat_d"]
            lat_tiles = [[Eo(nc.sbuf_tensor(f"{tag}lt{s}{i}", [128, 512], BF16))
                          for i in range(2)] for s in range(2)]
            lat_ld = [p.dsem("lls0"), p.dsem("lls1")]
            wl = wq
            lst = qsq[0]
            stl = p.dsem("stl")
            for ocl in range(2):
                if ocl > 0:
                    p.wait("sync", pe_p, pe_p.n)
                p.dma("sync", wl[:].rearrange("p (ic o) -> p ic o", o=128),
                      cfg.latent["dw"](ocl), ld_l)
                p.wait("tensor", pan_wait[0], pan_wait[1])
                p.wait("tensor", ld_l, ld_l.n)
                for tc in range(NQC):
                    pb = 0
                    p.wait("tensor", evq, pp_free[pb])
                    for ic in range(NCH):
                        p.op("tensor", lambda En, pb=pb, ic=ic, tc=tc: En.matmul(
                            ps_p[pb][:], wl[:, ic * 128:(ic + 1) * 128],
                            pan[ic][:, tc * 512:(tc + 1) * 512],
                            start=(ic == 0), stop=(ic == NCH - 1)), inc=pe_p)
                    p.wait("vector", pe_p, pe_p.n)
                    p.wait("vector", stl, stl.n)
                    p.op("vector", lambda En, pb=pb: En.tensor_copy(
                        lst[:], ps_p[pb][:]), inc=evq)
                    pp_free[pb] = evq.n
                    p.wait("scalar", evq, evq.n)
                    p.dma("scalar", lat_d[ocl * 128:(ocl + 1) * 128,
                                          tc * 512:(tc + 1) * 512], lst[:], stl)
            p.wait("sync", stl, stl.n)
            lat_store_post = stl.n
        kv_pan = pan

        state = {}

        def make_prep(oc, buf, prev_jobs_ex):
            fl = []
            p.wait("sync", pe_p, pe_p.n)
            p.wait("sync", ex, prev_jobs_ex)
            p.dma("sync", wq[:].rearrange("p (ic o) -> p ic o", o=128),
                  cfg.qw(oc), ld_w[0])
            if cfg.latent:
                p.dma("sync", wk[:].rearrange("p (ic o) -> p ic o", o=128),
                      cfg.latent["ku"](oc), ld_w[0])
                p.dma("sync", wv[:].rearrange("p (ic o) -> p ic o", o=128),
                      cfg.latent["vu"](oc), ld_w[0])
            else:
                p.dma("sync", wk[:].rearrange("p (ic o) -> p ic o", o=128),
                      cfg.kw_(oc), ld_w[0])
                p.dma("sync", wv[:].rearrange("p (ic o) -> p ic o", o=128),
                      cfg.vw(oc), ld_w[0])
            ldw_post = ld_w[0].n
            for which, dst, wt, src_pan, nkc, scale in (
                    ("q", qsb[buf], wq, pan, NCH, 0.125),
                    ("k", ksb[buf], wk, kv_pan, KV_CH, 1.0),
                    ("v", vsb, wv, kv_pan, KV_CH, 1.0)):
                for tc in range(NQC):
                    def proj(which=which, dst=dst, wt=wt, src_pan=src_pan,
                             nkc=nkc, scale=scale, tc=tc, ldw_post=ldw_post,
                             prev_jobs_ex=prev_jobs_ex):
                        pb = 0
                        lat_mode = cfg.latent and which != "q"
                        if lat_mode:
                            ls = tc % 2
                            p.wait("sync", pe_p, lat_free[ls])
                            for icl in range(2):
                                p.dma("sync", lat_tiles[ls][icl][:],
                                      cfg.latent["lat_d"][
                                          icl * 128:(icl + 1) * 128,
                                          tc * 512:(tc + 1) * 512], lat_ld[ls])
                            llpost = lat_ld[ls].n
                        p.wait("tensor", pan_wait[0], pan_wait[1])
                        p.wait("tensor", ld_w[0], ldw_post)
                        p.wait("tensor", ex, prev_jobs_ex)
                        if which == "v":
                            p.wait("tensor", st_v, state.get("v_st", 0))
                        p.wait("tensor", evq, pp_free[pb])
                        if lat_mode:
                            p.wait("tensor", lat_ld[ls], llpost)
                        for ic in range(nkc):
                            srcap = (lat_tiles[ls][ic][:, 0:512] if lat_mode
                                     else src_pan[ic][:, tc * 512:(tc + 1) * 512])
                            p.op("tensor", lambda En, pb=pb, ic=ic, wt=wt,
                                 nkc=nkc, srcap=srcap: En.matmul(
                                     ps_p[pb][:], wt[:, ic * 128:(ic + 1) * 128],
                                     srcap,
                                     start=(ic == 0), stop=(ic == nkc - 1)),
                                 inc=pe_p)
                        if lat_mode:
                            lat_free[ls] = pe_p.n
                        p.wait("vector", pe_p, pe_p.n)
                        if scale == 1.0:
                            p.op("vector", lambda En, pb=pb, dst=dst, tc=tc:
                                 En.tensor_copy(dst[:, tc * 512:(tc + 1) * 512],
                                                ps_p[pb][:]), inc=evq)
                        else:
                            p.op("vector", lambda En, pb=pb, dst=dst, tc=tc,
                                 scale=scale: En.tensor_scalar(
                                     dst[:, tc * 512:(tc + 1) * 512],
                                     ps_p[pb][:], scale, None, ALU.mult),
                                 inc=evq)
                        pp_free[pb] = evq.n
                    fl.append(proj)

            def wfree_mark():
                state[("evq_all", buf)] = evq.n
            fl.append(wfree_mark)

            nl = []

            def norm_init():
                p.wait("vector", evq, state[("evq_all", buf)])
                p.wait("vector", dve_n, dve_n.n)
                p.op("vector", lambda En: En.memset(mq2[:], 1e-20), inc=dve_n)
            nl.append(norm_init)
            ns_free = {}
            nslots = NS

            def nslot(i):
                return ps_s[i % nslots][0:1, 0:512]

            for wi, (which, srcf, col) in enumerate(
                    (("q", qsb, 0), ("k", ksb, 1))):
                for tc in range(NQC):
                    def norm(which=which, srcf=srcf, col=col, tc=tc, buf=buf,
                             idx2=None, wi=wi):
                        src = srcf[buf]
                        i2 = wi * NQC + tc
                        sb2 = i2 % 2
                        p.wait("scalar", evq, state[("evq_all", buf)])
                        p.wait("scalar", pe_n, ns_free.get(("sq", sb2), 0))
                        p.op("scalar", lambda En, sb2=sb2, src=src, tc=tc:
                             En.activation(qsq[sb2][:],
                                           src[:, tc * 512:(tc + 1) * 512],
                                           AF.Square), inc=act_c)
                        p.wait("tensor", act_c, act_c.n)
                        p.wait("tensor", dve_c, cst0)
                        p.wait("tensor", ex, ex.n)
                        p.wait("tensor", dve_n, ns_free.get(("ps", i2 % nslots), 0))
                        p.op("tensor", lambda En, i2=i2, sb2=sb2: En.matmul(
                            nslot(i2), ones_b[:, 0:1], qsq[sb2][:],
                            start=True, stop=True), inc=pe_n)
                        ns_free[("sq", sb2)] = pe_n.n
                        p.wait("vector", pe_n, pe_n.n)
                        p.op("vector", lambda En, i2=i2: En.tensor_reduce(
                            cch[0:1, 0:1], nslot(i2), AX.X, ALU.max),
                            inc=dve_n)
                        ns_free[("ps", i2 % nslots)] = dve_n.n
                        p.wait("vector", dve_n, dve_n.n)
                        p.op("vector", lambda En, col=col: En.tensor_max(
                            mq2[0:1, col:col + 1], mq2[0:1, col:col + 1],
                            cch[0:1, 0:1]), inc=dve_n)
                        p.wait("vector", dve_n, dve_n.n)
                    nl.append(norm)

            def cchain(buf=buf, oc=oc):
                p.op("vector", lambda En: En.tensor_tensor(
                    cch[0:1, 0:1], mq2[0:1, 0:1], mq2[0:1, 1:2], ALU.mult),
                    inc=dve_n)
                p.wait("scalar", dve_n, dve_n.n)
                p.op("scalar", lambda En: En.activation(
                    cch[0:1, 1:2], cch[0:1, 0:1], AF.Sqrt,
                    scale=(1.01 ** 2) / 64.0), inc=act_c)
                p.wait("vector", act_c, act_c.n)
                p.op("vector", lambda En: En.tensor_scalar(
                    cch[0:1, 0:1], cch[0:1, 1:2], -1.0, -2.0, ALU.mult,
                    ALU.add), inc=dve_n)
                p.wait("tensor", dve_n, dve_n.n)
                p.wait("tensor", act_zb, act_zb.n)
                p.wait("tensor", ex, state.get(("nC_used", buf), 0))
                p.op("tensor", lambda En: En.matmul(
                    aux[:, 0:1], ones_f[0:1, :], cch[0:1, 0:1],
                    start=True, stop=True), inc=pe_zb)
                p.wait("scalar", pe_zb, pe_zb.n)
                p.op("scalar", lambda En, buf=buf: En.activation(
                    nC[buf][:], aux[:, 0:1], AF.Copy), inc=act_zb)
                state[("nC", buf)] = act_zb.n
                if cfg.alibi:
                    for h2 in range(2):
                        h = oc * 2 + h2
                        p.wait("vector", act_zb, state[("nC", buf)])
                        p.wait("vector", dve_c, kgl_post)
                        p.wait("vector", ex, state.get(("nC_used", buf), 0))
                        p.op("vector", lambda En, h2=h2, h=h, buf=buf:
                             En.tensor_scalar(
                                 bias_h[h2][buf][:], kgl[:], _slope(h),
                                 nC[buf][:, 0:1], ALU.mult, ALU.add), inc=dve_c)
                    state[("bias", buf)] = dve_c.n
            nl.append(cchain)

            def vt_init(buf=buf):
                p.wait("vector", evq, state[("evq_all", buf)])
                p.wait("vector", pe_av, state.get(("vtm_used", buf), 0))
                p.op("vector", lambda En, buf=buf: En.memset(v_tm[buf][:], 1.0),
                     inc=dve_t)
                state[("vtm_ms", buf)] = dve_t.n
                p.wait("scalar", evq, state[("evq_all", buf)])
                p.dma("scalar", cfg.v_d, vsb[:], st_v)
                state["v_st"] = st_v.n
            fl.append(vt_init)
            for kt in range(NKT):
                def vt(kt=kt, buf=buf):
                    vs = kt % 2
                    p.wait("sync", st_v, state["v_st"])
                    p.wait("sync", dve_t, vstg_free[vs])
                    p.dma("sync", vstg[vs][:],
                          cfg.v_d[:, kt * 128:(kt + 1) * 128], ld_vt[vs],
                          transpose=True)
                    p.wait("vector", ld_vt[vs], ld_vt[vs].n)
                    p.wait("vector", dve_t, state[("vtm_ms", buf)])
                    for h2 in range(2):
                        p.op("vector", lambda En, h2=h2, kt=kt, buf=buf, vs=vs:
                             En.tensor_copy(
                                 v_tm[buf][:, (h2 * NKT + kt) * 65:
                                           (h2 * NKT + kt) * 65 + 64],
                                 vstg[vs][:, h2 * 64:(h2 + 1) * 64]),
                             inc=dve_t)
                    vstg_free[vs] = dve_t.n
                fl.append(vt)

            def vt_done(buf=buf):
                state[("vtm", buf)] = dve_t.n
            fl.append(vt_done)
            return fl, nl

        def run_jobs(oc, buf, fillers):
            units = []
            for h2 in range(2):
                for qc in range(NQC):
                    qb = qc * 512
                    if cfg.causal:
                        kts = []
                        for kt in range(NKT):
                            if kt * 128 > qb + 384:
                                break
                            kts.append((kt, max(0, (kt * 128 - qb) // 128)))
                    else:
                        kts = [(kt, 0) for kt in range(NKT)]
                    i0 = 0
                    while i0 < len(kts):
                        if (WIDE and i0 + 1 < len(kts)
                                and kts[i0][1] == 0 and kts[i0 + 1][1] == 0):
                            units.append((h2, qc, kts[i0:i0 + 2], i0, len(kts)))
                            i0 += 2
                        else:
                            units.append((h2, qc, kts[i0:i0 + 1], i0, len(kts)))
                            i0 += 1
            LAG = 2
            deferred = []
            negb_built = {}
            nu = len(units)
            nf = len(fillers)
            fill_at = {}
            if nf:
                stepf = max(1, nu // nf)
                for i in range(nf):
                    fill_at.setdefault(min(4 + i * stepf, nu - 1), []).append(
                        fillers[i])

            def emit_S(u):
                h2, qc, kts, i0, nkts = units[u]
                h = oc * 2 + h2
                qb = qc * 512
                prow = h2 * 64
                hs = slice(h2 * 64, (h2 + 1) * 64)
                c = sc_cnt[0]
                sc_cnt[0] += 1
                sb = c % NS
                nbb = None
                if cfg.alibi:
                    nbb = (h2 * NQC + qc) % 2
                    if i0 == 0:
                        p.wait("gpsimd", pe_s, negb_free[nbb])
                        p.wait("gpsimd", act_c, act_c.n)
                        p.op("gpsimd", lambda En, nbb=nbb, qb=qb, prow=prow:
                             En.iota(negBi[nbb][prow:prow + 1, :], [[1, 512]],
                                     base=qb, channel_multiplier=0), inc=po)
                        p.wait("scalar", po, po.n)
                        p.op("scalar", lambda En, nbb=nbb, h=h, prow=prow:
                             En.activation(negB[nbb][prow:prow + 1, :],
                                           negBi[nbb][prow:prow + 1, :],
                                           AF.Copy, scale=-_slope(h)), inc=act_c)
                        negb_built[(h2, qc)] = act_c.n
                if c >= NS:
                    p.wait("tensor", ex, exp_post[c - NS])
                if cfg.alibi:
                    p.wait("tensor", act_c, negb_built[(h2, qc)])
                msk_post = None
                for i, (kt, jj) in enumerate(kts):
                    qsl = slice(qb + jj * 128, qb + 512)
                    psl = slice(512 * i + jj * 128, 512 * (i + 1))
                    p.op("tensor", lambda En, sb=sb, hs=hs, kt=kt, qsl=qsl,
                         psl=psl, buf=buf: En.matmul(
                             ps_s[sb][:, psl],
                             ksb[buf][hs, kt * 128:(kt + 1) * 128],
                             qsb[buf][hs, qsl], start=True,
                             stop=not cfg.alibi), inc=pe_s)
                    if cfg.alibi:
                        p.op("tensor", lambda En, sb=sb, psl=psl, prow=prow,
                             nbb=nbb: En.matmul(
                                 ps_s[sb][:, psl], ones_b[prow:prow + 1, :],
                                 negB[nbb][prow:prow + 1,
                                           slice(jj * 128, 512)],
                                 start=False, stop=True), inc=pe_s)
                        negb_free[nbb] = pe_s.n
                mm_post = pe_s.n
                partial_any = False
                for i, (kt, jj) in enumerate(kts):
                    if cfg.causal and kt * 128 >= qb:
                        if not partial_any:
                            p.wait("vector", pe_s, mm_post)
                            p.wait("vector", po, mask_post)
                            partial_any = True
                        p.op("vector", lambda En, sb=sb, i=i, jj=jj:
                             En.tensor_tensor(
                                 ps_s[sb][:, 512 * i + jj * 128:
                                          512 * i + jj * 128 + 128],
                                 ps_s[sb][:, 512 * i + jj * 128:
                                          512 * i + jj * 128 + 128],
                                 mask0[:], ALU.add), inc=msk)
                if partial_any:
                    msk_post = msk.n
                ptb = c % NPT
                p.wait("scalar", pe_av, pt_free[ptb])
                if partial_any:
                    p.wait("scalar", msk, msk_post)
                else:
                    p.wait("scalar", pe_s, mm_post)
                jj0 = kts[0][1]
                espan = slice(jj0 * 128, 512 * len(kts))
                if cfg.alibi:
                    ba = bias_h[h2][buf][:, kts[0][0]:kts[0][0] + 1]
                else:
                    ba = nC[buf][:, 0:1]
                p.op("scalar", lambda En, ptb=ptb, sb=sb, espan=espan, ba=ba:
                     En.activation(PT[ptb][:, espan], ps_s[sb][:, espan],
                                   AF.Exp, bias=ba), inc=ex)
                exp_post[c] = ex.n
                return c

            def emit_AV(u, c):
                h2, qc, kts, i0, nkts = units[u]
                yb = (h2 * NQC + qc) % 2
                qb = qc * 512
                ptb = c % NPT
                p.wait("tensor", ex, exp_post[c])
                if i0 == 0:
                    p.wait("tensor", dve_t, state[("vtm", buf)])
                    p.wait("tensor", dve_z, yq_slot[yb][0])
                    p.wait("tensor", dve_y, yq_slot[yb][1])
                for i, (kt, jj) in enumerate(kts):
                    psl = slice(512 * i + jj * 128, 512 * (i + 1))
                    ysl = slice(jj * 128, 512)
                    p.op("tensor", lambda En, yb=yb, ptb=ptb, psl=psl, ysl=ysl,
                         h2=h2, kt=kt, first=(i0 == 0 and i == 0),
                         last=(i0 + i == nkts - 1), buf=buf: En.matmul(
                             ps_y[yb][:, ysl],
                             v_tm[buf][:, (h2 * NKT + kt) * 65:
                                       (h2 * NKT + kt) * 65 + 65],
                             PT[ptb][:, psl], start=first,
                             stop=last), inc=pe_av)
                pt_free[ptb] = pe_av.n
                if i0 + len(kts) == nkts:
                    for d in list(deferred):
                        d[1]()
                        deferred.remove(d)
                    av_post = pe_av.n
                    p.wait("vector", pe_av, av_post)
                    p.wait("vector", pe_zb, pe_zb.n)
                    p.op("vector", lambda En, yb=yb: En.tensor_scalar_max(
                        zr[64:65, :], ps_y[yb][64:65, :], 1e-35), inc=dve_z)
                    p.wait("scalar", dve_z, dve_z.n)
                    p.op("scalar", lambda En: En.activation(
                        zr[64:65, :], zr[64:65, :], AF.Ln), inc=act_zb)
                    p.wait("scalar", act_zb, act_zb.n)
                    p.op("scalar", lambda En: En.activation(
                        zr[64:65, :], zr[64:65, :], AF.Exp, scale=-1.0),
                        inc=dve_z)
                    z_post = dve_z.n

                    def stage2(z_post=z_post, yb=yb, oc=oc, h2=h2, qb=qb):
                        hs = slice(h2 * 64, (h2 + 1) * 64)
                        p.wait("tensor", dve_z, z_post)
                        p.wait("tensor", act_zb, act_zb.n)
                        p.op("tensor", lambda En: En.matmul(
                            aux[0:64, :], ones_f[64:65, 0:64], zr[64:65, :],
                            start=True, stop=True), inc=pe_zb)
                        p.wait("vector", pe_zb, pe_zb.n)
                        p.wait("vector", dve_y, dve_y.n)
                        p.op("vector", lambda En: En.tensor_copy(
                            zb_sb[0:64, :], aux[0:64, :]), inc=act_zb)
                        p.wait("vector", act_zb, act_zb.n)
                        p.op("vector", lambda En, yb=yb, oc=oc, hs=hs, qb=qb:
                             En.tensor_tensor(
                                 Y[oc][hs, qb:qb + 512], ps_y[yb][0:64, :],
                                 zb_sb[0:64, :], ALU.mult), inc=dve_y)
                        yq_slot[yb] = (dve_z.n, dve_y.n)
                    deferred.append([u + 6, stage2])

            p.wait("scalar", act_zb, state[("nC", buf)])
            if cfg.alibi:
                p.wait("scalar", dve_c, state[("bias", buf)])
            cq = []
            for u in range(nu):
                if u >= LAG:
                    emit_AV(u - LAG, cq[u - LAG])
                cq.append(emit_S(u))
                for f in fill_at.get(u, ()):
                    f()
                for d in list(deferred):
                    if d[0] <= u:
                        d[1]()
                        deferred.remove(d)
            for u in range(nu - LAG, nu):
                if u >= 0:
                    emit_AV(u, cq[u])
            for d in deferred:
                d[1]()
            state[("nC_used", buf)] = ex.n
            state[("vtm_used", buf)] = pe_av.n
            return ex.n

        prev_ex = {0: 0, 1: 0}
        fillers, norms0 = make_prep(0, 0, 0)
        for f in fillers:
            f()
        for f in norms0:
            f()
        p.wait("tensor", dve_n, dve_n.n)
        for oc in range(NCH):
            buf = oc % 2
            if oc + 1 < NCH:
                nxt, nxt_norms = make_prep(oc + 1, 1 - buf, prev_ex[1 - buf])
            else:
                nxt, nxt_norms = [], []
            prev_ex[buf] = run_jobs(oc, buf, nxt)
            for f in nxt_norms:
                f()
            if nxt_norms:
                p.wait("tensor", dve_n, dve_n.n)

        y_done = dve_y.n
        with contextlib.ExitStack() as stk2:
            Eo2 = stk2.enter_context
            res_pan = None
            if cfg.resid_src is not None:
                rp0 = Eo2(nc.sbuf_tensor(f"{tag}rp", [128, 512], BF16))
                res_pan = [rp0, rp0]
            wo = Eo2(nc.sbuf_tensor(f"{tag}wo", [128, D], BF16))
            o_sb0 = Eo2(nc.sbuf_tensor(f"{tag}os", [128, 512],
                                       F32 if cfg.out_f32 else BF16))
            o_sb = [o_sb0, o_sb0]
            ld_r = [p.dsem("ldr0"), p.dsem("ldr1")]
            sto_s = [p.dsem("sts0"), p.dsem("sts1")]
            ld_wo = [p.dsem("lwo0"), p.dsem("lwo1")]
            evo = p.sem("evo")
            wo_free = [0, 0]
            rp_free = [0, 0]
            op_free = [0, 0]
            for oc in range(NCH):
                wb = 0
                p.wait("sync", pe_p, wo_free[0])
                p.dma("sync", wo[:].rearrange("p (ic o) -> p ic o", o=128),
                      cfg.ow(oc), ld_wo[0])
                ldo_post = ld_wo[0].n
                p.wait("tensor", ld_wo[0], ldo_post)
                p.wait("tensor", dve_y, y_done)
                for tc in range(NQC):
                    pb = 0
                    rb = tc % 2
                    if cfg.resid_src is not None:
                        p.wait("sync", evo, rp_free[0])
                        p.wait("sync", evo, rp_free[1])
                        p.dma("sync", res_pan[rb][:],
                              cfg.resid_src[tc * 512:(tc + 1) * 512,
                                            oc * 128:(oc + 1) * 128],
                              ld_r[rb], transpose=True)
                    p.wait("tensor", evq, pp_free[pb])
                    p.wait("tensor", evo, op_free[pb])
                    for ic in range(NCH):
                        p.op("tensor", lambda En, pb=pb, ic=ic, tc=tc, wb=wb:
                             En.matmul(
                                 ps_p[pb][:], wo[:, ic * 128:(ic + 1) * 128],
                                 Y[ic][:, tc * 512:(tc + 1) * 512],
                                 start=(ic == 0), stop=(ic == NCH - 1)), inc=pe_p)
                    if cfg.resid_src is not None:
                        p.wait("vector", pe_p, pe_p.n)
                        p.wait("vector", ld_r[rb], ld_r[rb].n)
                        p.wait("vector", sto_s[0], sto_s[0].n)
                        p.wait("vector", sto_s[1], sto_s[1].n)
                        p.op("vector", lambda En, pb=pb, rb=rb: En.tensor_tensor(
                            o_sb[pb][:], ps_p[pb][:], res_pan[rb][:], ALU.add),
                            inc=evo)
                        rp_free[rb] = evo.n
                        op_free[pb] = evo.n
                        p.wait("scalar", evo, evo.n)
                    else:
                        p.wait("scalar", pe_p, pe_p.n)
                        p.wait("scalar", sto_s[0], sto_s[0].n)
                        p.wait("scalar", sto_s[1], sto_s[1].n)
                        p.op("scalar", lambda En, pb=pb: En.activation(
                            o_sb[pb][:], ps_p[pb][:], AF.Copy), inc=evo)
                        op_free[pb] = evo.n
                        p.wait("scalar", evo, evo.n)
                    p.dma("scalar", cfg.out_d[oc * 128:(oc + 1) * 128,
                                              tc * 512:(tc + 1) * 512],
                          o_sb[pb][:], sto_s[pb])
                wo_free[0] = pe_p.n
            p.wait("scalar", sto_s[0], sto_s[0].n)
            p.wait("scalar", sto_s[1], sto_s[1].n)
    p.barrier(tag, reclaim=True)


def p_ln2_gate(p, y1_d, xn3_d, r_d, lnw_ap, lnb_ap, gw_ap, gb_ap,
               cast_gw, tag="p5"):
    nc = p.nc
    with contextlib.ExitStack() as stk:
        Eo = stk.enter_context
        yt = [[Eo(nc.sbuf_tensor(f"{tag}y{r}_{i}", [128, 512], F32))
               for i in range(NCH)] for r in range(2)]
        ysq = [Eo(nc.sbuf_tensor(f"{tag}sq{i}", [128, 512], F32)) for i in range(2)]
        t1 = Eo(nc.sbuf_tensor(f"{tag}t1", [128, 512], F32))
        rstd_b = Eo(nc.sbuf_tensor(f"{tag}rb", [128, 512], F32))
        x3 = [Eo(nc.sbuf_tensor(f"{tag}x3{i}", [128, 512], BF16)) for i in range(2)]
        rows = Eo(nc.sbuf_tensor(f"{tag}rw", [1, 512 * 4], F32))
        epsc = Eo(nc.sbuf_tensor(f"{tag}eps", [1, 1], F32))
        onef = Eo(nc.sbuf_tensor(f"{tag}onef", [128, 128], F32))
        oneb = Eo(nc.sbuf_tensor(f"{tag}oneb", [128, 128], BF16))
        identb = Eo(nc.sbuf_tensor(f"{tag}idb", [128, 128], BF16))
        gcol = [Eo(nc.sbuf_tensor(f"{tag}gc{i}", [128, 1], F32)) for i in range(NCH)]
        bcol = [Eo(nc.sbuf_tensor(f"{tag}bc{i}", [128, 1], F32)) for i in range(NCH)]
        wrow = Eo(nc.sbuf_tensor(f"{tag}wr", [1, D], F32))
        brow = Eo(nc.sbuf_tensor(f"{tag}br", [1, D], F32))
        ps_a = Eo(nc.psum_tensor(f"{tag}pa", [1, 512], F32))
        ps_b = Eo(nc.psum_tensor(f"{tag}pb", [1, 512], F32))
        ps_c = Eo(nc.psum_tensor(f"{tag}pc", [128, 512], F32))
        ps_d = Eo(nc.psum_tensor(f"{tag}pd", [128, 512], F32))
        ld_af = p.dsem("ldaf"); ld_y = [p.dsem("ldy0"), p.dsem("ldy1")]
        gld = p.dsem("gld")
        act = p.sem("act"); dve = p.sem("dve")
        pe = p.sem("pe"); po = p.sem("po"); sto = p.dsem("sto")
        sto_x = [p.dsem("stx0"), p.dsem("stx1")]
        evr = p.sem("evr")
        p.op("vector", lambda En: En.memset(epsc[:], float(EPS)), inc=dve)
        p.op("vector", lambda En: En.memset(onef[:], 1.0), inc=dve)
        p.op("vector", lambda En: En.memset(oneb[:], 1.0), inc=dve)
        p.op("vector", lambda En: En.memset(identb[:], 1.0), inc=dve)
        cst0 = dve.n
        p.wait("gpsimd", dve, cst0)
        p.op("gpsimd", lambda En: En.affine_select(
            identb[:], identb[:], [[-1, 128]], ALU.is_equal, 0.0,
            base=0, channel_multiplier=1), inc=po)
        ident_post = po.n
        p.dma("sync", wrow[:], lnw_ap.rearrange("(o d) -> o d", o=1), ld_af)
        p.dma("sync", brow[:], lnb_ap.rearrange("(o d) -> o d", o=1), ld_af)
        p.wait("tensor", ld_af, ld_af.n)
        p.wait("tensor", dve, cst0)
        for ic in range(NCH):
            for row, dstc in ((wrow, gcol[ic]), (brow, bcol[ic])):
                p.wait("tensor", act, act.n)
                p.op("tensor", lambda En, ic=ic, row=row: En.transpose(
                    ps_c[:, 0:1], row[0:1, ic * 128:(ic + 1) * 128],
                    onef[0:1, 0:1]), inc=pe)
                p.wait("scalar", pe, pe.n)
                p.op("scalar", lambda En, dstc=dstc: En.activation(
                    dstc[:], ps_c[:, 0:1], AF.Copy), inc=act)
        affc_post = act.n
        yt_free = [(0, 0, 0), (0, 0, 0)]
        for tc in range(T // 512):
            r = tc % 2
            p.wait("sync", pe, yt_free[r][0])
            p.wait("sync", act, yt_free[r][1])
            p.wait("sync", dve, yt_free[r][2])
            for ic in range(NCH):
                p.dma("sync", yt[r][ic][:],
                      y1_d[ic * 128:(ic + 1) * 128, tc * 512:(tc + 1) * 512],
                      ld_y[r])
            ld_post = ld_y[r].n
            for ic in range(NCH):
                p.wait("tensor", ld_y[r], ld_post)
                if ic == 0:
                    p.wait("tensor", dve, dve.n)
                p.op("tensor", lambda En, r=r, ic=ic: En.matmul(
                    ps_a[:], onef[:, 0:1], yt[r][ic][:],
                    start=(ic == 0), stop=(ic == NCH - 1)), inc=pe)
            for ic in range(NCH):
                sb2 = ic % 2
                p.wait("scalar", ld_y[r], ld_post)
                p.wait("scalar", pe, pe.n)
                p.op("scalar", lambda En, r=r, ic=ic, sb2=sb2: En.activation(
                    ysq[sb2][:], yt[r][ic][:], AF.Square), inc=act)
                p.wait("tensor", act, act.n)
                p.op("tensor", lambda En, sb2=sb2, ic=ic: En.matmul(
                    ps_b[:], onef[:, 0:1], ysq[sb2][:],
                    start=(ic == 0), stop=(ic == NCH - 1)), inc=pe)
            p.wait("vector", pe, pe.n)
            p.op("vector", lambda En: En.tensor_scalar(
                rows[0:1, 0:512], ps_a[:], 1.0 / D, None, ALU.mult), inc=dve)
            p.wait("vector", dve, dve.n)
            p.op("vector", lambda En: En.tensor_tensor(
                rows[0:1, 1536:2048], rows[0:1, 0:512], rows[0:1, 0:512],
                ALU.mult), inc=dve)
            p.wait("vector", dve, dve.n)
            p.op("vector", lambda En: En.scalar_tensor_tensor(
                rows[0:1, 512:1024], ps_b[:], 1.0 / D, rows[0:1, 1536:2048],
                ALU.mult, ALU.subtract), inc=dve)
            p.wait("scalar", dve, dve.n)
            p.op("scalar", lambda En: En.activation(
                rows[0:1, 1024:1536], rows[0:1, 512:1024], AF.Ln,
                bias=epsc[0:1, 0:1]), inc=act)
            p.wait("scalar", act, act.n)
            p.op("scalar", lambda En: En.activation(
                rows[0:1, 1024:1536], rows[0:1, 1024:1536], AF.Exp,
                scale=-0.5), inc=act)
            rstd_post = act.n
            p.wait("tensor", act, rstd_post)
            p.wait("tensor", dve, dve.n)
            p.op("tensor", lambda En: En.matmul(
                ps_c[:], onef[0:1, :], rows[0:1, 0:512], start=True, stop=True),
                inc=pe)
            p.op("tensor", lambda En: En.matmul(
                ps_d[:], onef[0:1, :], rows[0:1, 1024:1536], start=True,
                stop=True), inc=pe)
            bc_post = pe.n
            p.wait("scalar", bc_post if False else pe, bc_post)
            p.op("scalar", lambda En: En.activation(
                rstd_b[:], ps_d[:], AF.Copy), inc=act)
            rb_post = act.n
            for ic in range(NCH):
                xb = ic % 2
                p.wait("vector", dve, dve.n)
                p.wait("vector", act, rb_post)
                p.wait("vector", pe, bc_post)
                p.wait("vector", act, affc_post)
                p.wait("vector", sto_x[xb], sto_x[xb].n)
                p.op("vector", lambda En, r=r, ic=ic: En.tensor_tensor(
                    t1[:], yt[r][ic][:], ps_c[:], ALU.subtract), inc=dve)
                p.wait("vector", dve, dve.n)
                p.op("vector", lambda En: En.tensor_tensor(
                    t1[:], t1[:], rstd_b[:], ALU.mult), inc=dve)
                p.wait("vector", dve, dve.n)
                p.op("vector", lambda En, ic=ic, xb=xb: En.tensor_scalar(
                    x3[xb][:], t1[:], gcol[ic][:, 0:1], bcol[ic][:, 0:1],
                    ALU.mult, ALU.add), inc=dve)
                p.wait("scalar", dve, dve.n)
                p.dma("scalar", xn3_d[ic * 128:(ic + 1) * 128,
                                      tc * 512:(tc + 1) * 512], x3[xb][:],
                      sto_x[xb])
            yt_free[r] = (pe.n, act.n, dve.n)
        p.wait("scalar", sto_x[0], sto_x[0].n)
        p.wait("scalar", sto_x[1], sto_x[1].n)
        with contextlib.ExitStack() as stk2:
            Eo2 = stk2.enter_context
            x3p = [Eo2(nc.sbuf_tensor(f"{tag}xp{i}", [128, T], BF16))
                   for i in range(NCH)]
            xtf = Eo2(nc.sbuf_tensor(f"{tag}xtf", [128, 128], F32))
            gwt = Eo2(nc.sbuf_tensor(f"{tag}gw", [128, NCH * E], F32))
            gbr = Eo2(nc.sbuf_tensor(f"{tag}gb", [1, E], F32))
            io8i = Eo2(nc.sbuf_tensor(f"{tag}i8i", [128, E], I32))
            io8 = Eo2(nc.sbuf_tensor(f"{tag}i8", [128, E], F32))
            lg = Eo2(nc.sbuf_tensor(f"{tag}lg", [128, E], F32))
            mx = Eo2(nc.sbuf_tensor(f"{tag}mx", [128, 1], F32))
            eq = Eo2(nc.sbuf_tensor(f"{tag}eq", [128, E], F32))
            sel = Eo2(nc.sbuf_tensor(f"{tag}sel", [128, E], F32))
            rcl = Eo2(nc.sbuf_tensor(f"{tag}rcl", [128, 1], F32))
            rclb = Eo2(nc.sbuf_tensor(f"{tag}rcb", [128, 1], BF16))
            rrow = Eo2(nc.sbuf_tensor(f"{tag}rr", [1, T], BF16))
            ps_l = [Eo2(nc.psum_tensor(f"{tag}pl{i}", [128, 512], F32))
                    for i in range(2)]
            ps_r = Eo2(nc.psum_tensor(f"{tag}pr", [1, 128], BF16))
            for cs, cn in cast_gw:
                p.wait("sync", cs, cn)
            p.wait("sync", sto_x[0], sto_x[0].n)
            p.wait("sync", sto_x[1], sto_x[1].n)
            p.dma("sync", gwt[:].rearrange("p (ic e) -> p ic e", e=E),
                  gw_ap.rearrange("(ic p) e -> p ic e", p=128), gld)
            p.dma("sync", gbr[:], gb_ap.rearrange("(o e) -> o e", o=1), gld)
            for ic in range(NCH):
                p.dma("sync", x3p[ic][:], xn3_d[ic * 128:(ic + 1) * 128, :], gld)
            gld_post = gld.n
            p.op("gpsimd", lambda En: En.iota(
                io8i[:], [[1, E]], base=0, channel_multiplier=0), inc=po)
            p.wait("vector", po, po.n)
            p.op("vector", lambda En: En.tensor_copy(io8[:], io8i[:]), inc=dve)
            io_post = dve.n
            lg_free = [0, 0]
            xtf_post = 0
            for tt in range(NT):
                lb = tt % 2
                p.wait("tensor", gld, gld_post)
                p.wait("tensor", dve, lg_free[lb])
                for ic in range(NCH):
                    p.wait("scalar", gld, gld_post)
                    p.wait("scalar", pe, xtf_post)
                    p.op("scalar", lambda En, ic=ic, tt=tt: En.activation(
                        xtf[:], x3p[ic][:, tt * 128:(tt + 1) * 128], AF.Copy),
                        inc=act)
                    p.wait("tensor", act, act.n)
                    ent = p.op("tensor", lambda En, ic=ic, tt=tt, lb=lb: En.matmul(
                        ps_l[lb][:, 0:E], xtf[:],
                        gwt[:, ic * E:(ic + 1) * E],
                        start=(ic == 0), stop=False), inc=pe)
                    xtf_post = ent["post"]
                p.op("tensor", lambda En, lb=lb: En.matmul(
                    ps_l[lb][:, 0:E], onef[0:1, :], gbr[0:1, :],
                    start=False, stop=True), inc=pe)
                p.wait("vector", pe, pe.n)
                p.wait("vector", dve, io_post)
                p.op("vector", lambda En, lb=lb: En.tensor_copy(
                    lg[:], ps_l[lb][:, 0:E]), inc=dve)
                lg_free[lb] = dve.n
                p.wait("vector", dve, dve.n)
                p.op("vector", lambda En: En.reduce_max(mx[:], lg[:], AX.X),
                     inc=dve)
                p.wait("vector", dve, dve.n)
                p.op("vector", lambda En: En.tensor_scalar(
                    eq[:], lg[:], mx[:, 0:1], None, ALU.is_ge), inc=dve)
                p.wait("vector", dve, dve.n)
                p.op("vector", lambda En: En.tensor_scalar(
                    sel[:], eq[:], -999.0, 999.0, ALU.mult, ALU.add), inc=dve)
                p.wait("vector", dve, dve.n)
                p.op("vector", lambda En: En.tensor_add(sel[:], sel[:], io8[:]),
                     inc=dve)
                p.wait("vector", dve, dve.n)
                p.op("vector", lambda En: En.tensor_reduce(
                    rcl[:], sel[:], AX.X, ALU.min), inc=dve)
                p.wait("vector", dve, dve.n)
                p.wait("vector", evr, evr.n)
                p.op("vector", lambda En: En.tensor_copy(rclb[:], rcl[:]),
                     inc=dve)
                p.wait("tensor", dve, dve.n)
                p.wait("tensor", po, ident_post)
                p.wait("tensor", evr, evr.n)
                p.op("tensor", lambda En: En.transpose(
                    ps_r[:], rclb[:, 0:1], identb[:]), inc=pe)
                p.wait("vector", pe, pe.n)
                p.op("vector", lambda En, tt=tt: En.tensor_copy(
                    rrow[0:1, tt * 128:(tt + 1) * 128], ps_r[:]), inc=evr)
            p.wait("scalar", evr, evr.n)
            p.dma("scalar", r_d.rearrange("(o t) -> o t", o=1), rrow[:], sto)
            p.wait("scalar", sto, sto.n)
    p.barrier(tag, reclaim=True)


def p_moe(p, xn3_d, r_d, moe_d, w1_b, w2_b, h_d, h_d2, cast_w2, tag="p6"):
    nc = p.nc
    NHC = HID // 128
    NTC = T // 512
    hds = [h_d, h_d2]
    with contextlib.ExitStack() as stk:
        Eo = stk.enter_context
        acc = [Eo(nc.sbuf_tensor(f"{tag}ac{i}", [128, T], BF16))
               for i in range(NCH)]
        x3p = [Eo(nc.sbuf_tensor(f"{tag}xp{i}", [128, T], BF16))
               for i in range(NCH)]
        w2t = [Eo(nc.sbuf_tensor(f"{tag}w2{i}", [128, HID], BF16))
               for i in range(2)]
        hbl = [Eo(nc.sbuf_tensor(f"{tag}hb{i}", [128, 512], BF16))
               for i in range(NHC)]
        w1t = [Eo(nc.sbuf_tensor(f"{tag}w1{i}", [128, D], BF16))
               for i in range(2)]
        hst = [Eo(nc.sbuf_tensor(f"{tag}hs{i}", [128, 512], BF16))
               for i in range(3)]
        yst = [Eo(nc.sbuf_tensor(f"{tag}ys{i}", [128, 512], BF16))
               for i in range(3)]
        mei = [Eo(nc.sbuf_tensor(f"{tag}me{i}", [128, 512], I32))
               for i in range(2)]
        rrow = Eo(nc.sbuf_tensor(f"{tag}rr", [1, T], BF16))
        oneb = Eo(nc.sbuf_tensor(f"{tag}oneb", [1, 128], BF16))
        rbs = [Eo(nc.sbuf_tensor(f"{tag}rb{i}", [128, 512], BF16))
               for i in range(NTC)]
        ps_h = [Eo(nc.psum_tensor(f"{tag}ph{i}", [128, 512], F32))
                for i in range(3)]
        ps_y = [Eo(nc.psum_tensor(f"{tag}py{i}", [128, 512], F32))
                for i in range(4)]
        ps_rb = [ps_h[0], ps_h[1]]
        pe = p.sem("pe"); act = p.sem("act"); dve = p.sem("dve")
        ldr = p.sem("ldr"); ldx = p.dsem("ldx")
        ldw1 = [p.dsem("lw0"), p.dsem("lw1")]
        ldw2 = [p.dsem("l20"), p.dsem("l21")]
        ldh = [p.dsem("ldh0"), p.dsem("ldh1")]
        ldhg = [p.sem("lhg0", persist=True, dma=True),
                p.sem("lhg1", persist=True, dma=True)]
        sto1 = [p.dsem("s10"),
                p.sem("s11", persist=True, dma=True), p.dsem("s12")]
        sto = p.dsem("sto")
        for cs, cn in cast_w2:
            p.wait("sync", cs, cn)
        p.dma("sync", rrow[:], r_d.rearrange("(o t) -> o t", o=1), ldr)
        for ic in range(NCH):
            p.dma("sync", x3p[ic][:], xn3_d[ic * 128:(ic + 1) * 128, :], ldx)
        x3_post = ldx.n
        p.op("vector", lambda En: En.memset(oneb[:], 1.0), inc=dve)
        for i in range(NCH):
            p.op("vector", lambda En, i=i: En.memset(acc[i][:], 0.0), inc=dve)
        acc0 = dve.n
        rb_free = [0, 0]
        for tc in range(NTC):
            b = tc % 2
            p.wait("tensor", ldr, ldr.n)
            p.wait("tensor", dve, acc0)
            p.wait("tensor", act, rb_free[b])
            p.op("tensor", lambda En, b=b, tc=tc: En.matmul(
                ps_rb[b][:], oneb[:], rrow[0:1, tc * 512:(tc + 1) * 512],
                start=True, stop=True), inc=pe)
            p.wait("scalar", pe, pe.n)
            p.op("scalar", lambda En, b=b, tc=tc: En.activation(
                rbs[tc][:], ps_rb[b][:], AF.Copy), inc=act)
            rb_free[b] = act.n
        rb_post = act.n
        w1_free = [0, 0]
        psh_free = [rb_free[0], rb_free[1], 0]
        w2s_free = [0, 0]
        py_free = [0, 0, 0, 0]
        yst_free = [0, 0, 0]
        me_post = [0, 0]
        ldh_after = {}
        p1_done = {}
        for e in range(E):
            hd = hds[e % 2]
            if e >= 2:
                for se in ("scalar", "gpsimd", "sync"):
                    p.wait(se, ldh[0], ldh_after[e - 2][0])
                    p.wait(se, ldh[1], ldh_after[e - 2][1])
                    p.wait(se, ldhg[0], ldh_after[e - 2][2])
                    p.wait(se, ldhg[1], ldh_after[e - 2][3])
            st_eng = {0: "scalar", 1: "gpsimd", 2: "sync"}
            for hc in range(NHC):
                wb = hc % 2
                p.wait("sync", pe, w1_free[wb])
                p.dma("sync",
                      w1t[wb][:, 0:512].rearrange("p (ic o) -> p ic o", o=128),
                      w1_b[e][0:512, hc * 128:(hc + 1) * 128].rearrange(
                          "(ic p) o -> p ic o", p=128), ldw1[wb])
                p.dma("sync",
                      w1t[wb][:, 512:1024].rearrange("p (ic o) -> p ic o", o=128),
                      w1_b[e][512:1024, hc * 128:(hc + 1) * 128].rearrange(
                          "(ic p) o -> p ic o", p=128), ldw1[wb])
                w1_post = ldw1[wb].n
                p.wait("tensor", ldw1[wb], w1_post)
                if hc == 0 and e == 0:
                    p.wait("tensor", ldx, x3_post)
                for tc in range(NTC):
                    hb = tc % 3
                    p.wait("tensor", act, psh_free[hb])
                    for ic in range(NCH):
                        p.op("tensor", lambda En, hb=hb, wb=wb, ic=ic, tc=tc:
                             En.matmul(
                                 ps_h[hb][:],
                                 w1t[wb][:, ic * 128:(ic + 1) * 128],
                                 x3p[ic][:, tc * 512:(tc + 1) * 512],
                                 start=(ic == 0), stop=(ic == NCH - 1)),
                             inc=pe)
                    p.wait("scalar", pe, pe.n)
                    p.wait("scalar", sto1[hb], sto1[hb].n)
                    p.op("scalar", lambda En, hb=hb: En.activation(
                        hst[hb][:], ps_h[hb][:], AF.Gelu_apprx_tanh), inc=act)
                    psh_free[hb] = act.n
                    se = st_eng[hb]
                    p.wait(se, act, act.n)
                    p.dma(se, hd[hc * 128:(hc + 1) * 128,
                                 tc * 512:(tc + 1) * 512],
                          hst[hb][:], sto1[hb])
                w1_free[wb] = pe.n
            p1_done[e] = (sto1[0].n, sto1[1].n, sto1[2].n)
            for si in range(3):
                p.wait("scalar", sto1[si], p1_done[e][si])
                p.wait("gpsimd", sto1[si], p1_done[e][si])
            NHH = NHC // 2

            def load_half(tc2, half, gate_pe):
                p.wait("scalar", pe, gate_pe)
                p.wait("gpsimd", pe, gate_pe)
                for hc in range(half * NHH, (half + 1) * NHH):
                    he = "scalar" if hc % 2 == 0 else "gpsimd"
                    p.dma(he, hbl[hc][:],
                          hd[hc * 128:(hc + 1) * 128,
                             tc2 * 512:(tc2 + 1) * 512],
                          ldh[half] if he == "scalar" else ldhg[half])
                return (ldh[half].n, ldhg[half].n)

            hp = {(0, 0): load_half(0, 0, 0)}
            hp[(0, 1)] = load_half(0, 1, 0)

            def load_w2tile(tc2, ocg, hq, slot, gate_pe):
                for qi, we in ((0, "sync"), (1, "sync")):
                    p.wait(we, pe, gate_pe)
                    p.dma(we,
                          w2t[slot][:, qi * 2048:(qi + 1) * 2048].rearrange(
                              "p (hc o) -> p hc o", o=512),
                          w2_b[e][hq * 1024 + qi * 512:
                                  hq * 1024 + (qi + 1) * 512,
                                  ocg * 512:(ocg + 1) * 512].rearrange(
                              "(hc p) o -> p hc o", p=128), ldw2[slot])
                return ldw2[slot].n

            tiles = [(tc, ocg, hq) for tc in range(NTC)
                     for ocg in range(2) for hq in range(4)]
            tpost = {}
            mei_post = 0
            for j, (tc, ocg, hq) in enumerate(tiles):
                slot = j % 2
                if j == 0:
                    tpost[0] = load_w2tile(*tiles[0], 0, w2s_free[0])
                    tpost[1] = load_w2tile(*tiles[1], 1, w2s_free[1])
                mb = tc % 2
                if ocg == 0 and hq == 0:
                    p.wait("vector", dve, me_post[mb])
                    p.wait("vector", act, rb_post)
                    p.op("vector", lambda En, mb=mb, tc=tc, e=e:
                         En.tensor_scalar(mei[mb][:], rbs[tc][:], float(e),
                                          None, ALU.is_equal), inc=dve)
                    mei_post = dve.n
                p.wait("tensor", ldw2[slot], tpost[j])
                if hq in (0, 2):
                    p.wait("tensor", ldh[hq // 2], hp[(tc, hq // 2)][0])
                    p.wait("tensor", ldhg[hq // 2], hp[(tc, hq // 2)][1])
                if hq == 0:
                    for ocl in range(4):
                        p.wait("tensor", act, py_free[ocl])
                for ocl in range(4):
                    for hcl in range(NHH // 2):
                        p.op("tensor",
                             lambda En, ocl=ocl, hcl=hcl, hq=hq, slot=slot:
                             En.matmul(
                                 ps_y[ocl][:],
                                 w2t[slot][:, hcl * 512 + ocl * 128:
                                           hcl * 512 + ocl * 128 + 128],
                                 hbl[hq * (NHH // 2) + hcl][:],
                                 start=(hq == 0 and hcl == 0),
                                 stop=(hq == 3 and hcl == NHH // 2 - 1)),
                             inc=pe)
                w2s_free[slot] = pe.n
                if j + 2 < len(tiles):
                    tpost[j + 2] = load_w2tile(*tiles[j + 2], slot, pe.n)
                if ocg == 1 and hq == 1 and tc + 1 < NTC:
                    hp[(tc + 1, 0)] = load_half(tc + 1, 0, pe.n)
                if ocg == 1 and hq == 3 and tc + 1 < NTC:
                    hp[(tc + 1, 1)] = load_half(tc + 1, 1, pe.n)
                if hq == 3:
                    p.wait("scalar", pe, pe.n)
                    for ocl in range(4):
                        oc = ocg * 4 + ocl
                        yb = oc % 3
                        p.wait("scalar", dve, yst_free[yb])
                        p.op("scalar", lambda En, ocl=ocl, yb=yb:
                             En.activation(yst[yb][:], ps_y[ocl][:], AF.Copy),
                             inc=act)
                        py_free[ocl] = act.n
                        p.wait("vector", act, act.n)
                        p.wait("vector", dve, mei_post)
                        p.op("vector", lambda En, yb=yb, mb=mb, oc=oc, tc=tc:
                             En.copy_predicated(
                                 acc[oc][:, tc * 512:(tc + 1) * 512],
                                 mei[mb][:], yst[yb][:]), inc=dve)
                        yst_free[yb] = dve.n
                    if ocg == 1:
                        me_post[mb] = dve.n
            ldh_after[e] = (ldh[0].n, ldh[1].n, ldhg[0].n, ldhg[1].n)
        p.wait("scalar", dve, dve.n)
        for oc in range(NCH):
            p.dma("scalar", moe_d[oc * 128:(oc + 1) * 128, :], acc[oc][:], sto)
        p.wait("scalar", sto, sto.n)
    p.barrier(tag, reclaim=True)


def p_gsilu(p, y1_d, moe_d, g_d, tag="p7"):
    nc = p.nc
    with contextlib.ExitStack() as stk:
        Eo = stk.enter_context
        NB3 = 3
        yt = [Eo(nc.sbuf_tensor(f"{tag}y{i}", [128, 512], F32)) for i in range(NB3)]
        mt = [Eo(nc.sbuf_tensor(f"{tag}m{i}", [128, 512], BF16)) for i in range(NB3)]
        st_ = [Eo(nc.sbuf_tensor(f"{tag}s{i}", [128, 512], F32)) for i in range(NB3)]
        gt = [Eo(nc.sbuf_tensor(f"{tag}g{i}", [128, 512], BF16)) for i in range(NB3)]
        ld = [p.dsem(f"ld{i}") for i in range(NB3)]
        sto = [p.dsem(f"st{i}") for i in range(NB3)]
        dve = p.sem("dve"); act = p.sem("act")
        buf_free = [0] * NB3
        st_free = [0] * NB3
        for i in range(NCH * (T // 512)):
            oc, tc = divmod(i, T // 512)
            b = i % NB3
            p.wait("sync", dve, buf_free[b])
            p.wait("sync", act, st_free[b])
            sl = (slice(oc * 128, (oc + 1) * 128), slice(tc * 512, (tc + 1) * 512))
            p.dma("sync", yt[b][:], y1_d[sl[0], sl[1]], ld[b])
            p.dma("sync", mt[b][:], moe_d[sl[0], sl[1]], ld[b])
            p.wait("vector", ld[b], ld[b].n)
            p.op("vector", lambda En, b=b: En.tensor_add(
                st_[b][:], yt[b][:], mt[b][:]), inc=dve)
            buf_free[b] = dve.n
            p.wait("scalar", dve, dve.n)
            p.wait("scalar", sto[b], sto[b].n)
            p.op("scalar", lambda En, b=b: En.activation(
                gt[b][:], st_[b][:], AF.Silu), inc=act)
            st_free[b] = act.n
            p.wait("scalar", act, act.n)
            p.dma("scalar", g_d[sl[0], sl[1]], gt[b][:], sto[b])
        p.wait("scalar", sto[0], sto[0].n)
        p.wait("scalar", sto[1], sto[1].n)
    p.barrier(tag, reclaim=True)


def p_resln(p, x_ap, s_d, g_d, x1_d, xn1_d, tag="p8"):
    nc = p.nc
    with contextlib.ExitStack() as stk:
        Eo = stk.enter_context
        NB = 3
        xt = [Eo(nc.sbuf_tensor(f"{tag}x{i}", [128, D], F32)) for i in range(NB)]
        st_ = [Eo(nc.sbuf_tensor(f"{tag}s{i}", [128, D], BF16)) for i in range(NB)]
        gt = [Eo(nc.sbuf_tensor(f"{tag}g{i}", [128, D], BF16)) for i in range(NB)]
        x1t = [Eo(nc.sbuf_tensor(f"{tag}o{i}", [128, D], F32)) for i in range(NB)]
        xnt = [Eo(nc.sbuf_tensor(f"{tag}n{i}", [128, D], BF16)) for i in range(NB)]
        xsq = [Eo(nc.sbuf_tensor(f"{tag}q{i}", [128, D], F32)) for i in range(2)]
        scol = [Eo(nc.sbuf_tensor(f"{tag}sc{i}", [128, 1], F32)) for i in range(NB)]
        qcol = [Eo(nc.sbuf_tensor(f"{tag}qc{i}", [128, 1], F32)) for i in range(NB)]
        mcol = [Eo(nc.sbuf_tensor(f"{tag}mc{i}", [128, 1], F32)) for i in range(NB)]
        vcol = [Eo(nc.sbuf_tensor(f"{tag}vc{i}", [128, 1], F32)) for i in range(NB)]
        rcol = [Eo(nc.sbuf_tensor(f"{tag}rc{i}", [128, 1], F32)) for i in range(NB)]
        epsc = Eo(nc.sbuf_tensor(f"{tag}eps", [128, 1], F32))
        ld = [p.dsem(f"ld{i}") for i in range(NB)]
        sto1 = [p.dsem(f"sa{i}") for i in range(NB)]
        sto2 = [p.dsem(f"sb{i}") for i in range(NB)]
        act = p.sem("act"); dve = p.sem("dve")
        rstd = p.sem("rstd"); app = p.sem("app")
        cst = p.sem("cst")
        p.op("vector", lambda En: En.memset(epsc[:], float(EPS)), inc=cst)
        p.wait("scalar", cst, cst.n)
        sq_post = {}
        app_post = {}
        for i in range(NT):
            b = i % NB
            if i >= NB:
                p.wait("sync", app, app_post[i - NB])
                p.wait("sync", act, sq_post[i - NB])
                p.wait("sync", sto1[b], sto1[b].n)
            p.dma("sync", xt[b][:], x_ap[i * 128:(i + 1) * 128, :], ld[b])
            p.dma("sync", st_[b][:], s_d[:, i * 128:(i + 1) * 128], ld[b],
                  transpose=True)
            p.dma("sync", gt[b][:], g_d[:, i * 128:(i + 1) * 128], ld[b],
                  transpose=True)
            p.wait("vector", ld[b], ld[b].n)
            p.op("vector", lambda En, b=b: En.tensor_mul(
                x1t[b][:], st_[b][:], gt[b][:]), inc=dve)
            p.wait("vector", dve, dve.n)
            p.op("vector", lambda En, b=b: En.tensor_add(
                x1t[b][:], x1t[b][:], xt[b][:]), inc=dve)
            x1p = dve.n
            p.wait("scalar", dve, x1p)
            p.dma("scalar", x1_d[i * 128:(i + 1) * 128, :], x1t[b][:], sto1[b])
            p.wait("vector", dve, x1p)
            p.op("vector", lambda En, b=b: En.reduce_sum(
                scol[b][:], x1t[b][:], AX.X), inc=dve)
            p.wait("vector", dve, dve.n)
            p.op("vector", lambda En, b=b: En.tensor_scalar(
                mcol[b][:], scol[b][:], 1.0 / D, None, ALU.mult), inc=dve)
            p.wait("scalar", dve, dve.n)
            ent = p.op("scalar", lambda En, b=b, i=i: En.activation(
                xsq[i % 2][:], x1t[b][:], AF.Square, accum_out=qcol[b][:, 0:1]),
                inc=act)
            sq_post[i] = ent["post"]
            p.wait("vector", dve, dve.n)
            p.wait("vector", act, act.n)
            p.op("vector", lambda En, b=b: En.tensor_tensor(
                vcol[b][:], mcol[b][:], mcol[b][:], ALU.mult), inc=dve)
            p.wait("vector", dve, dve.n)
            p.op("vector", lambda En, b=b: En.scalar_tensor_tensor(
                vcol[b][:], qcol[b][:], 1.0 / D, vcol[b][:], ALU.mult,
                ALU.subtract), inc=dve)
            p.wait("scalar", dve, dve.n)
            p.op("scalar", lambda En, b=b: En.activation(
                rcol[b][:], vcol[b][:], AF.Ln, bias=epsc[:, 0:1]), inc=act)
            p.wait("scalar", act, act.n)
            p.op("scalar", lambda En, b=b: En.activation(
                rcol[b][:], rcol[b][:], AF.Exp, scale=-0.5), inc=rstd)
            p.wait("vector", rstd, rstd.n)
            p.wait("vector", sto2[b], sto2[b].n)
            p.op("vector", lambda En, b=b: En.tensor_scalar(
                xnt[b][:], x1t[b][:], mcol[b][:, 0:1], rcol[b][:, 0:1],
                ALU.subtract, ALU.mult), inc=app)
            app_post[i] = app.n
            p.wait("scalar", app, app.n)
            p.dma("scalar", xn1_d[i * 128:(i + 1) * 128, :], xnt[b][:], sto2[b])
        for s in sto1 + sto2:
            p.wait("scalar", s, s.n)
    p.barrier(tag, reclaim=True)


def p_l1ffn(p, xn1_d, wg_b, wu_b, wd_b, g1_d, cast_wd, tag="p11"):
    nc = p.nc
    NH = FHID // 128
    with contextlib.ExitStack() as stk:
        Eo = stk.enter_context
        wgp = [Eo(nc.sbuf_tensor(f"{tag}wg{i}", [128, FHID], BF16))
               for i in range(NCH)]
        wup = [Eo(nc.sbuf_tensor(f"{tag}wu{i}", [128, FHID], BF16))
               for i in range(NCH)]
        wdp = [Eo(nc.sbuf_tensor(f"{tag}wd{i}", [128, D], BF16))
               for i in range(NH)]
        xin = [Eo(nc.sbuf_tensor(f"{tag}xi{i}", [128, 512], BF16))
               for i in range(NCH)]
        hsb = [Eo(nc.sbuf_tensor(f"{tag}h{i}", [128, 512], BF16))
               for i in range(NH)]
        gst = [Eo(nc.sbuf_tensor(f"{tag}gs{i}", [128, 512], BF16))
               for i in range(2)]
        ost = [Eo(nc.sbuf_tensor(f"{tag}os{i}", [128, 512], BF16))
               for i in range(2)]
        ps_g = [Eo(nc.psum_tensor(f"{tag}pg{i}", [128, 512], F32))
                for i in range(2)]
        ps_u = [Eo(nc.psum_tensor(f"{tag}pu{i}", [128, 512], F32))
                for i in range(2)]
        ps_o = [Eo(nc.psum_tensor(f"{tag}po{i}", [128, 512], F32))
                for i in range(2)]
        ld = p.dsem("ld"); ldx = p.dsem("ldx"); pe = p.sem("pe")
        act = p.sem("act"); dve = p.sem("dve")
        sto = [p.dsem("st0"), p.dsem("st1")]
        for cs, cn in cast_wd:
            p.wait("sync", cs, cn)
        for ic in range(NCH):
            p.dma("sync", wgp[ic][:], wg_b[:, ic * 128:(ic + 1) * 128], ld,
                  transpose=True)
            p.dma("sync", wup[ic][:], wu_b[:, ic * 128:(ic + 1) * 128], ld,
                  transpose=True)
        for hc in range(NH):
            p.dma("sync", wdp[hc][:], wd_b[:, hc * 128:(hc + 1) * 128], ld,
                  transpose=True)
        w_post = ld.n
        g_free = [0, 0]
        o_free = [0, 0]
        ost_free = [0, 0]
        for tc in range(T // 512):
            if tc > 0:
                p.wait("sync", pe, pe.n)
            for ic in range(NCH):
                p.dma("sync", xin[ic][:],
                      xn1_d[tc * 512:(tc + 1) * 512, ic * 128:(ic + 1) * 128],
                      ldx, transpose=True)
            x_post = ldx.n
            hsb_post = 0
            for hc in range(NH):
                gb = hc % 2
                p.wait("tensor", ld, w_post)
                p.wait("tensor", ldx, x_post)
                p.wait("tensor", dve, g_free[gb])
                for ic in range(NCH):
                    p.op("tensor", lambda En, gb=gb, hc=hc, ic=ic: En.matmul(
                        ps_g[gb][:], wgp[ic][:, hc * 128:(hc + 1) * 128],
                        xin[ic][:], start=(ic == 0), stop=(ic == NCH - 1)),
                        inc=pe)
                for ic in range(NCH):
                    p.op("tensor", lambda En, gb=gb, hc=hc, ic=ic: En.matmul(
                        ps_u[gb][:], wup[ic][:, hc * 128:(hc + 1) * 128],
                        xin[ic][:], start=(ic == 0), stop=(ic == NCH - 1)),
                        inc=pe)
                p.wait("scalar", pe, pe.n)
                p.wait("scalar", dve, g_free[gb])
                p.op("scalar", lambda En, gb=gb: En.activation(
                    gst[gb][:], ps_g[gb][:], AF.Silu, scale=1.0 / 127.0),
                    inc=act)
                p.wait("vector", act, act.n)
                p.op("vector", lambda En, gb=gb, hc=hc: En.scalar_tensor_tensor(
                    hsb[hc][:], ps_u[gb][:], 1.0 / 127.0, gst[gb][:],
                    ALU.mult, ALU.mult), inc=dve)
                g_free[gb] = dve.n
            h_post = dve.n
            for oc in range(NCH):
                ob = oc % 2
                p.wait("tensor", dve, h_post)
                p.wait("tensor", act, o_free[ob])
                for hc in range(NH):
                    p.op("tensor", lambda En, ob=ob, oc=oc, hc=hc: En.matmul(
                        ps_o[ob][:], wdp[hc][:, oc * 128:(oc + 1) * 128],
                        hsb[hc][:], start=(hc == 0), stop=(hc == NH - 1)),
                        inc=pe)
                p.wait("scalar", pe, pe.n)
                p.wait("scalar", sto[ob], sto[ob].n)
                p.op("scalar", lambda En, ob=ob: En.activation(
                    ost[ob][:], ps_o[ob][:], AF.Silu, scale=1.0 / 127.0),
                    inc=act)
                o_free[ob] = act.n
                p.wait("scalar", act, act.n)
                p.dma("scalar", g1_d[oc * 128:(oc + 1) * 128,
                                     tc * 512:(tc + 1) * 512], ost[ob][:],
                      sto[ob])
        p.wait("scalar", sto[0], sto[0].n)
        p.wait("scalar", sto[1], sto[1].n)
    p.barrier(tag, reclaim=True)


def p_out(p, x1_d, s1_d, g1_d, out_ap, tag="p12"):
    nc = p.nc
    with contextlib.ExitStack() as stk:
        Eo = stk.enter_context
        NB = 3
        xt = [Eo(nc.sbuf_tensor(f"{tag}x{i}", [128, D], F32)) for i in range(NB)]
        st_ = [Eo(nc.sbuf_tensor(f"{tag}s{i}", [128, D], BF16)) for i in range(NB)]
        gt = [Eo(nc.sbuf_tensor(f"{tag}g{i}", [128, D], BF16)) for i in range(NB)]
        ot = [Eo(nc.sbuf_tensor(f"{tag}o{i}", [128, D], F32)) for i in range(NB)]
        ld = [p.dsem(f"ld{i}") for i in range(NB)]
        sto = [p.dsem(f"st{i}") for i in range(NB)]
        dve = p.sem("dve")
        dv_post = {}
        for i in range(NT):
            b = i % NB
            if i >= NB:
                p.wait("sync", dve, dv_post[i - NB])
                p.wait("sync", sto[b], sto[b].n)
            p.dma("sync", xt[b][:], x1_d[i * 128:(i + 1) * 128, :], ld[b])
            p.dma("sync", st_[b][:], s1_d[:, i * 128:(i + 1) * 128], ld[b],
                  transpose=True)
            p.dma("sync", gt[b][:], g1_d[:, i * 128:(i + 1) * 128], ld[b],
                  transpose=True)
            p.wait("vector", ld[b], ld[b].n)
            p.op("vector", lambda En, b=b: En.tensor_mul(
                ot[b][:], st_[b][:], gt[b][:]), inc=dve)
            p.wait("vector", dve, dve.n)
            p.op("vector", lambda En, b=b: En.tensor_add(
                ot[b][:], ot[b][:], xt[b][:]), inc=dve)
            dv_post[i] = dve.n
            p.wait("scalar", dve, dve.n)
            p.dma("scalar", out_ap[i * 128:(i + 1) * 128, :], ot[b][:], sto[b])
        for s in sto:
            p.wait("scalar", s, s.n)
    p.barrier(tag, reclaim=True)


def declare_scratch(nc):
    d = {}
    def mk(name, shape, dt):
        d[name] = nc.dram_tensor(name, shape, dt, kind="Internal").ap()
    mk("xn_d", [T, D], BF16)
    mk("xn2_d", [T, D], BF16)
    mk("s_d", [D, T], BF16)
    mk("y1_d", [D, T], F32)
    mk("xn3_d", [D, T], BF16)
    mk("r_d", [T], BF16)
    mk("moe_d", [D, T], BF16)
    mk("g_d", [D, T], BF16)
    mk("x1_d", [T, D], F32)
    mk("xn1_d", [T, D], BF16)
    mk("h_d", [HID, T], BF16)
    mk("h_d2", [HID, T], BF16)
    mk("lat_d", [256, T], BF16)
    mk("vd_d", [128, T], BF16)
    mk("s1_d", [D, T], BF16)
    mk("g1_d", [D, T], BF16)
    mk("qw_b", [D, D], BF16)
    mk("kw_b", [D, D], BF16)
    mk("vw_b", [D, D], BF16)
    mk("ow_b", [D, D], BF16)
    mk("wqkv_b", [D, 3 * D], BF16)
    mk("wproj_b", [D, D], BF16)
    mk("gw_b", [D, E], BF16)
    mk("gb_b", [E], BF16)
    mk("w1_b", [E, D, HID], BF16)
    mk("w2_b", [E, HID, D], BF16)
    mk("l1q_b", [D, D], BF16)
    mk("l1kv_b", [D, 256], BF16)
    mk("l1ku_b", [256, D], BF16)
    mk("l1vu_b", [256, D], BF16)
    mk("l1o_b", [D, D], BF16)
    mk("wg_b", [FHID, D], BF16)
    mk("wu_b", [FHID, D], BF16)
    mk("wd_b", [D, FHID], BF16)
    return d


def declare_inputs(nc):
    a = {}
    def mk(name, shape, dt):
        a[name] = nc.dram_tensor(name, shape, dt, kind="ExternalInput").ap()
    mk("x", [T, D], F32)
    mk("L0_attn_q_w", [D, D], F32)
    mk("L0_attn_k_w", [D, D], F32)
    mk("L0_attn_v_w", [D, D], F32)
    mk("L0_attn_o_w", [D, D], F32)
    mk("L0_ffn_ln1_w", [D], F32)
    mk("L0_ffn_ln1_b", [D], F32)
    mk("L0_ffn_w_qkv", [D, 3 * D], F32)
    mk("L0_ffn_w_proj", [D, D], F32)
    mk("L0_ffn_ln2_w", [D], F32)
    mk("L0_ffn_ln2_b", [D], F32)
    mk("L0_ffn_gate_w", [D, E], F32)
    mk("L0_ffn_gate_b", [E], F32)
    mk("L0_ffn_w1", [E, D, HID], F32)
    mk("L0_ffn_w2", [E, HID, D], F32)
    mk("L1_attn_Wq", [D, H, DKH], F32)
    mk("L1_attn_Wkv_down", [D, 256], F32)
    mk("L1_attn_Wk_up", [256, H, DKH], F32)
    mk("L1_attn_Wv_up", [256, H, DKH], F32)
    mk("L1_attn_Wout", [H, DKH, D], F32)
    mk("L1_ffn_w_gate", [FHID, D], I32)
    mk("L1_ffn_w_up", [FHID, D], I32)
    mk("L1_ffn_w_down", [D, FHID], I32)
    return a


CASTS = [
    ("L0_attn_q_w", "qw_b"), ("L0_attn_k_w", "kw_b"), ("L0_attn_v_w", "vw_b"),
    ("L0_attn_o_w", "ow_b"),
    ("L0_ffn_w_qkv", "wqkv_b"), ("L0_ffn_w_proj", "wproj_b"),
    ("L1_attn_Wq", "l1q_b"), ("L1_attn_Wkv_down", "l1kv_b"),
    ("L1_attn_Wk_up", "l1ku_b"), ("L1_attn_Wv_up", "l1vu_b"),
    ("L1_attn_Wout", "l1o_b"),
    ("L1_ffn_w_gate", "wg_b"), ("L1_ffn_w_up", "wu_b"),
    ("L1_ffn_w_down", "wd_b"),
    ("L0_ffn_w1", "w1_b"), ("L0_ffn_w2", "w2_b"),
]


def emit_casts(p, ins, dram):
    after = {}
    for src, dst in CASTS:
        sap = ins[src]
        dap = dram[dst]
        cs = p.sem("c_" + dst, persist=True)
        if src in ("L0_ffn_w1", "L0_ffn_w2"):
            for e in range(E):
                p.dma("gpsimd", dap[e], sap[e], cs)
        elif src in ("L1_attn_Wq", "L1_attn_Wk_up", "L1_attn_Wv_up"):
            p.dma("gpsimd", dap, sap.rearrange("a h k -> a (h k)"), cs)
        elif src == "L1_attn_Wout":
            p.dma("gpsimd", dap, sap.rearrange("h k o -> (h k) o"), cs)
        else:
            p.dma("gpsimd", dap, sap, cs)
        after[dst] = (cs, cs.n)
    return after


def build_launch(phase_fns):
    nc = bass.Bass("TRN2", target_bir_lowering=False)
    with contextlib.ExitStack() as stack:
        p = Prog(nc, stack)
        ins = declare_inputs(nc)
        dram = declare_scratch(nc)
        out = nc.dram_tensor("out", [T, D], F32, kind="ExternalOutput").ap()
        for f in phase_fns:
            f(p, ins, dram, out)
        p.emit()
    return nc


def wcol(ap, oc, off=0):
    return ap[:, off + oc * 128: off + (oc + 1) * 128].rearrange(
        "(ic p) o -> p ic o", p=128)


def all_phases(p, ins, dram, out):
    ca = emit_casts(p, ins, dram)
    p_ln_tokmajor(p, ins["x"], dram["xn_d"], F32, "p01")
    p_attn(p, AttnCfg(
        tag="a0", v_d=dram["vd_d"], src_d=dram["xn_d"], out_d=dram["s_d"], causal=True, alibi=True,
        qw=lambda oc: wcol(dram["qw_b"], oc),
        kw_=lambda oc: wcol(dram["kw_b"], oc),
        vw=lambda oc: wcol(dram["vw_b"], oc),
        ow=lambda oc: wcol(dram["ow_b"], oc),
        wait_cast=[ca["qw_b"], ca["kw_b"], ca["vw_b"], ca["ow_b"]]))
    p_ln_tokmajor(p, dram["xn_d"], dram["xn2_d"], BF16, "p3")
    p_attn(p, AttnCfg(
        tag="a1", v_d=dram["vd_d"], src_d=dram["xn2_d"], out_d=dram["y1_d"], causal=False,
        alibi=False, out_f32=True, resid_src=dram["xn_d"],
        affine=(ins["L0_ffn_ln1_w"], ins["L0_ffn_ln1_b"]),
        qw=lambda oc: wcol(dram["wqkv_b"], oc, 0),
        kw_=lambda oc: wcol(dram["wqkv_b"], oc, D),
        vw=lambda oc: wcol(dram["wqkv_b"], oc, 2 * D),
        ow=lambda oc: wcol(dram["wproj_b"], oc),
        wait_cast=[ca["wqkv_b"], ca["wproj_b"]]))
    p_ln2_gate(p, dram["y1_d"], dram["xn3_d"], dram["r_d"],
               ins["L0_ffn_ln2_w"], ins["L0_ffn_ln2_b"],
               ins["L0_ffn_gate_w"], ins["L0_ffn_gate_b"], [])
    p_moe(p, dram["xn3_d"], dram["r_d"], dram["moe_d"],
          dram["w1_b"], dram["w2_b"], dram["h_d"], dram["h_d2"],
          [ca["w1_b"], ca["w2_b"]])
    p_gsilu(p, dram["y1_d"], dram["moe_d"], dram["g_d"])
    p_resln(p, ins["x"], dram["s_d"], dram["g_d"], dram["x1_d"], dram["xn1_d"])
    p_attn(p, AttnCfg(
        tag="a2", v_d=dram["vd_d"], src_d=dram["xn1_d"], out_d=dram["s1_d"], causal=True,
        alibi=False,
        qw=lambda oc: wcol(dram["l1q_b"], oc),
        kw_=None, vw=None,
        latent=dict(dw=lambda ocl: wcol(dram["l1kv_b"], ocl),
                    ku=lambda oc: wcol(dram["l1ku_b"], oc),
                    vu=lambda oc: wcol(dram["l1vu_b"], oc),
                    lat_d=dram["lat_d"]),
        ow=lambda oc: wcol(dram["l1o_b"], oc),
        wait_cast=[ca["l1q_b"], ca["l1kv_b"], ca["l1ku_b"], ca["l1vu_b"],
                   ca["l1o_b"]]))
    p_l1ffn(p, dram["xn1_d"], dram["wg_b"], dram["wu_b"], dram["wd_b"],
            dram["g1_d"], [ca["wg_b"], ca["wu_b"], ca["wd_b"]])
    p_out(p, dram["x1_d"], dram["s1_d"], dram["g1_d"], out)


def kernel(**inputs):
    xfull = np.ascontiguousarray(inputs["L0_attn_x"])
    weights = {k: np.ascontiguousarray(v) for k, v in inputs.items()
               if k != "L0_attn_x"}
    in_maps = []
    for c in range(8):
        m = {"x": xfull[c]}
        m.update(weights)
        in_maps.append(m)
    nc = build_launch([all_phases])
    res = run_bass_kernel_spmd(nc, in_maps, core_ids=list(range(8)))
    return np.stack([r["out"] for r in res.results], axis=0)


# version 12 (final)
# speedup vs baseline: 1.2203x; 1.0020x over previous
import contextlib
import math
import numpy as np

import concourse.bass as bass
import concourse.mybir as mybir
from concourse.bass_utils import run_bass_kernel_spmd

F32 = mybir.dt.float32
BF16 = mybir.dt.bfloat16
I32 = mybir.dt.int32
AF = mybir.ActivationFunctionType
ALU = mybir.AluOpType
AX = mybir.AxisListType

ENGINES = ("sync", "scalar", "vector", "tensor", "gpsimd")

T = 4096
D = 1024
NT = T // 128
NCH = D // 128


def set_dims(T_):
    global T, NT
    T = T_
    NT = T // 128
H = 16
DKH = 64
HID = 4096
E = 8
FHID = 3072
EPS = 1e-5
NEG = -1e30


class Sem:
    def __init__(self, prog, name):
        self.h = prog.stack.enter_context(prog.nc.semaphore(name))
        self.n = 0
        self.dma_issuers = set()


class Prog:

    def __init__(self, nc, stack):
        self.nc = nc
        self.stack = stack
        self.entries = {e: [] for e in ENGINES}
        self._semid = 0
        self._all_sems = []
        self._pool = {False: [], True: []}
        self._phase_sems = []
        self._bar = None
        self._barcnt = 0
        self._qsems = {}

    def sem(self, tag="", persist=False, dma=False):
        if not persist and self._pool[dma]:
            s = self._pool[dma].pop()
        else:
            self._semid += 1
            s = Sem(self, f"s{self._semid}{tag}")
            self._all_sems.append(s)
            s.is_dma = dma
        if not persist:
            self._phase_sems.append(s)
        return s

    def dsem(self, tag=""):
        return self.sem(tag, dma=True)

    def op(self, eng, fn, inc=None, amt=1):
        ent = {"fn": fn, "sem": inc, "amt": amt if inc is not None else 0,
               "wait": None}
        if inc is not None:
            inc.n += amt
            ent["post"] = inc.n
        self.entries[eng].append(ent)
        return ent

    def wait(self, eng, sem, n=None):
        tgt = sem.n if n is None else n
        self.entries[eng].append({"fn": None, "sem": None, "amt": 0,
                                  "wait": (sem, tgt)})
        return tgt

    def dma(self, eng, out, in_, inc, **kw):
        inc.dma_issuers.add(eng)
        return self.op(eng,
                       lambda En, o=out, i=in_, k=dict(kw): En.dma_start(out=o, in_=i, **k),
                       inc=inc, amt=16)

    def barrier(self, tag="", reclaim=False):
        if self._bar is None:
            self._bar = self.sem("bar", persist=True)
            for e in ENGINES:
                self._qsems[e] = self.sem("q" + e, persist=True)
        bar = self._bar
        for s in self._all_sems:
            for e in s.dma_issuers:
                self.wait(e, s, s.n)
        for e in ENGINES:
            last = None
            for ent in reversed(self.entries[e]):
                if ent["fn"] is not None:
                    last = ent
                    break
            if last is not None and last["sem"] is not bar:
                if last["sem"] is None:
                    q = self._qsems[e]
                    q.n += 1
                    last["sem"] = q
                    last["amt"] = 1
                    last["post"] = q.n
                self.wait(e, last["sem"], last["post"])
            self.op(e, lambda En: En.nop(), inc=bar, amt=1)
        self._barcnt += 5
        for e in ENGINES:
            self.wait(e, bar, self._barcnt)
        if reclaim:
            for s in self._phase_sems:
                self._pool[getattr(s, "is_dma", False)].append(s)
            self._phase_sems = []

    def emit(self):
        nc = self.nc
        handles = {"sync": nc.sync, "scalar": nc.scalar, "vector": nc.vector,
                   "tensor": nc.tensor, "gpsimd": nc.gpsimd}

        def run(e):
            En = handles[e]
            for ent in self.entries[e]:
                if ent["wait"] is not None:
                    s, t = ent["wait"]
                    En.wait_ge(s.h, t)
                    continue
                r = ent["fn"](En)
                if ent["sem"] is not None:
                    r.then_inc(ent["sem"].h, ent["amt"])

        with nc.Block() as block:
            block.sync(lambda _: run("sync"))
            block.scalar(lambda _: run("scalar"))
            block.vector(lambda _: run("vector"))
            block.tensor(lambda _: run("tensor"))
            block.gpsimd(lambda _: run("gpsimd"))


def p_ln_tokmajor(p, x_ap, out_d, in_dtype, tag):
    nc = p.nc
    with contextlib.ExitStack() as stk:
        Eo = stk.enter_context
        NB = 3
        xt = [Eo(nc.sbuf_tensor(f"{tag}x{i}", [128, D], in_dtype)) for i in range(NB)]
        xnt = [Eo(nc.sbuf_tensor(f"{tag}n{i}", [128, D], BF16)) for i in range(NB)]
        xsq = [Eo(nc.sbuf_tensor(f"{tag}q{i}", [128, D], F32)) for i in range(2)]
        scol = [Eo(nc.sbuf_tensor(f"{tag}s{i}", [128, 1], F32)) for i in range(NB)]
        qcol = [Eo(nc.sbuf_tensor(f"{tag}c{i}", [128, 1], F32)) for i in range(NB)]
        mcol = [Eo(nc.sbuf_tensor(f"{tag}m{i}", [128, 1], F32)) for i in range(NB)]
        vcol = [Eo(nc.sbuf_tensor(f"{tag}v{i}", [128, 1], F32)) for i in range(NB)]
        rcol = [Eo(nc.sbuf_tensor(f"{tag}r{i}", [128, 1], F32)) for i in range(NB)]
        epsc = Eo(nc.sbuf_tensor(f"{tag}eps", [128, 1], F32))
        ld = [p.dsem(f"ld{i}") for i in range(NB)]
        sto = [p.dsem(f"st{i}") for i in range(NB)]
        act = p.sem("act")
        dve = p.sem("dve")
        rstd = p.sem("rstd")
        app = p.sem("app")
        cst = p.sem("cst")
        p.op("vector", lambda En: En.memset(epsc[:], float(EPS)), inc=cst)
        p.wait("scalar", cst, cst.n)
        sq_post = {}
        app_post = {}
        for i in range(NT):
            b = i % NB
            if i >= NB:
                p.wait("sync", app, app_post[i - NB])
                p.wait("sync", act, sq_post[i - NB])
            p.dma("sync", xt[b][:], x_ap[i * 128:(i + 1) * 128, :], ld[b])
            p.wait("vector", ld[b], ld[b].n)
            p.op("vector", lambda En, b=b: En.reduce_sum(scol[b][:], xt[b][:], AX.X),
                 inc=dve)
            p.wait("vector", dve, dve.n)
            p.op("vector", lambda En, b=b: En.tensor_scalar(
                mcol[b][:], scol[b][:], 1.0 / D, None, ALU.mult), inc=dve)
            p.wait("scalar", ld[b], ld[b].n)
            ent = p.op("scalar", lambda En, b=b, i=i: En.activation(
                xsq[i % 2][:], xt[b][:], AF.Square, accum_out=qcol[b][:, 0:1]),
                inc=act)
            sq_post[i] = ent["post"]
            p.wait("vector", dve, dve.n)
            p.wait("vector", act, act.n)
            p.op("vector", lambda En, b=b: En.tensor_tensor(
                vcol[b][:], mcol[b][:], mcol[b][:], ALU.mult), inc=dve)
            p.wait("vector", dve, dve.n)
            p.op("vector", lambda En, b=b: En.scalar_tensor_tensor(
                vcol[b][:], qcol[b][:], 1.0 / D, vcol[b][:], ALU.mult,
                ALU.subtract), inc=dve)
            p.wait("scalar", dve, dve.n)
            p.op("scalar", lambda En, b=b: En.activation(
                rcol[b][:], vcol[b][:], AF.Ln, bias=epsc[:, 0:1]), inc=act)
            p.wait("scalar", act, act.n)
            p.op("scalar", lambda En, b=b: En.activation(
                rcol[b][:], rcol[b][:], AF.Exp, scale=-0.5), inc=rstd)
            p.wait("vector", rstd, rstd.n)
            p.wait("vector", sto[b], sto[b].n)
            p.op("vector", lambda En, b=b: En.tensor_scalar(
                xnt[b][:], xt[b][:], mcol[b][:, 0:1], rcol[b][:, 0:1],
                ALU.subtract, ALU.mult), inc=app)
            app_post[i] = app.n
            p.wait("scalar", app, app.n)
            p.dma("scalar", out_d[i * 128:(i + 1) * 128, :], xnt[b][:], sto[b])
        for b in range(NB):
            p.wait("scalar", sto[b], sto[b].n)
    p.barrier(tag, reclaim=True)


class AttnCfg:
    def __init__(self, **kw):
        self.tag = kw["tag"]
        self.src_d = kw["src_d"]
        self.out_d = kw["out_d"]
        self.out_f32 = kw.get("out_f32", False)
        self.resid_src = kw.get("resid_src")
        self.causal = kw["causal"]
        self.alibi = kw["alibi"]
        self.qw = kw["qw"]
        self.kw_ = kw["kw_"]
        self.vw = kw["vw"]
        self.latent = kw.get("latent")
        self.ow = kw["ow"]
        self.affine = kw.get("affine")
        self.v_d = kw.get("v_d")
        self.wait_cast = kw.get("wait_cast", [])


def _slope(h):
    return 2.0 ** (-0.5 * (h + 1))


def p_attn(p, cfg):
    nc = p.nc
    tag = cfg.tag
    KV_CH = 2 if cfg.latent else NCH
    NKT = NT
    NQC = T // 512
    with contextlib.ExitStack() as stk:
        Eo = stk.enter_context
        if not cfg.alibi:
            PT = [Eo(nc.sbuf_tensor(f"{tag}pt{i}", [128, 1024], BF16))
                  for i in range(2)]
            NPT = 2
        else:
            PT = [Eo(nc.sbuf_tensor(f"{tag}pt{i}", [128, 512], BF16))
                  for i in range(3)]
            NPT = 3
        wq = Eo(nc.sbuf_tensor(f"{tag}wq", [128, D], BF16))
        wk = Eo(nc.sbuf_tensor(f"{tag}wk", [128, KV_CH * 128], BF16))
        wv = Eo(nc.sbuf_tensor(f"{tag}wv", [128, KV_CH * 128], BF16))
        qsq = [Eo(nc.sbuf_tensor(f"{tag}sq{i}", [128, 512], BF16))
               for i in range(2)]
        zzb = Eo(nc.sbuf_tensor(f"{tag}zzb", [65, 512], F32))
        zb_sb = zzb
        zr = zzb
        ones_f = Eo(nc.sbuf_tensor(f"{tag}onef", [128, 128], F32))
        ones_b = Eo(nc.sbuf_tensor(f"{tag}oneb", [128, 128], BF16))
        ident_b = Eo(nc.sbuf_tensor(f"{tag}idb", [128, 128], BF16))
        mask0 = Eo(nc.sbuf_tensor(f"{tag}mk", [128, 128], F32))
        nC = [Eo(nc.sbuf_tensor(f"{tag}nC{i}", [128, 1], F32)) for i in range(2)]
        mq2 = Eo(nc.sbuf_tensor(f"{tag}mq", [1, 2], F32))
        cch = Eo(nc.sbuf_tensor(f"{tag}cc", [1, 2], F32))
        if cfg.alibi:
            negB = [Eo(nc.sbuf_tensor(f"{tag}nb{i}", [65, 512], BF16))
                    for i in range(2)]
            negBi0 = Eo(nc.sbuf_tensor(f"{tag}nbi", [65, 512], mybir.dt.int16))
            negBi = [negBi0, negBi0]
            kgl_i = Eo(nc.sbuf_tensor(f"{tag}kgi", [128, NT], I32))
            kgl = Eo(nc.sbuf_tensor(f"{tag}kgf", [128, NT], F32))
            bias_h = [[Eo(nc.sbuf_tensor(f"{tag}bh{i}{j}", [128, NT], F32))
                       for j in range(2)] for i in range(2)]
        gcol = [Eo(nc.sbuf_tensor(f"{tag}gc{i}", [128, 1], F32)) for i in range(NCH)]
        bcol = [Eo(nc.sbuf_tensor(f"{tag}bc{i}", [128, 1], F32)) for i in range(NCH)]
        ps_p0 = Eo(nc.psum_tensor(f"{tag}pp", [128, 512], F32))
        ps_p = [ps_p0, ps_p0]
        WIDE = not cfg.alibi
        if WIDE:
            NS = 2
            ps_s = [Eo(nc.psum_tensor(f"{tag}ps{i}", [128, 1024], F32))
                    for i in range(NS)]
        else:
            NS = 4
            ps_s = [Eo(nc.psum_tensor(f"{tag}ps{i}", [128, 512], F32))
                    for i in range(NS)]
        ps_y = [Eo(nc.psum_tensor(f"{tag}py{i}", [65, 512], F32)) for i in range(2)]
        aux = Eo(nc.psum_tensor(f"{tag}aux", [128, 512], F32))
        vstg = [Eo(nc.sbuf_tensor(f"{tag}vg{i}", [128, 128], BF16))
                for i in range(2)]
        vstg_free = [0, 0]
        st_v = p.dsem("stv")
        ld_vt = [p.dsem("lvt0"), p.dsem("lvt1")]
        ld_pan = p.dsem("ldp"); ld_af = p.dsem("ldaf"); ld_l = p.dsem("ldl")
        ld_w = [p.dsem("ldw0"), p.dsem("ldw1")]; po = p.sem("po")
        pe_p = p.sem("pep"); evq = p.sem("evq"); aff = p.sem("aff")
        pe_n = p.sem("pen"); dve_n = p.sem("dvn")
        act_c = p.sem("acc"); dve_c = p.sem("dvc")
        pe_t = p.sem("pet"); dve_t = p.sem("dvt")
        pe_s = p.sem("pes"); msk = p.sem("msk"); ex = p.sem("ex")
        pe_av = p.sem("pav"); dve_z = p.sem("dvz")
        pe_zb = p.sem("pzb"); act_zb = p.sem("azb"); dve_y = p.sem("dvy")

        for cs, cn in cfg.wait_cast:
            p.wait("sync", cs, cn)

        pp_free = [0, 0]
        negb_free = [0, 0]
        yq_slot = [(0, 0), (0, 0)]
        pt_free = [0, 0, 0]
        exp_post = {}
        sc_cnt = [0]

        p.op("vector", lambda En: En.memset(ones_f[:], 1.0), inc=dve_c)
        p.op("vector", lambda En: En.memset(ones_b[:], 1.0), inc=dve_c)
        p.op("vector", lambda En: En.memset(mask0[:], 0.0), inc=dve_c)
        p.op("vector", lambda En: En.memset(ident_b[:], 1.0), inc=dve_c)
        cst0 = dve_c.n
        p.wait("gpsimd", dve_c, cst0)
        p.op("gpsimd", lambda En: En.affine_select(
            ident_b[:], ident_b[:], [[-1, 128]], ALU.is_equal, 0.0,
            base=0, channel_multiplier=1), inc=po)
        ident_post = po.n
        mask_post = 0
        if cfg.causal:
            p.op("gpsimd", lambda En: En.affine_select(
                mask0[:], mask0[:], [[1, 128]], ALU.is_ge, float(NEG),
                base=0, channel_multiplier=-1), inc=po)
            mask_post = po.n
        kgl_post = 0
        if cfg.alibi:
            p.op("gpsimd", lambda En: En.iota(
                kgl_i[:], [[128, NT]], base=0, channel_multiplier=1), inc=po)
            p.wait("vector", po, po.n)
            p.op("vector", lambda En: En.tensor_copy(kgl[:], kgl_i[:]), inc=dve_c)
            kgl_post = dve_c.n
        aff_cols_post = 0
        if cfg.affine is not None:
            with contextlib.ExitStack() as stka:
                wrow = stka.enter_context(nc.sbuf_tensor(f"{tag}wr", [1, D], F32))
                brow = stka.enter_context(nc.sbuf_tensor(f"{tag}br", [1, D], F32))
                p.dma("sync", wrow[:], cfg.affine[0].rearrange("(o d) -> o d", o=1), ld_af)
                p.dma("sync", brow[:], cfg.affine[1].rearrange("(o d) -> o d", o=1), ld_af)
                p.wait("tensor", ld_af, ld_af.n)
                p.wait("tensor", dve_c, cst0)
                for ic in range(NCH):
                    for row, dstc in ((wrow, gcol[ic]), (brow, bcol[ic])):
                        p.wait("tensor", act_c, act_c.n)
                        p.op("tensor", lambda En, ic=ic, row=row: En.transpose(
                            aux[:, 0:1], row[0:1, ic * 128:(ic + 1) * 128],
                            ones_f[0:1, 0:1]), inc=pe_t)
                        p.wait("scalar", pe_t, pe_t.n)
                        p.op("scalar", lambda En, dstc=dstc: En.activation(
                            dstc[:], aux[:, 0:1], AF.Copy), inc=act_c)
                aff_cols_post = act_c.n
                p.wait("sync", act_c, aff_cols_post)

        pan = [Eo(nc.sbuf_tensor(f"{tag}pan{i}", [128, T], BF16)) for i in range(NCH)]
        Y = [Eo(nc.sbuf_tensor(f"{tag}Y{i}", [128, T], BF16)) for i in range(NCH)]
        qsb = [Eo(nc.sbuf_tensor(f"{tag}q{i}", [128, T], BF16)) for i in range(2)]
        ksb = [Eo(nc.sbuf_tensor(f"{tag}k{i}", [128, T], BF16)) for i in range(2)]
        vsb = Eo(nc.sbuf_tensor(f"{tag}v", [128, T], BF16))
        v_tm = [Eo(nc.sbuf_tensor(f"{tag}vt{i}", [128, 2 * NT * 65], BF16))
                for i in range(2)]
        for ic in range(NCH):
            p.dma("sync", pan[ic][:], cfg.src_d[:, ic * 128:(ic + 1) * 128],
                  ld_pan, transpose=True)
        pan_total = ld_pan.n
        if cfg.affine is not None:
            for ic in range(NCH):
                p.wait("vector", ld_pan, pan_total)
                p.wait("vector", act_c, aff_cols_post)
                p.op("vector", lambda En, ic=ic: En.tensor_scalar(
                    pan[ic][:], pan[ic][:], gcol[ic][:, 0:1], bcol[ic][:, 0:1],
                    ALU.mult, ALU.add), inc=aff)
            pan_wait = (aff, aff.n)
        else:
            pan_wait = (ld_pan, pan_total)

        lat_tiles = None
        lat_ld = None
        lat_free = [0, 0]
        if cfg.latent:
            lat_d = cfg.latent["lat_d"]
            lat_tiles = [[Eo(nc.sbuf_tensor(f"{tag}lt{s}{i}", [128, 512], BF16))
                          for i in range(2)] for s in range(2)]
            lat_ld = [p.dsem("lls0"), p.dsem("lls1")]
            wl = wq
            lst = qsq[0]
            stl = p.dsem("stl")
            for ocl in range(2):
                if ocl > 0:
                    p.wait("sync", pe_p, pe_p.n)
                p.dma("sync", wl[:].rearrange("p (ic o) -> p ic o", o=128),
                      cfg.latent["dw"](ocl), ld_l)
                p.wait("tensor", pan_wait[0], pan_wait[1])
                p.wait("tensor", ld_l, ld_l.n)
                for tc in range(NQC):
                    pb = 0
                    p.wait("tensor", evq, pp_free[pb])
                    for ic in range(NCH):
                        p.op("tensor", lambda En, pb=pb, ic=ic, tc=tc: En.matmul(
                            ps_p[pb][:], wl[:, ic * 128:(ic + 1) * 128],
                            pan[ic][:, tc * 512:(tc + 1) * 512],
                            start=(ic == 0), stop=(ic == NCH - 1)), inc=pe_p)
                    p.wait("vector", pe_p, pe_p.n)
                    p.wait("vector", stl, stl.n)
                    p.op("vector", lambda En, pb=pb: En.tensor_copy(
                        lst[:], ps_p[pb][:]), inc=evq)
                    pp_free[pb] = evq.n
                    p.wait("scalar", evq, evq.n)
                    p.dma("scalar", lat_d[ocl * 128:(ocl + 1) * 128,
                                          tc * 512:(tc + 1) * 512], lst[:], stl)
            p.wait("sync", stl, stl.n)
            lat_store_post = stl.n
        kv_pan = pan

        state = {}

        def make_prep(oc, buf, prev_jobs_ex):
            fl = []
            p.wait("sync", pe_p, pe_p.n)
            p.wait("sync", ex, prev_jobs_ex)
            p.dma("sync", wq[:].rearrange("p (ic o) -> p ic o", o=128),
                  cfg.qw(oc), ld_w[0])
            if cfg.latent:
                p.dma("sync", wk[:].rearrange("p (ic o) -> p ic o", o=128),
                      cfg.latent["ku"](oc), ld_w[0])
                p.dma("sync", wv[:].rearrange("p (ic o) -> p ic o", o=128),
                      cfg.latent["vu"](oc), ld_w[0])
            else:
                p.dma("sync", wk[:].rearrange("p (ic o) -> p ic o", o=128),
                      cfg.kw_(oc), ld_w[0])
                p.dma("sync", wv[:].rearrange("p (ic o) -> p ic o", o=128),
                      cfg.vw(oc), ld_w[0])
            ldw_post = ld_w[0].n
            for which, dst, wt, src_pan, nkc, scale in (
                    ("q", qsb[buf], wq, pan, NCH, 0.125),
                    ("k", ksb[buf], wk, kv_pan, KV_CH, 1.0),
                    ("v", vsb, wv, kv_pan, KV_CH, 1.0)):
                for tc in range(NQC):
                    def proj(which=which, dst=dst, wt=wt, src_pan=src_pan,
                             nkc=nkc, scale=scale, tc=tc, ldw_post=ldw_post,
                             prev_jobs_ex=prev_jobs_ex):
                        pb = 0
                        lat_mode = cfg.latent and which != "q"
                        if lat_mode:
                            ls = tc % 2
                            p.wait("sync", pe_p, lat_free[ls])
                            for icl in range(2):
                                p.dma("sync", lat_tiles[ls][icl][:],
                                      cfg.latent["lat_d"][
                                          icl * 128:(icl + 1) * 128,
                                          tc * 512:(tc + 1) * 512], lat_ld[ls])
                            llpost = lat_ld[ls].n
                        p.wait("tensor", pan_wait[0], pan_wait[1])
                        p.wait("tensor", ld_w[0], ldw_post)
                        p.wait("tensor", ex, prev_jobs_ex)
                        if which == "v":
                            p.wait("tensor", st_v, state.get("v_st", 0))
                        p.wait("tensor", evq, pp_free[pb])
                        if lat_mode:
                            p.wait("tensor", lat_ld[ls], llpost)
                        for ic in range(nkc):
                            srcap = (lat_tiles[ls][ic][:, 0:512] if lat_mode
                                     else src_pan[ic][:, tc * 512:(tc + 1) * 512])
                            p.op("tensor", lambda En, pb=pb, ic=ic, wt=wt,
                                 nkc=nkc, srcap=srcap: En.matmul(
                                     ps_p[pb][:], wt[:, ic * 128:(ic + 1) * 128],
                                     srcap,
                                     start=(ic == 0), stop=(ic == nkc - 1)),
                                 inc=pe_p)
                        if lat_mode:
                            lat_free[ls] = pe_p.n
                        p.wait("vector", pe_p, pe_p.n)
                        if scale == 1.0:
                            p.op("vector", lambda En, pb=pb, dst=dst, tc=tc:
                                 En.tensor_copy(dst[:, tc * 512:(tc + 1) * 512],
                                                ps_p[pb][:]), inc=evq)
                        else:
                            p.op("vector", lambda En, pb=pb, dst=dst, tc=tc,
                                 scale=scale: En.tensor_scalar(
                                     dst[:, tc * 512:(tc + 1) * 512],
                                     ps_p[pb][:], scale, None, ALU.mult),
                                 inc=evq)
                        pp_free[pb] = evq.n
                    fl.append(proj)

            def wfree_mark():
                state[("evq_all", buf)] = evq.n
            fl.append(wfree_mark)

            nl = []

            def norm_init():
                p.wait("vector", evq, state[("evq_all", buf)])
                p.wait("vector", dve_n, dve_n.n)
                p.op("vector", lambda En: En.memset(mq2[:], 1e-20), inc=dve_n)
            nl.append(norm_init)
            ns_free = {}
            nslots = NS

            def nslot(i):
                return ps_s[i % nslots][0:1, 0:512]

            for wi, (which, srcf, col) in enumerate(
                    (("q", qsb, 0), ("k", ksb, 1))):
                for tc in range(NQC):
                    def norm(which=which, srcf=srcf, col=col, tc=tc, buf=buf,
                             idx2=None, wi=wi):
                        src = srcf[buf]
                        i2 = wi * NQC + tc
                        sb2 = i2 % 2
                        p.wait("scalar", evq, state[("evq_all", buf)])
                        p.wait("scalar", pe_n, ns_free.get(("sq", sb2), 0))
                        p.op("scalar", lambda En, sb2=sb2, src=src, tc=tc:
                             En.activation(qsq[sb2][:],
                                           src[:, tc * 512:(tc + 1) * 512],
                                           AF.Square), inc=act_c)
                        p.wait("tensor", act_c, act_c.n)
                        p.wait("tensor", dve_c, cst0)
                        p.wait("tensor", ex, ex.n)
                        p.wait("tensor", dve_n, ns_free.get(("ps", i2 % nslots), 0))
                        p.op("tensor", lambda En, i2=i2, sb2=sb2: En.matmul(
                            nslot(i2), ones_b[:, 0:1], qsq[sb2][:],
                            start=True, stop=True), inc=pe_n)
                        ns_free[("sq", sb2)] = pe_n.n
                        p.wait("vector", pe_n, pe_n.n)
                        p.op("vector", lambda En, i2=i2: En.tensor_reduce(
                            cch[0:1, 0:1], nslot(i2), AX.X, ALU.max),
                            inc=dve_n)
                        ns_free[("ps", i2 % nslots)] = dve_n.n
                        p.wait("vector", dve_n, dve_n.n)
                        p.op("vector", lambda En, col=col: En.tensor_max(
                            mq2[0:1, col:col + 1], mq2[0:1, col:col + 1],
                            cch[0:1, 0:1]), inc=dve_n)
                        p.wait("vector", dve_n, dve_n.n)
                    nl.append(norm)

            def cchain(buf=buf, oc=oc):
                p.op("vector", lambda En: En.tensor_tensor(
                    cch[0:1, 0:1], mq2[0:1, 0:1], mq2[0:1, 1:2], ALU.mult),
                    inc=dve_n)
                p.wait("scalar", dve_n, dve_n.n)
                p.op("scalar", lambda En: En.activation(
                    cch[0:1, 1:2], cch[0:1, 0:1], AF.Sqrt,
                    scale=(1.01 ** 2) / 64.0), inc=act_c)
                p.wait("vector", act_c, act_c.n)
                p.op("vector", lambda En: En.tensor_scalar(
                    cch[0:1, 0:1], cch[0:1, 1:2], -1.0, -2.0, ALU.mult,
                    ALU.add), inc=dve_n)
                p.wait("tensor", dve_n, dve_n.n)
                p.wait("tensor", act_zb, act_zb.n)
                p.wait("tensor", ex, state.get(("nC_used", buf), 0))
                p.op("tensor", lambda En: En.matmul(
                    aux[:, 0:1], ones_f[0:1, :], cch[0:1, 0:1],
                    start=True, stop=True), inc=pe_zb)
                p.wait("scalar", pe_zb, pe_zb.n)
                p.op("scalar", lambda En, buf=buf: En.activation(
                    nC[buf][:], aux[:, 0:1], AF.Copy), inc=act_zb)
                state[("nC", buf)] = act_zb.n
                if cfg.alibi:
                    for h2 in range(2):
                        h = oc * 2 + h2
                        p.wait("vector", act_zb, state[("nC", buf)])
                        p.wait("vector", dve_c, kgl_post)
                        p.wait("vector", ex, state.get(("nC_used", buf), 0))
                        p.op("vector", lambda En, h2=h2, h=h, buf=buf:
                             En.tensor_scalar(
                                 bias_h[h2][buf][:], kgl[:], _slope(h),
                                 nC[buf][:, 0:1], ALU.mult, ALU.add), inc=dve_c)
                    state[("bias", buf)] = dve_c.n
            nl.append(cchain)

            def vt_init(buf=buf):
                p.wait("vector", evq, state[("evq_all", buf)])
                p.wait("vector", pe_av, state.get(("vtm_used", buf), 0))
                p.op("vector", lambda En, buf=buf: En.memset(v_tm[buf][:], 1.0),
                     inc=dve_t)
                state[("vtm_ms", buf)] = dve_t.n
                p.wait("scalar", evq, state[("evq_all", buf)])
                p.dma("scalar", cfg.v_d, vsb[:], st_v)
                state["v_st"] = st_v.n
            fl.append(vt_init)
            for kt in range(NKT):
                def vt(kt=kt, buf=buf):
                    vs = kt % 2
                    p.wait("sync", st_v, state["v_st"])
                    p.wait("sync", dve_t, vstg_free[vs])
                    p.dma("sync", vstg[vs][:],
                          cfg.v_d[:, kt * 128:(kt + 1) * 128], ld_vt[vs],
                          transpose=True)
                    p.wait("vector", ld_vt[vs], ld_vt[vs].n)
                    p.wait("vector", dve_t, state[("vtm_ms", buf)])
                    for h2 in range(2):
                        p.op("vector", lambda En, h2=h2, kt=kt, buf=buf, vs=vs:
                             En.tensor_copy(
                                 v_tm[buf][:, (h2 * NKT + kt) * 65:
                                           (h2 * NKT + kt) * 65 + 64],
                                 vstg[vs][:, h2 * 64:(h2 + 1) * 64]),
                             inc=dve_t)
                    vstg_free[vs] = dve_t.n
                fl.append(vt)

            def vt_done(buf=buf):
                state[("vtm", buf)] = dve_t.n
            fl.append(vt_done)
            return fl, nl

        def run_jobs(oc, buf, fillers):
            units = []
            for h2 in range(2):
                for qc in range(NQC):
                    qb = qc * 512
                    if cfg.causal:
                        kts = []
                        for kt in range(NKT):
                            if kt * 128 > qb + 384:
                                break
                            kts.append((kt, max(0, (kt * 128 - qb) // 128)))
                    else:
                        kts = [(kt, 0) for kt in range(NKT)]
                    i0 = 0
                    while i0 < len(kts):
                        if (WIDE and i0 + 1 < len(kts)
                                and kts[i0][1] == 0 and kts[i0 + 1][1] == 0):
                            units.append((h2, qc, kts[i0:i0 + 2], i0, len(kts)))
                            i0 += 2
                        else:
                            units.append((h2, qc, kts[i0:i0 + 1], i0, len(kts)))
                            i0 += 1
            LAG = 2
            deferred = []
            negb_built = {}
            nu = len(units)
            nf = len(fillers)
            fill_at = {}
            if nf:
                stepf = max(1, nu // nf)
                for i in range(nf):
                    fill_at.setdefault(min(4 + i * stepf, nu - 1), []).append(
                        fillers[i])

            def emit_S(u):
                h2, qc, kts, i0, nkts = units[u]
                h = oc * 2 + h2
                qb = qc * 512
                prow = h2 * 64
                hs = slice(h2 * 64, (h2 + 1) * 64)
                c = sc_cnt[0]
                sc_cnt[0] += 1
                sb = c % NS
                nbb = None
                if cfg.alibi:
                    nbb = (h2 * NQC + qc) % 2
                    if i0 == 0:
                        p.wait("gpsimd", pe_s, negb_free[nbb])
                        p.wait("gpsimd", act_c, act_c.n)
                        p.op("gpsimd", lambda En, nbb=nbb, qb=qb, prow=prow:
                             En.iota(negBi[nbb][prow:prow + 1, :], [[1, 512]],
                                     base=qb, channel_multiplier=0), inc=po)
                        p.wait("scalar", po, po.n)
                        p.op("scalar", lambda En, nbb=nbb, h=h, prow=prow:
                             En.activation(negB[nbb][prow:prow + 1, :],
                                           negBi[nbb][prow:prow + 1, :],
                                           AF.Copy, scale=-_slope(h)), inc=act_c)
                        negb_built[(h2, qc)] = act_c.n
                if c >= NS:
                    p.wait("tensor", ex, exp_post[c - NS])
                if cfg.alibi:
                    p.wait("tensor", act_c, negb_built[(h2, qc)])
                msk_post = None
                for i, (kt, jj) in enumerate(kts):
                    qsl = slice(qb + jj * 128, qb + 512)
                    psl = slice(512 * i + jj * 128, 512 * (i + 1))
                    p.op("tensor", lambda En, sb=sb, hs=hs, kt=kt, qsl=qsl,
                         psl=psl, buf=buf: En.matmul(
                             ps_s[sb][:, psl],
                             ksb[buf][hs, kt * 128:(kt + 1) * 128],
                             qsb[buf][hs, qsl], start=True,
                             stop=not cfg.alibi), inc=pe_s)
                    if cfg.alibi:
                        p.op("tensor", lambda En, sb=sb, psl=psl, prow=prow,
                             nbb=nbb: En.matmul(
                                 ps_s[sb][:, psl], ones_b[prow:prow + 1, :],
                                 negB[nbb][prow:prow + 1,
                                           slice(jj * 128, 512)],
                                 start=False, stop=True), inc=pe_s)
                        negb_free[nbb] = pe_s.n
                mm_post = pe_s.n
                partial_any = False
                for i, (kt, jj) in enumerate(kts):
                    if cfg.causal and kt * 128 >= qb:
                        if not partial_any:
                            p.wait("vector", pe_s, mm_post)
                            p.wait("vector", po, mask_post)
                            partial_any = True
                        p.op("vector", lambda En, sb=sb, i=i, jj=jj:
                             En.tensor_tensor(
                                 ps_s[sb][:, 512 * i + jj * 128:
                                          512 * i + jj * 128 + 128],
                                 ps_s[sb][:, 512 * i + jj * 128:
                                          512 * i + jj * 128 + 128],
                                 mask0[:], ALU.add), inc=msk)
                if partial_any:
                    msk_post = msk.n
                ptb = c % NPT
                p.wait("scalar", pe_av, pt_free[ptb])
                if partial_any:
                    p.wait("scalar", msk, msk_post)
                else:
                    p.wait("scalar", pe_s, mm_post)
                jj0 = kts[0][1]
                espan = slice(jj0 * 128, 512 * len(kts))
                if cfg.alibi:
                    ba = bias_h[h2][buf][:, kts[0][0]:kts[0][0] + 1]
                else:
                    ba = nC[buf][:, 0:1]
                p.op("scalar", lambda En, ptb=ptb, sb=sb, espan=espan, ba=ba:
                     En.activation(PT[ptb][:, espan], ps_s[sb][:, espan],
                                   AF.Exp, bias=ba), inc=ex)
                exp_post[c] = ex.n
                return c

            def emit_AV(u, c):
                h2, qc, kts, i0, nkts = units[u]
                yb = (h2 * NQC + qc) % 2
                qb = qc * 512
                ptb = c % NPT
                p.wait("tensor", ex, exp_post[c])
                if i0 == 0:
                    p.wait("tensor", dve_t, state[("vtm", buf)])
                    p.wait("tensor", dve_z, yq_slot[yb][0])
                    p.wait("tensor", dve_y, yq_slot[yb][1])
                for i, (kt, jj) in enumerate(kts):
                    psl = slice(512 * i + jj * 128, 512 * (i + 1))
                    ysl = slice(jj * 128, 512)
                    p.op("tensor", lambda En, yb=yb, ptb=ptb, psl=psl, ysl=ysl,
                         h2=h2, kt=kt, first=(i0 == 0 and i == 0),
                         last=(i0 + i == nkts - 1), buf=buf: En.matmul(
                             ps_y[yb][:, ysl],
                             v_tm[buf][:, (h2 * NKT + kt) * 65:
                                       (h2 * NKT + kt) * 65 + 65],
                             PT[ptb][:, psl], start=first,
                             stop=last), inc=pe_av)
                pt_free[ptb] = pe_av.n
                if i0 + len(kts) == nkts:
                    for d in list(deferred):
                        d[1]()
                        deferred.remove(d)
                    av_post = pe_av.n
                    p.wait("vector", pe_av, av_post)
                    p.wait("vector", pe_zb, pe_zb.n)
                    p.op("vector", lambda En, yb=yb: En.tensor_scalar_max(
                        zr[64:65, :], ps_y[yb][64:65, :], 1e-35), inc=dve_z)
                    p.wait("scalar", dve_z, dve_z.n)
                    p.op("scalar", lambda En: En.activation(
                        zr[64:65, :], zr[64:65, :], AF.Ln), inc=act_zb)
                    p.wait("scalar", act_zb, act_zb.n)
                    p.op("scalar", lambda En: En.activation(
                        zr[64:65, :], zr[64:65, :], AF.Exp, scale=-1.0),
                        inc=dve_z)
                    z_post = dve_z.n

                    def stage2(z_post=z_post, yb=yb, oc=oc, h2=h2, qb=qb):
                        hs = slice(h2 * 64, (h2 + 1) * 64)
                        p.wait("tensor", dve_z, z_post)
                        p.wait("tensor", act_zb, act_zb.n)
                        p.op("tensor", lambda En: En.matmul(
                            aux[0:64, :], ones_f[64:65, 0:64], zr[64:65, :],
                            start=True, stop=True), inc=pe_zb)
                        p.wait("vector", pe_zb, pe_zb.n)
                        p.wait("vector", dve_y, dve_y.n)
                        p.op("vector", lambda En: En.tensor_copy(
                            zb_sb[0:64, :], aux[0:64, :]), inc=act_zb)
                        p.wait("vector", act_zb, act_zb.n)
                        p.op("vector", lambda En, yb=yb, oc=oc, hs=hs, qb=qb:
                             En.tensor_tensor(
                                 Y[oc][hs, qb:qb + 512], ps_y[yb][0:64, :],
                                 zb_sb[0:64, :], ALU.mult), inc=dve_y)
                        yq_slot[yb] = (dve_z.n, dve_y.n)
                    deferred.append([u + 6, stage2])

            p.wait("scalar", act_zb, state[("nC", buf)])
            if cfg.alibi:
                p.wait("scalar", dve_c, state[("bias", buf)])
            cq = []
            for u in range(nu):
                if u >= LAG:
                    emit_AV(u - LAG, cq[u - LAG])
                cq.append(emit_S(u))
                for f in fill_at.get(u, ()):
                    f()
                for d in list(deferred):
                    if d[0] <= u:
                        d[1]()
                        deferred.remove(d)
            for u in range(nu - LAG, nu):
                if u >= 0:
                    emit_AV(u, cq[u])
            for d in deferred:
                d[1]()
            state[("nC_used", buf)] = ex.n
            state[("vtm_used", buf)] = pe_av.n
            return ex.n

        prev_ex = {0: 0, 1: 0}
        fillers, norms0 = make_prep(0, 0, 0)
        for f in fillers:
            f()
        for f in norms0:
            f()
        p.wait("tensor", dve_n, dve_n.n)
        for oc in range(NCH):
            buf = oc % 2
            if oc + 1 < NCH:
                nxt, nxt_norms = make_prep(oc + 1, 1 - buf, prev_ex[1 - buf])
            else:
                nxt, nxt_norms = [], []
            prev_ex[buf] = run_jobs(oc, buf, nxt)
            for f in nxt_norms:
                f()
            if nxt_norms:
                p.wait("tensor", dve_n, dve_n.n)

        y_done = dve_y.n
        with contextlib.ExitStack() as stk2:
            Eo2 = stk2.enter_context
            res_pan = None
            if cfg.resid_src is not None:
                rp0 = Eo2(nc.sbuf_tensor(f"{tag}rp", [128, 512], BF16))
                res_pan = [rp0, rp0]
            wo = Eo2(nc.sbuf_tensor(f"{tag}wo", [128, D], BF16))
            o_sb0 = Eo2(nc.sbuf_tensor(f"{tag}os", [128, 512],
                                       F32 if cfg.out_f32 else BF16))
            o_sb = [o_sb0, o_sb0]
            ld_r = [p.dsem("ldr0"), p.dsem("ldr1")]
            sto_s = [p.dsem("sts0"), p.dsem("sts1")]
            ld_wo = [p.dsem("lwo0"), p.dsem("lwo1")]
            evo = p.sem("evo")
            wo_free = [0, 0]
            rp_free = [0, 0]
            op_free = [0, 0]
            for oc in range(NCH):
                wb = 0
                p.wait("sync", pe_p, wo_free[0])
                p.dma("sync", wo[:].rearrange("p (ic o) -> p ic o", o=128),
                      cfg.ow(oc), ld_wo[0])
                ldo_post = ld_wo[0].n
                p.wait("tensor", ld_wo[0], ldo_post)
                p.wait("tensor", dve_y, y_done)
                for tc in range(NQC):
                    pb = 0
                    rb = tc % 2
                    if cfg.resid_src is not None:
                        p.wait("sync", evo, rp_free[0])
                        p.wait("sync", evo, rp_free[1])
                        p.dma("sync", res_pan[rb][:],
                              cfg.resid_src[tc * 512:(tc + 1) * 512,
                                            oc * 128:(oc + 1) * 128],
                              ld_r[rb], transpose=True)
                    p.wait("tensor", evq, pp_free[pb])
                    p.wait("tensor", evo, op_free[pb])
                    for ic in range(NCH):
                        p.op("tensor", lambda En, pb=pb, ic=ic, tc=tc, wb=wb:
                             En.matmul(
                                 ps_p[pb][:], wo[:, ic * 128:(ic + 1) * 128],
                                 Y[ic][:, tc * 512:(tc + 1) * 512],
                                 start=(ic == 0), stop=(ic == NCH - 1)), inc=pe_p)
                    if cfg.resid_src is not None:
                        p.wait("vector", pe_p, pe_p.n)
                        p.wait("vector", ld_r[rb], ld_r[rb].n)
                        p.wait("vector", sto_s[0], sto_s[0].n)
                        p.wait("vector", sto_s[1], sto_s[1].n)
                        p.op("vector", lambda En, pb=pb, rb=rb: En.tensor_tensor(
                            o_sb[pb][:], ps_p[pb][:], res_pan[rb][:], ALU.add),
                            inc=evo)
                        rp_free[rb] = evo.n
                        op_free[pb] = evo.n
                        p.wait("scalar", evo, evo.n)
                    else:
                        p.wait("scalar", pe_p, pe_p.n)
                        p.wait("scalar", sto_s[0], sto_s[0].n)
                        p.wait("scalar", sto_s[1], sto_s[1].n)
                        p.op("scalar", lambda En, pb=pb: En.activation(
                            o_sb[pb][:], ps_p[pb][:], AF.Copy), inc=evo)
                        op_free[pb] = evo.n
                        p.wait("scalar", evo, evo.n)
                    p.dma("scalar", cfg.out_d[oc * 128:(oc + 1) * 128,
                                              tc * 512:(tc + 1) * 512],
                          o_sb[pb][:], sto_s[pb])
                wo_free[0] = pe_p.n
            p.wait("scalar", sto_s[0], sto_s[0].n)
            p.wait("scalar", sto_s[1], sto_s[1].n)
    p.barrier(tag, reclaim=True)


def p_ln2_gate(p, y1_d, xn3_d, r_d, lnw_ap, lnb_ap, gw_ap, gb_ap,
               cast_gw, tag="p5"):
    nc = p.nc
    with contextlib.ExitStack() as stk:
        Eo = stk.enter_context
        yt = [[Eo(nc.sbuf_tensor(f"{tag}y{r}_{i}", [128, 512], F32))
               for i in range(NCH)] for r in range(2)]
        ysq = [Eo(nc.sbuf_tensor(f"{tag}sq{i}", [128, 512], F32)) for i in range(2)]
        t1 = Eo(nc.sbuf_tensor(f"{tag}t1", [128, 512], F32))
        rstd_b = Eo(nc.sbuf_tensor(f"{tag}rb", [128, 512], F32))
        x3 = [Eo(nc.sbuf_tensor(f"{tag}x3{i}", [128, 512], BF16)) for i in range(2)]
        rows = Eo(nc.sbuf_tensor(f"{tag}rw", [1, 512 * 4], F32))
        epsc = Eo(nc.sbuf_tensor(f"{tag}eps", [1, 1], F32))
        onef = Eo(nc.sbuf_tensor(f"{tag}onef", [128, 128], F32))
        oneb = Eo(nc.sbuf_tensor(f"{tag}oneb", [128, 128], BF16))
        identb = Eo(nc.sbuf_tensor(f"{tag}idb", [128, 128], BF16))
        gcol = [Eo(nc.sbuf_tensor(f"{tag}gc{i}", [128, 1], F32)) for i in range(NCH)]
        bcol = [Eo(nc.sbuf_tensor(f"{tag}bc{i}", [128, 1], F32)) for i in range(NCH)]
        wrow = Eo(nc.sbuf_tensor(f"{tag}wr", [1, D], F32))
        brow = Eo(nc.sbuf_tensor(f"{tag}br", [1, D], F32))
        ps_a = Eo(nc.psum_tensor(f"{tag}pa", [1, 512], F32))
        ps_b = Eo(nc.psum_tensor(f"{tag}pb", [1, 512], F32))
        ps_c = Eo(nc.psum_tensor(f"{tag}pc", [128, 512], F32))
        ps_d = Eo(nc.psum_tensor(f"{tag}pd", [128, 512], F32))
        ld_af = p.dsem("ldaf"); ld_y = [p.dsem("ldy0"), p.dsem("ldy1")]
        gld = p.dsem("gld")
        act = p.sem("act"); dve = p.sem("dve")
        pe = p.sem("pe"); po = p.sem("po"); sto = p.dsem("sto")
        sto_x = [p.dsem("stx0"), p.dsem("stx1")]
        evr = p.sem("evr")
        p.op("vector", lambda En: En.memset(epsc[:], float(EPS)), inc=dve)
        p.op("vector", lambda En: En.memset(onef[:], 1.0), inc=dve)
        p.op("vector", lambda En: En.memset(oneb[:], 1.0), inc=dve)
        p.op("vector", lambda En: En.memset(identb[:], 1.0), inc=dve)
        cst0 = dve.n
        p.wait("gpsimd", dve, cst0)
        p.op("gpsimd", lambda En: En.affine_select(
            identb[:], identb[:], [[-1, 128]], ALU.is_equal, 0.0,
            base=0, channel_multiplier=1), inc=po)
        ident_post = po.n
        p.dma("sync", wrow[:], lnw_ap.rearrange("(o d) -> o d", o=1), ld_af)
        p.dma("sync", brow[:], lnb_ap.rearrange("(o d) -> o d", o=1), ld_af)
        p.wait("tensor", ld_af, ld_af.n)
        p.wait("tensor", dve, cst0)
        for ic in range(NCH):
            for row, dstc in ((wrow, gcol[ic]), (brow, bcol[ic])):
                p.wait("tensor", act, act.n)
                p.op("tensor", lambda En, ic=ic, row=row: En.transpose(
                    ps_c[:, 0:1], row[0:1, ic * 128:(ic + 1) * 128],
                    onef[0:1, 0:1]), inc=pe)
                p.wait("scalar", pe, pe.n)
                p.op("scalar", lambda En, dstc=dstc: En.activation(
                    dstc[:], ps_c[:, 0:1], AF.Copy), inc=act)
        affc_post = act.n
        yt_free = [(0, 0, 0), (0, 0, 0)]
        for tc in range(T // 512):
            r = tc % 2
            p.wait("sync", pe, yt_free[r][0])
            p.wait("sync", act, yt_free[r][1])
            p.wait("sync", dve, yt_free[r][2])
            for ic in range(NCH):
                p.dma("sync", yt[r][ic][:],
                      y1_d[ic * 128:(ic + 1) * 128, tc * 512:(tc + 1) * 512],
                      ld_y[r])
            ld_post = ld_y[r].n
            for ic in range(NCH):
                p.wait("tensor", ld_y[r], ld_post)
                if ic == 0:
                    p.wait("tensor", dve, dve.n)
                p.op("tensor", lambda En, r=r, ic=ic: En.matmul(
                    ps_a[:], onef[:, 0:1], yt[r][ic][:],
                    start=(ic == 0), stop=(ic == NCH - 1)), inc=pe)
            for ic in range(NCH):
                sb2 = ic % 2
                p.wait("scalar", ld_y[r], ld_post)
                p.wait("scalar", pe, pe.n)
                p.op("scalar", lambda En, r=r, ic=ic, sb2=sb2: En.activation(
                    ysq[sb2][:], yt[r][ic][:], AF.Square), inc=act)
                p.wait("tensor", act, act.n)
                p.op("tensor", lambda En, sb2=sb2, ic=ic: En.matmul(
                    ps_b[:], onef[:, 0:1], ysq[sb2][:],
                    start=(ic == 0), stop=(ic == NCH - 1)), inc=pe)
            p.wait("vector", pe, pe.n)
            p.op("vector", lambda En: En.tensor_scalar(
                rows[0:1, 0:512], ps_a[:], 1.0 / D, None, ALU.mult), inc=dve)
            p.wait("vector", dve, dve.n)
            p.op("vector", lambda En: En.tensor_tensor(
                rows[0:1, 1536:2048], rows[0:1, 0:512], rows[0:1, 0:512],
                ALU.mult), inc=dve)
            p.wait("vector", dve, dve.n)
            p.op("vector", lambda En: En.scalar_tensor_tensor(
                rows[0:1, 512:1024], ps_b[:], 1.0 / D, rows[0:1, 1536:2048],
                ALU.mult, ALU.subtract), inc=dve)
            p.wait("scalar", dve, dve.n)
            p.op("scalar", lambda En: En.activation(
                rows[0:1, 1024:1536], rows[0:1, 512:1024], AF.Ln,
                bias=epsc[0:1, 0:1]), inc=act)
            p.wait("scalar", act, act.n)
            p.op("scalar", lambda En: En.activation(
                rows[0:1, 1024:1536], rows[0:1, 1024:1536], AF.Exp,
                scale=-0.5), inc=act)
            rstd_post = act.n
            p.wait("tensor", act, rstd_post)
            p.wait("tensor", dve, dve.n)
            p.op("tensor", lambda En: En.matmul(
                ps_c[:], onef[0:1, :], rows[0:1, 0:512], start=True, stop=True),
                inc=pe)
            p.op("tensor", lambda En: En.matmul(
                ps_d[:], onef[0:1, :], rows[0:1, 1024:1536], start=True,
                stop=True), inc=pe)
            bc_post = pe.n
            p.wait("scalar", bc_post if False else pe, bc_post)
            p.op("scalar", lambda En: En.activation(
                rstd_b[:], ps_d[:], AF.Copy), inc=act)
            rb_post = act.n
            for ic in range(NCH):
                xb = ic % 2
                p.wait("vector", dve, dve.n)
                p.wait("vector", act, rb_post)
                p.wait("vector", pe, bc_post)
                p.wait("vector", act, affc_post)
                p.wait("vector", sto_x[xb], sto_x[xb].n)
                p.op("vector", lambda En, r=r, ic=ic: En.tensor_tensor(
                    t1[:], yt[r][ic][:], ps_c[:], ALU.subtract), inc=dve)
                p.wait("vector", dve, dve.n)
                p.op("vector", lambda En: En.tensor_tensor(
                    t1[:], t1[:], rstd_b[:], ALU.mult), inc=dve)
                p.wait("vector", dve, dve.n)
                p.op("vector", lambda En, ic=ic, xb=xb: En.tensor_scalar(
                    x3[xb][:], t1[:], gcol[ic][:, 0:1], bcol[ic][:, 0:1],
                    ALU.mult, ALU.add), inc=dve)
                p.wait("scalar", dve, dve.n)
                p.dma("scalar", xn3_d[ic * 128:(ic + 1) * 128,
                                      tc * 512:(tc + 1) * 512], x3[xb][:],
                      sto_x[xb])
            yt_free[r] = (pe.n, act.n, dve.n)
        p.wait("scalar", sto_x[0], sto_x[0].n)
        p.wait("scalar", sto_x[1], sto_x[1].n)
        with contextlib.ExitStack() as stk2:
            Eo2 = stk2.enter_context
            x3p = [Eo2(nc.sbuf_tensor(f"{tag}xp{i}", [128, T], BF16))
                   for i in range(NCH)]
            xtf = Eo2(nc.sbuf_tensor(f"{tag}xtf", [128, 128], F32))
            gwt = Eo2(nc.sbuf_tensor(f"{tag}gw", [128, NCH * E], F32))
            gbr = Eo2(nc.sbuf_tensor(f"{tag}gb", [1, E], F32))
            io8i = Eo2(nc.sbuf_tensor(f"{tag}i8i", [128, E], I32))
            io8 = Eo2(nc.sbuf_tensor(f"{tag}i8", [128, E], F32))
            lg = Eo2(nc.sbuf_tensor(f"{tag}lg", [128, E], F32))
            mx = Eo2(nc.sbuf_tensor(f"{tag}mx", [128, 1], F32))
            eq = Eo2(nc.sbuf_tensor(f"{tag}eq", [128, E], F32))
            sel = Eo2(nc.sbuf_tensor(f"{tag}sel", [128, E], F32))
            rcl = Eo2(nc.sbuf_tensor(f"{tag}rcl", [128, 1], F32))
            rclb = Eo2(nc.sbuf_tensor(f"{tag}rcb", [128, 1], BF16))
            rrow = Eo2(nc.sbuf_tensor(f"{tag}rr", [1, T], BF16))
            ps_l = [Eo2(nc.psum_tensor(f"{tag}pl{i}", [128, 512], F32))
                    for i in range(2)]
            ps_r = Eo2(nc.psum_tensor(f"{tag}pr", [1, 128], BF16))
            for cs, cn in cast_gw:
                p.wait("sync", cs, cn)
            p.wait("sync", sto_x[0], sto_x[0].n)
            p.wait("sync", sto_x[1], sto_x[1].n)
            p.dma("sync", gwt[:].rearrange("p (ic e) -> p ic e", e=E),
                  gw_ap.rearrange("(ic p) e -> p ic e", p=128), gld)
            p.dma("sync", gbr[:], gb_ap.rearrange("(o e) -> o e", o=1), gld)
            for ic in range(NCH):
                p.dma("sync", x3p[ic][:], xn3_d[ic * 128:(ic + 1) * 128, :], gld)
            gld_post = gld.n
            p.op("gpsimd", lambda En: En.iota(
                io8i[:], [[1, E]], base=0, channel_multiplier=0), inc=po)
            p.wait("vector", po, po.n)
            p.op("vector", lambda En: En.tensor_copy(io8[:], io8i[:]), inc=dve)
            io_post = dve.n
            lg_free = [0, 0]
            xtf_post = 0
            for tt in range(NT):
                lb = tt % 2
                p.wait("tensor", gld, gld_post)
                p.wait("tensor", dve, lg_free[lb])
                for ic in range(NCH):
                    p.wait("scalar", gld, gld_post)
                    p.wait("scalar", pe, xtf_post)
                    p.op("scalar", lambda En, ic=ic, tt=tt: En.activation(
                        xtf[:], x3p[ic][:, tt * 128:(tt + 1) * 128], AF.Copy),
                        inc=act)
                    p.wait("tensor", act, act.n)
                    ent = p.op("tensor", lambda En, ic=ic, tt=tt, lb=lb: En.matmul(
                        ps_l[lb][:, 0:E], xtf[:],
                        gwt[:, ic * E:(ic + 1) * E],
                        start=(ic == 0), stop=False), inc=pe)
                    xtf_post = ent["post"]
                p.op("tensor", lambda En, lb=lb: En.matmul(
                    ps_l[lb][:, 0:E], onef[0:1, :], gbr[0:1, :],
                    start=False, stop=True), inc=pe)
                p.wait("vector", pe, pe.n)
                p.wait("vector", dve, io_post)
                p.op("vector", lambda En, lb=lb: En.tensor_copy(
                    lg[:], ps_l[lb][:, 0:E]), inc=dve)
                lg_free[lb] = dve.n
                p.wait("vector", dve, dve.n)
                p.op("vector", lambda En: En.reduce_max(mx[:], lg[:], AX.X),
                     inc=dve)
                p.wait("vector", dve, dve.n)
                p.op("vector", lambda En: En.tensor_scalar(
                    eq[:], lg[:], mx[:, 0:1], None, ALU.is_ge), inc=dve)
                p.wait("vector", dve, dve.n)
                p.op("vector", lambda En: En.tensor_scalar(
                    sel[:], eq[:], -999.0, 999.0, ALU.mult, ALU.add), inc=dve)
                p.wait("vector", dve, dve.n)
                p.op("vector", lambda En: En.tensor_add(sel[:], sel[:], io8[:]),
                     inc=dve)
                p.wait("vector", dve, dve.n)
                p.op("vector", lambda En: En.tensor_reduce(
                    rcl[:], sel[:], AX.X, ALU.min), inc=dve)
                p.wait("vector", dve, dve.n)
                p.wait("vector", evr, evr.n)
                p.op("vector", lambda En: En.tensor_copy(rclb[:], rcl[:]),
                     inc=dve)
                p.wait("tensor", dve, dve.n)
                p.wait("tensor", po, ident_post)
                p.wait("tensor", evr, evr.n)
                p.op("tensor", lambda En: En.transpose(
                    ps_r[:], rclb[:, 0:1], identb[:]), inc=pe)
                p.wait("vector", pe, pe.n)
                p.op("vector", lambda En, tt=tt: En.tensor_copy(
                    rrow[0:1, tt * 128:(tt + 1) * 128], ps_r[:]), inc=evr)
            p.wait("scalar", evr, evr.n)
            p.dma("scalar", r_d.rearrange("(o t) -> o t", o=1), rrow[:], sto)
            p.wait("scalar", sto, sto.n)
    p.barrier(tag, reclaim=True)


def p_moe(p, xn3_d, r_d, moe_d, w1_b, w2_b, h_d, h_d2, cast_w2, tag="p6"):
    nc = p.nc
    NHC = HID // 128
    NTC = T // 512
    hds = [h_d, h_d2]
    with contextlib.ExitStack() as stk:
        Eo = stk.enter_context
        acc = [Eo(nc.sbuf_tensor(f"{tag}ac{i}", [128, T], BF16))
               for i in range(NCH)]
        x3p = [Eo(nc.sbuf_tensor(f"{tag}xp{i}", [128, T], BF16))
               for i in range(NCH)]
        w2t = [Eo(nc.sbuf_tensor(f"{tag}w2{i}", [128, HID // 2], BF16))
               for i in range(4)]
        hbl = [Eo(nc.sbuf_tensor(f"{tag}hb{i}", [128, 512], BF16))
               for i in range(NHC)]
        w1t = [Eo(nc.sbuf_tensor(f"{tag}w1{i}", [128, D], BF16))
               for i in range(2)]
        hst = [Eo(nc.sbuf_tensor(f"{tag}hs{i}", [128, 512], BF16))
               for i in range(3)]
        yst = [Eo(nc.sbuf_tensor(f"{tag}ys{i}", [128, 512], BF16))
               for i in range(3)]
        mei = [Eo(nc.sbuf_tensor(f"{tag}me{i}", [128, 512], I32))
               for i in range(2)]
        rrow = Eo(nc.sbuf_tensor(f"{tag}rr", [1, T], BF16))
        oneb = Eo(nc.sbuf_tensor(f"{tag}oneb", [1, 128], BF16))
        rbs = [Eo(nc.sbuf_tensor(f"{tag}rb{i}", [128, 512], BF16))
               for i in range(NTC)]
        ps_h = [Eo(nc.psum_tensor(f"{tag}ph{i}", [128, 512], F32))
                for i in range(3)]
        ps_y = [Eo(nc.psum_tensor(f"{tag}py{i}", [128, 512], F32))
                for i in range(4)]
        ps_rb = [ps_h[0], ps_h[1]]
        pe = p.sem("pe"); act = p.sem("act"); dve = p.sem("dve")
        ldr = p.sem("ldr"); ldx = p.dsem("ldx")
        ldw1 = [p.dsem("lw0"), p.dsem("lw1")]
        ldw2 = [p.dsem(f"l2{i}") for i in range(4)]
        ldh = [p.dsem("ldh0"), p.dsem("ldh1")]
        ldhg = [p.sem("lhg0", persist=True, dma=True),
                p.sem("lhg1", persist=True, dma=True)]
        sto1 = [p.dsem("s10"),
                p.sem("s11", persist=True, dma=True), p.dsem("s12")]
        sto = p.dsem("sto")
        for cs, cn in cast_w2:
            p.wait("sync", cs, cn)
        p.dma("sync", rrow[:], r_d.rearrange("(o t) -> o t", o=1), ldr)
        for ic in range(NCH):
            p.dma("sync", x3p[ic][:], xn3_d[ic * 128:(ic + 1) * 128, :], ldx)
        x3_post = ldx.n
        p.op("vector", lambda En: En.memset(oneb[:], 1.0), inc=dve)
        for i in range(NCH):
            p.op("vector", lambda En, i=i: En.memset(acc[i][:], 0.0), inc=dve)
        acc0 = dve.n
        rb_free = [0, 0]
        for tc in range(NTC):
            b = tc % 2
            p.wait("tensor", ldr, ldr.n)
            p.wait("tensor", dve, acc0)
            p.wait("tensor", act, rb_free[b])
            p.op("tensor", lambda En, b=b, tc=tc: En.matmul(
                ps_rb[b][:], oneb[:], rrow[0:1, tc * 512:(tc + 1) * 512],
                start=True, stop=True), inc=pe)
            p.wait("scalar", pe, pe.n)
            p.op("scalar", lambda En, b=b, tc=tc: En.activation(
                rbs[tc][:], ps_rb[b][:], AF.Copy), inc=act)
            rb_free[b] = act.n
        rb_post = act.n
        w1_free = [0, 0]
        psh_free = [rb_free[0], rb_free[1], 0]
        w2s_free = [0, 0, 0, 0]
        py_free = [0, 0, 0, 0]
        yst_free = [0, 0, 0]
        me_post = [0, 0]
        ldh_after = {}
        p1_done = {}
        for e in range(E):
            hd = hds[e % 2]
            if e >= 2:
                for se in ("scalar", "gpsimd", "sync"):
                    p.wait(se, ldh[0], ldh_after[e - 2][0])
                    p.wait(se, ldh[1], ldh_after[e - 2][1])
                    p.wait(se, ldhg[0], ldh_after[e - 2][2])
                    p.wait(se, ldhg[1], ldh_after[e - 2][3])
            st_eng = {0: "scalar", 1: "gpsimd", 2: "sync"}
            for hc in range(NHC):
                wb = hc % 2
                p.wait("sync", pe, w1_free[wb])
                p.dma("sync",
                      w1t[wb][:, 0:512].rearrange("p (ic o) -> p ic o", o=128),
                      w1_b[e][0:512, hc * 128:(hc + 1) * 128].rearrange(
                          "(ic p) o -> p ic o", p=128), ldw1[wb])
                p.dma("sync",
                      w1t[wb][:, 512:1024].rearrange("p (ic o) -> p ic o", o=128),
                      w1_b[e][512:1024, hc * 128:(hc + 1) * 128].rearrange(
                          "(ic p) o -> p ic o", p=128), ldw1[wb])
                w1_post = ldw1[wb].n
                p.wait("tensor", ldw1[wb], w1_post)
                if hc == 0 and e == 0:
                    p.wait("tensor", ldx, x3_post)
                for tc in range(NTC):
                    hb = tc % 3
                    p.wait("tensor", act, psh_free[hb])
                    for ic in range(NCH):
                        p.op("tensor", lambda En, hb=hb, wb=wb, ic=ic, tc=tc:
                             En.matmul(
                                 ps_h[hb][:],
                                 w1t[wb][:, ic * 128:(ic + 1) * 128],
                                 x3p[ic][:, tc * 512:(tc + 1) * 512],
                                 start=(ic == 0), stop=(ic == NCH - 1)),
                             inc=pe)
                    p.wait("scalar", pe, pe.n)
                    p.wait("scalar", sto1[hb], sto1[hb].n)
                    p.op("scalar", lambda En, hb=hb: En.activation(
                        hst[hb][:], ps_h[hb][:], AF.Gelu_apprx_tanh), inc=act)
                    psh_free[hb] = act.n
                    se = st_eng[hb]
                    p.wait(se, act, act.n)
                    p.dma(se, hd[hc * 128:(hc + 1) * 128,
                                 tc * 512:(tc + 1) * 512],
                          hst[hb][:], sto1[hb])
                w1_free[wb] = pe.n
            p1_done[e] = (sto1[0].n, sto1[1].n, sto1[2].n)
            for si in range(3):
                p.wait("scalar", sto1[si], p1_done[e][si])
                p.wait("gpsimd", sto1[si], p1_done[e][si])
            NHH = NHC // 2

            def load_half(tc2, half, gate_pe):
                p.wait("scalar", pe, gate_pe)
                p.wait("gpsimd", pe, gate_pe)
                for hc in range(half * NHH, (half + 1) * NHH):
                    he = "scalar" if hc % 2 == 0 else "gpsimd"
                    p.dma(he, hbl[hc][:],
                          hd[hc * 128:(hc + 1) * 128,
                             tc2 * 512:(tc2 + 1) * 512],
                          ldh[half] if he == "scalar" else ldhg[half])
                return (ldh[half].n, ldhg[half].n)

            hp = {(0, 0): load_half(0, 0, 0)}
            hp[(0, 1)] = load_half(0, 1, 0)

            def load_w2tile(tc2, ocg, hq, slot, gate_pe):
                p.wait("sync", pe, gate_pe)
                p.dma("sync",
                      w2t[slot][:].rearrange("p (hc o) -> p hc o", o=512),
                      w2_b[e][hq * 512:(hq + 1) * 512,
                              ocg * 512:(ocg + 1) * 512].rearrange(
                          "(hc p) o -> p hc o", p=128), ldw2[slot])
                return ldw2[slot].n

            tiles = [(tc, ocg, hq) for tc in range(NTC)
                     for ocg in range(2) for hq in range(8)]
            tpost = {}
            mei_post = 0
            for j, (tc, ocg, hq) in enumerate(tiles):
                slot = j % 4
                if j == 0:
                    for jj in range(4):
                        tpost[jj] = load_w2tile(*tiles[jj], jj, w2s_free[jj])
                mb = tc % 2
                if ocg == 0 and hq == 0:
                    p.wait("vector", dve, me_post[mb])
                    p.wait("vector", act, rb_post)
                    p.op("vector", lambda En, mb=mb, tc=tc, e=e:
                         En.tensor_scalar(mei[mb][:], rbs[tc][:], float(e),
                                          None, ALU.is_equal), inc=dve)
                    mei_post = dve.n
                p.wait("tensor", ldw2[slot], tpost[j])
                if hq in (0, 4):
                    p.wait("tensor", ldh[hq // 4], hp[(tc, hq // 4)][0])
                    p.wait("tensor", ldhg[hq // 4], hp[(tc, hq // 4)][1])
                if hq == 0:
                    for ocl in range(4):
                        p.wait("tensor", act, py_free[ocl])
                for ocl in range(4):
                    for hcl in range(4):
                        p.op("tensor",
                             lambda En, ocl=ocl, hcl=hcl, hq=hq, slot=slot:
                             En.matmul(
                                 ps_y[ocl][:],
                                 w2t[slot][:, hcl * 512 + ocl * 128:
                                           hcl * 512 + ocl * 128 + 128],
                                 hbl[hq * 4 + hcl][:],
                                 start=(hq == 0 and hcl == 0),
                                 stop=(hq == 7 and hcl == 3)),
                             inc=pe)
                w2s_free[slot] = pe.n
                if j + 4 < len(tiles):
                    tpost[j + 4] = load_w2tile(*tiles[j + 4], slot, pe.n)
                if ocg == 1 and hq == 3 and tc + 1 < NTC:
                    hp[(tc + 1, 0)] = load_half(tc + 1, 0, pe.n)
                if ocg == 1 and hq == 7 and tc + 1 < NTC:
                    hp[(tc + 1, 1)] = load_half(tc + 1, 1, pe.n)
                if hq == 7:
                    p.wait("scalar", pe, pe.n)
                    for ocl in range(4):
                        oc = ocg * 4 + ocl
                        yb = oc % 3
                        p.wait("scalar", dve, yst_free[yb])
                        p.op("scalar", lambda En, ocl=ocl, yb=yb:
                             En.activation(yst[yb][:], ps_y[ocl][:], AF.Copy),
                             inc=act)
                        py_free[ocl] = act.n
                        p.wait("vector", act, act.n)
                        p.wait("vector", dve, mei_post)
                        p.op("vector", lambda En, yb=yb, mb=mb, oc=oc, tc=tc:
                             En.copy_predicated(
                                 acc[oc][:, tc * 512:(tc + 1) * 512],
                                 mei[mb][:], yst[yb][:]), inc=dve)
                        yst_free[yb] = dve.n
                    if ocg == 1:
                        me_post[mb] = dve.n
            ldh_after[e] = (ldh[0].n, ldh[1].n, ldhg[0].n, ldhg[1].n)
        p.wait("scalar", dve, dve.n)
        for oc in range(NCH):
            p.dma("scalar", moe_d[oc * 128:(oc + 1) * 128, :], acc[oc][:], sto)
        p.wait("scalar", sto, sto.n)
    p.barrier(tag, reclaim=True)


def p_gsilu(p, y1_d, moe_d, g_d, tag="p7"):
    nc = p.nc
    with contextlib.ExitStack() as stk:
        Eo = stk.enter_context
        NB3 = 3
        yt = [Eo(nc.sbuf_tensor(f"{tag}y{i}", [128, 512], F32)) for i in range(NB3)]
        mt = [Eo(nc.sbuf_tensor(f"{tag}m{i}", [128, 512], BF16)) for i in range(NB3)]
        st_ = [Eo(nc.sbuf_tensor(f"{tag}s{i}", [128, 512], F32)) for i in range(NB3)]
        gt = [Eo(nc.sbuf_tensor(f"{tag}g{i}", [128, 512], BF16)) for i in range(NB3)]
        ld = [p.dsem(f"ld{i}") for i in range(NB3)]
        sto = [p.dsem(f"st{i}") for i in range(NB3)]
        dve = p.sem("dve"); act = p.sem("act")
        buf_free = [0] * NB3
        st_free = [0] * NB3
        for i in range(NCH * (T // 512)):
            oc, tc = divmod(i, T // 512)
            b = i % NB3
            p.wait("sync", dve, buf_free[b])
            p.wait("sync", act, st_free[b])
            sl = (slice(oc * 128, (oc + 1) * 128), slice(tc * 512, (tc + 1) * 512))
            p.dma("sync", yt[b][:], y1_d[sl[0], sl[1]], ld[b])
            p.dma("sync", mt[b][:], moe_d[sl[0], sl[1]], ld[b])
            p.wait("vector", ld[b], ld[b].n)
            p.op("vector", lambda En, b=b: En.tensor_add(
                st_[b][:], yt[b][:], mt[b][:]), inc=dve)
            buf_free[b] = dve.n
            p.wait("scalar", dve, dve.n)
            p.wait("scalar", sto[b], sto[b].n)
            p.op("scalar", lambda En, b=b: En.activation(
                gt[b][:], st_[b][:], AF.Silu), inc=act)
            st_free[b] = act.n
            p.wait("scalar", act, act.n)
            p.dma("scalar", g_d[sl[0], sl[1]], gt[b][:], sto[b])
        p.wait("scalar", sto[0], sto[0].n)
        p.wait("scalar", sto[1], sto[1].n)
    p.barrier(tag, reclaim=True)


def p_resln(p, x_ap, s_d, g_d, x1_d, xn1_d, tag="p8"):
    nc = p.nc
    with contextlib.ExitStack() as stk:
        Eo = stk.enter_context
        NB = 3
        xt = [Eo(nc.sbuf_tensor(f"{tag}x{i}", [128, D], F32)) for i in range(NB)]
        st_ = [Eo(nc.sbuf_tensor(f"{tag}s{i}", [128, D], BF16)) for i in range(NB)]
        gt = [Eo(nc.sbuf_tensor(f"{tag}g{i}", [128, D], BF16)) for i in range(NB)]
        x1t = [Eo(nc.sbuf_tensor(f"{tag}o{i}", [128, D], F32)) for i in range(NB)]
        xnt = [Eo(nc.sbuf_tensor(f"{tag}n{i}", [128, D], BF16)) for i in range(NB)]
        xsq = [Eo(nc.sbuf_tensor(f"{tag}q{i}", [128, D], F32)) for i in range(2)]
        scol = [Eo(nc.sbuf_tensor(f"{tag}sc{i}", [128, 1], F32)) for i in range(NB)]
        qcol = [Eo(nc.sbuf_tensor(f"{tag}qc{i}", [128, 1], F32)) for i in range(NB)]
        mcol = [Eo(nc.sbuf_tensor(f"{tag}mc{i}", [128, 1], F32)) for i in range(NB)]
        vcol = [Eo(nc.sbuf_tensor(f"{tag}vc{i}", [128, 1], F32)) for i in range(NB)]
        rcol = [Eo(nc.sbuf_tensor(f"{tag}rc{i}", [128, 1], F32)) for i in range(NB)]
        epsc = Eo(nc.sbuf_tensor(f"{tag}eps", [128, 1], F32))
        ld = [p.dsem(f"ld{i}") for i in range(NB)]
        sto1 = [p.dsem(f"sa{i}") for i in range(NB)]
        sto2 = [p.dsem(f"sb{i}") for i in range(NB)]
        act = p.sem("act"); dve = p.sem("dve")
        rstd = p.sem("rstd"); app = p.sem("app")
        cst = p.sem("cst")
        p.op("vector", lambda En: En.memset(epsc[:], float(EPS)), inc=cst)
        p.wait("scalar", cst, cst.n)
        sq_post = {}
        app_post = {}
        for i in range(NT):
            b = i % NB
            if i >= NB:
                p.wait("sync", app, app_post[i - NB])
                p.wait("sync", act, sq_post[i - NB])
                p.wait("sync", sto1[b], sto1[b].n)
            p.dma("sync", xt[b][:], x_ap[i * 128:(i + 1) * 128, :], ld[b])
            p.dma("sync", st_[b][:], s_d[:, i * 128:(i + 1) * 128], ld[b],
                  transpose=True)
            p.dma("sync", gt[b][:], g_d[:, i * 128:(i + 1) * 128], ld[b],
                  transpose=True)
            p.wait("vector", ld[b], ld[b].n)
            p.op("vector", lambda En, b=b: En.tensor_mul(
                x1t[b][:], st_[b][:], gt[b][:]), inc=dve)
            p.wait("vector", dve, dve.n)
            p.op("vector", lambda En, b=b: En.tensor_add(
                x1t[b][:], x1t[b][:], xt[b][:]), inc=dve)
            x1p = dve.n
            p.wait("scalar", dve, x1p)
            p.dma("scalar", x1_d[i * 128:(i + 1) * 128, :], x1t[b][:], sto1[b])
            p.wait("vector", dve, x1p)
            p.op("vector", lambda En, b=b: En.reduce_sum(
                scol[b][:], x1t[b][:], AX.X), inc=dve)
            p.wait("vector", dve, dve.n)
            p.op("vector", lambda En, b=b: En.tensor_scalar(
                mcol[b][:], scol[b][:], 1.0 / D, None, ALU.mult), inc=dve)
            p.wait("scalar", dve, dve.n)
            ent = p.op("scalar", lambda En, b=b, i=i: En.activation(
                xsq[i % 2][:], x1t[b][:], AF.Square, accum_out=qcol[b][:, 0:1]),
                inc=act)
            sq_post[i] = ent["post"]
            p.wait("vector", dve, dve.n)
            p.wait("vector", act, act.n)
            p.op("vector", lambda En, b=b: En.tensor_tensor(
                vcol[b][:], mcol[b][:], mcol[b][:], ALU.mult), inc=dve)
            p.wait("vector", dve, dve.n)
            p.op("vector", lambda En, b=b: En.scalar_tensor_tensor(
                vcol[b][:], qcol[b][:], 1.0 / D, vcol[b][:], ALU.mult,
                ALU.subtract), inc=dve)
            p.wait("scalar", dve, dve.n)
            p.op("scalar", lambda En, b=b: En.activation(
                rcol[b][:], vcol[b][:], AF.Ln, bias=epsc[:, 0:1]), inc=act)
            p.wait("scalar", act, act.n)
            p.op("scalar", lambda En, b=b: En.activation(
                rcol[b][:], rcol[b][:], AF.Exp, scale=-0.5), inc=rstd)
            p.wait("vector", rstd, rstd.n)
            p.wait("vector", sto2[b], sto2[b].n)
            p.op("vector", lambda En, b=b: En.tensor_scalar(
                xnt[b][:], x1t[b][:], mcol[b][:, 0:1], rcol[b][:, 0:1],
                ALU.subtract, ALU.mult), inc=app)
            app_post[i] = app.n
            p.wait("scalar", app, app.n)
            p.dma("scalar", xn1_d[i * 128:(i + 1) * 128, :], xnt[b][:], sto2[b])
        for s in sto1 + sto2:
            p.wait("scalar", s, s.n)
    p.barrier(tag, reclaim=True)


def p_l1ffn(p, xn1_d, wg_b, wu_b, wd_b, g1_d, cast_wd, tag="p11"):
    nc = p.nc
    NH = FHID // 128
    with contextlib.ExitStack() as stk:
        Eo = stk.enter_context
        wgp = [Eo(nc.sbuf_tensor(f"{tag}wg{i}", [128, FHID], BF16))
               for i in range(NCH)]
        wup = [Eo(nc.sbuf_tensor(f"{tag}wu{i}", [128, FHID], BF16))
               for i in range(NCH)]
        wdp = [Eo(nc.sbuf_tensor(f"{tag}wd{i}", [128, D], BF16))
               for i in range(NH)]
        xin = [Eo(nc.sbuf_tensor(f"{tag}xi{i}", [128, 512], BF16))
               for i in range(NCH)]
        hsb = [Eo(nc.sbuf_tensor(f"{tag}h{i}", [128, 512], BF16))
               for i in range(NH)]
        gst = [Eo(nc.sbuf_tensor(f"{tag}gs{i}", [128, 512], BF16))
               for i in range(2)]
        ost = [Eo(nc.sbuf_tensor(f"{tag}os{i}", [128, 512], BF16))
               for i in range(2)]
        ps_g = [Eo(nc.psum_tensor(f"{tag}pg{i}", [128, 512], F32))
                for i in range(2)]
        ps_u = [Eo(nc.psum_tensor(f"{tag}pu{i}", [128, 512], F32))
                for i in range(2)]
        ps_o = [Eo(nc.psum_tensor(f"{tag}po{i}", [128, 512], F32))
                for i in range(2)]
        ldc = [p.dsem(f"lc{i}") for i in range(4)]
        ldd = p.dsem("ldd"); ldx = p.dsem("ldx"); pe = p.sem("pe")
        act = p.sem("act"); dve = p.sem("dve")
        sto = [p.dsem("st0"), p.dsem("st1")]
        for cs, cn in cast_wd:
            p.wait("sync", cs, cn)
        CW = NH // 4 * 128
        c_post = []
        for c in range(4):
            for ic in range(NCH):
                p.dma("sync", wgp[ic][:, c * CW:(c + 1) * CW],
                      wg_b[c * CW:(c + 1) * CW, ic * 128:(ic + 1) * 128],
                      ldc[c], transpose=True)
                p.dma("sync", wup[ic][:, c * CW:(c + 1) * CW],
                      wu_b[c * CW:(c + 1) * CW, ic * 128:(ic + 1) * 128],
                      ldc[c], transpose=True)
            c_post.append(ldc[c].n)
        for hc in range(NH):
            p.dma("sync", wdp[hc][:], wd_b[:, hc * 128:(hc + 1) * 128], ldd,
                  transpose=True)
        wd_post = ldd.n
        g_free = [0, 0]
        o_free = [0, 0]
        xin_last = 0
        for tc in range(T // 512):
            if tc > 0:
                p.wait("sync", pe, xin_last)
            for ic in range(NCH):
                p.dma("sync", xin[ic][:],
                      xn1_d[tc * 512:(tc + 1) * 512, ic * 128:(ic + 1) * 128],
                      ldx, transpose=True)
            x_post = ldx.n
            for hc in range(NH):
                gb = hc % 2
                p.wait("tensor", ldc[hc // (NH // 4)], c_post[hc // (NH // 4)])
                p.wait("tensor", ldx, x_post)
                p.wait("tensor", dve, g_free[gb])
                for ic in range(NCH):
                    p.op("tensor", lambda En, gb=gb, hc=hc, ic=ic: En.matmul(
                        ps_g[gb][:], wgp[ic][:, hc * 128:(hc + 1) * 128],
                        xin[ic][:], start=(ic == 0), stop=(ic == NCH - 1)),
                        inc=pe)
                for ic in range(NCH):
                    p.op("tensor", lambda En, gb=gb, hc=hc, ic=ic: En.matmul(
                        ps_u[gb][:], wup[ic][:, hc * 128:(hc + 1) * 128],
                        xin[ic][:], start=(ic == 0), stop=(ic == NCH - 1)),
                        inc=pe)
                p.wait("scalar", pe, pe.n)
                p.wait("scalar", dve, g_free[gb])
                p.op("scalar", lambda En, gb=gb: En.activation(
                    gst[gb][:], ps_g[gb][:], AF.Silu, scale=1.0 / 127.0),
                    inc=act)
                p.wait("vector", act, act.n)
                p.op("vector", lambda En, gb=gb, hc=hc: En.scalar_tensor_tensor(
                    hsb[hc][:], ps_u[gb][:], 1.0 / 127.0, gst[gb][:],
                    ALU.mult, ALU.mult), inc=dve)
                g_free[gb] = dve.n
            h_post = dve.n
            xin_last = pe.n
            for oc in range(NCH):
                ob = oc % 2
                p.wait("tensor", dve, h_post)
                p.wait("tensor", ldd, wd_post)
                p.wait("tensor", act, o_free[ob])
                for hc in range(NH):
                    p.op("tensor", lambda En, ob=ob, oc=oc, hc=hc: En.matmul(
                        ps_o[ob][:], wdp[hc][:, oc * 128:(oc + 1) * 128],
                        hsb[hc][:], start=(hc == 0), stop=(hc == NH - 1)),
                        inc=pe)
                p.wait("scalar", pe, pe.n)
                p.wait("scalar", sto[ob], sto[ob].n)
                p.op("scalar", lambda En, ob=ob: En.activation(
                    ost[ob][:], ps_o[ob][:], AF.Silu, scale=1.0 / 127.0),
                    inc=act)
                o_free[ob] = act.n
                p.wait("scalar", act, act.n)
                p.dma("scalar", g1_d[oc * 128:(oc + 1) * 128,
                                     tc * 512:(tc + 1) * 512], ost[ob][:],
                      sto[ob])
        p.wait("scalar", sto[0], sto[0].n)
        p.wait("scalar", sto[1], sto[1].n)
    p.barrier(tag, reclaim=True)


def p_out(p, x1_d, s1_d, g1_d, out_ap, tag="p12"):
    nc = p.nc
    with contextlib.ExitStack() as stk:
        Eo = stk.enter_context
        NB = 3
        xt = [Eo(nc.sbuf_tensor(f"{tag}x{i}", [128, D], F32)) for i in range(NB)]
        st_ = [Eo(nc.sbuf_tensor(f"{tag}s{i}", [128, D], BF16)) for i in range(NB)]
        gt = [Eo(nc.sbuf_tensor(f"{tag}g{i}", [128, D], BF16)) for i in range(NB)]
        ot = [Eo(nc.sbuf_tensor(f"{tag}o{i}", [128, D], F32)) for i in range(NB)]
        ld = [p.dsem(f"ld{i}") for i in range(NB)]
        sto = [p.dsem(f"st{i}") for i in range(NB)]
        dve = p.sem("dve")
        dv_post = {}
        for i in range(NT):
            b = i % NB
            if i >= NB:
                p.wait("sync", dve, dv_post[i - NB])
                p.wait("sync", sto[b], sto[b].n)
            p.dma("sync", xt[b][:], x1_d[i * 128:(i + 1) * 128, :], ld[b])
            p.dma("sync", st_[b][:], s1_d[:, i * 128:(i + 1) * 128], ld[b],
                  transpose=True)
            p.dma("sync", gt[b][:], g1_d[:, i * 128:(i + 1) * 128], ld[b],
                  transpose=True)
            p.wait("vector", ld[b], ld[b].n)
            p.op("vector", lambda En, b=b: En.tensor_mul(
                ot[b][:], st_[b][:], gt[b][:]), inc=dve)
            p.wait("vector", dve, dve.n)
            p.op("vector", lambda En, b=b: En.tensor_add(
                ot[b][:], ot[b][:], xt[b][:]), inc=dve)
            dv_post[i] = dve.n
            p.wait("scalar", dve, dve.n)
            p.dma("scalar", out_ap[i * 128:(i + 1) * 128, :], ot[b][:], sto[b])
        for s in sto:
            p.wait("scalar", s, s.n)
    p.barrier(tag, reclaim=True)


def declare_scratch(nc):
    d = {}
    def mk(name, shape, dt):
        d[name] = nc.dram_tensor(name, shape, dt, kind="Internal").ap()
    mk("xn_d", [T, D], BF16)
    mk("xn2_d", [T, D], BF16)
    mk("s_d", [D, T], BF16)
    mk("y1_d", [D, T], F32)
    mk("xn3_d", [D, T], BF16)
    mk("r_d", [T], BF16)
    mk("moe_d", [D, T], BF16)
    mk("g_d", [D, T], BF16)
    mk("x1_d", [T, D], F32)
    mk("xn1_d", [T, D], BF16)
    mk("h_d", [HID, T], BF16)
    mk("h_d2", [HID, T], BF16)
    mk("lat_d", [256, T], BF16)
    mk("vd_d", [128, T], BF16)
    mk("s1_d", [D, T], BF16)
    mk("g1_d", [D, T], BF16)
    mk("qw_b", [D, D], BF16)
    mk("kw_b", [D, D], BF16)
    mk("vw_b", [D, D], BF16)
    mk("ow_b", [D, D], BF16)
    mk("wqkv_b", [D, 3 * D], BF16)
    mk("wproj_b", [D, D], BF16)
    mk("gw_b", [D, E], BF16)
    mk("gb_b", [E], BF16)
    mk("w1_b", [E, D, HID], BF16)
    mk("w2_b", [E, HID, D], BF16)
    mk("l1q_b", [D, D], BF16)
    mk("l1kv_b", [D, 256], BF16)
    mk("l1ku_b", [256, D], BF16)
    mk("l1vu_b", [256, D], BF16)
    mk("l1o_b", [D, D], BF16)
    mk("wg_b", [FHID, D], BF16)
    mk("wu_b", [FHID, D], BF16)
    mk("wd_b", [D, FHID], BF16)
    return d


def declare_inputs(nc):
    a = {}
    def mk(name, shape, dt):
        a[name] = nc.dram_tensor(name, shape, dt, kind="ExternalInput").ap()
    mk("x", [T, D], F32)
    mk("L0_attn_q_w", [D, D], F32)
    mk("L0_attn_k_w", [D, D], F32)
    mk("L0_attn_v_w", [D, D], F32)
    mk("L0_attn_o_w", [D, D], F32)
    mk("L0_ffn_ln1_w", [D], F32)
    mk("L0_ffn_ln1_b", [D], F32)
    mk("L0_ffn_w_qkv", [D, 3 * D], F32)
    mk("L0_ffn_w_proj", [D, D], F32)
    mk("L0_ffn_ln2_w", [D], F32)
    mk("L0_ffn_ln2_b", [D], F32)
    mk("L0_ffn_gate_w", [D, E], F32)
    mk("L0_ffn_gate_b", [E], F32)
    mk("L0_ffn_w1", [E, D, HID], F32)
    mk("L0_ffn_w2", [E, HID, D], F32)
    mk("L1_attn_Wq", [D, H, DKH], F32)
    mk("L1_attn_Wkv_down", [D, 256], F32)
    mk("L1_attn_Wk_up", [256, H, DKH], F32)
    mk("L1_attn_Wv_up", [256, H, DKH], F32)
    mk("L1_attn_Wout", [H, DKH, D], F32)
    mk("L1_ffn_w_gate", [FHID, D], I32)
    mk("L1_ffn_w_up", [FHID, D], I32)
    mk("L1_ffn_w_down", [D, FHID], I32)
    return a


CASTS = [
    ("L0_attn_q_w", "qw_b"), ("L0_attn_k_w", "kw_b"), ("L0_attn_v_w", "vw_b"),
    ("L0_attn_o_w", "ow_b"),
    ("L0_ffn_w_qkv", "wqkv_b"), ("L0_ffn_w_proj", "wproj_b"),
    ("L1_attn_Wq", "l1q_b"), ("L1_attn_Wkv_down", "l1kv_b"),
    ("L1_attn_Wk_up", "l1ku_b"), ("L1_attn_Wv_up", "l1vu_b"),
    ("L1_attn_Wout", "l1o_b"),
    ("L1_ffn_w_gate", "wg_b"), ("L1_ffn_w_up", "wu_b"),
    ("L1_ffn_w_down", "wd_b"),
    ("L0_ffn_w1", "w1_b"), ("L0_ffn_w2", "w2_b"),
]


def emit_casts(p, ins, dram):
    after = {}
    for src, dst in CASTS:
        sap = ins[src]
        dap = dram[dst]
        cs = p.sem("c_" + dst, persist=True)
        if src in ("L0_ffn_w1", "L0_ffn_w2"):
            for e in range(E):
                p.dma("gpsimd", dap[e], sap[e], cs)
        elif src in ("L1_attn_Wq", "L1_attn_Wk_up", "L1_attn_Wv_up"):
            p.dma("gpsimd", dap, sap.rearrange("a h k -> a (h k)"), cs)
        elif src == "L1_attn_Wout":
            p.dma("gpsimd", dap, sap.rearrange("h k o -> (h k) o"), cs)
        else:
            p.dma("gpsimd", dap, sap, cs)
        after[dst] = (cs, cs.n)
    return after


def build_launch(phase_fns):
    nc = bass.Bass("TRN2", target_bir_lowering=False)
    with contextlib.ExitStack() as stack:
        p = Prog(nc, stack)
        ins = declare_inputs(nc)
        dram = declare_scratch(nc)
        out = nc.dram_tensor("out", [T, D], F32, kind="ExternalOutput").ap()
        for f in phase_fns:
            f(p, ins, dram, out)
        p.emit()
    return nc


def wcol(ap, oc, off=0):
    return ap[:, off + oc * 128: off + (oc + 1) * 128].rearrange(
        "(ic p) o -> p ic o", p=128)


def all_phases(p, ins, dram, out):
    ca = emit_casts(p, ins, dram)
    p_ln_tokmajor(p, ins["x"], dram["xn_d"], F32, "p01")
    p_attn(p, AttnCfg(
        tag="a0", v_d=dram["vd_d"], src_d=dram["xn_d"], out_d=dram["s_d"], causal=True, alibi=True,
        qw=lambda oc: wcol(dram["qw_b"], oc),
        kw_=lambda oc: wcol(dram["kw_b"], oc),
        vw=lambda oc: wcol(dram["vw_b"], oc),
        ow=lambda oc: wcol(dram["ow_b"], oc),
        wait_cast=[ca["qw_b"], ca["kw_b"], ca["vw_b"], ca["ow_b"]]))
    p_ln_tokmajor(p, dram["xn_d"], dram["xn2_d"], BF16, "p3")
    p_attn(p, AttnCfg(
        tag="a1", v_d=dram["vd_d"], src_d=dram["xn2_d"], out_d=dram["y1_d"], causal=False,
        alibi=False, out_f32=True, resid_src=dram["xn_d"],
        affine=(ins["L0_ffn_ln1_w"], ins["L0_ffn_ln1_b"]),
        qw=lambda oc: wcol(dram["wqkv_b"], oc, 0),
        kw_=lambda oc: wcol(dram["wqkv_b"], oc, D),
        vw=lambda oc: wcol(dram["wqkv_b"], oc, 2 * D),
        ow=lambda oc: wcol(dram["wproj_b"], oc),
        wait_cast=[ca["wqkv_b"], ca["wproj_b"]]))
    p_ln2_gate(p, dram["y1_d"], dram["xn3_d"], dram["r_d"],
               ins["L0_ffn_ln2_w"], ins["L0_ffn_ln2_b"],
               ins["L0_ffn_gate_w"], ins["L0_ffn_gate_b"], [])
    p_moe(p, dram["xn3_d"], dram["r_d"], dram["moe_d"],
          dram["w1_b"], dram["w2_b"], dram["h_d"], dram["h_d2"],
          [ca["w1_b"], ca["w2_b"]])
    p_gsilu(p, dram["y1_d"], dram["moe_d"], dram["g_d"])
    p_resln(p, ins["x"], dram["s_d"], dram["g_d"], dram["x1_d"], dram["xn1_d"])
    p_attn(p, AttnCfg(
        tag="a2", v_d=dram["vd_d"], src_d=dram["xn1_d"], out_d=dram["s1_d"], causal=True,
        alibi=False,
        qw=lambda oc: wcol(dram["l1q_b"], oc),
        kw_=None, vw=None,
        latent=dict(dw=lambda ocl: wcol(dram["l1kv_b"], ocl),
                    ku=lambda oc: wcol(dram["l1ku_b"], oc),
                    vu=lambda oc: wcol(dram["l1vu_b"], oc),
                    lat_d=dram["lat_d"]),
        ow=lambda oc: wcol(dram["l1o_b"], oc),
        wait_cast=[ca["l1q_b"], ca["l1kv_b"], ca["l1ku_b"], ca["l1vu_b"],
                   ca["l1o_b"]]))
    p_l1ffn(p, dram["xn1_d"], dram["wg_b"], dram["wu_b"], dram["wd_b"],
            dram["g1_d"], [ca["wg_b"], ca["wu_b"], ca["wd_b"]])
    p_out(p, dram["x1_d"], dram["s1_d"], dram["g1_d"], out)


def kernel(**inputs):
    xfull = np.ascontiguousarray(inputs["L0_attn_x"])
    weights = {k: np.ascontiguousarray(v) for k, v in inputs.items()
               if k != "L0_attn_x"}
    in_maps = []
    for c in range(8):
        m = {"x": xfull[c]}
        m.update(weights)
        in_maps.append(m)
    nc = build_launch([all_phases])
    res = run_bass_kernel_spmd(nc, in_maps, core_ids=list(range(8)))
    return np.stack([r["out"] for r in res.results], axis=0)
